# version 02
# baseline (speedup 1.0000x reference)
import numpy as np
from contextlib import ExitStack

import concourse.bass as bass
import concourse.mybir as mybir
from concourse.bass_utils import run_bass_kernel_spmd

F32 = mybir.dt.float32
BF16 = mybir.dt.bfloat16
AF = mybir.ActivationFunctionType
ALU = mybir.AluOpType
AX = mybir.AxisListType

NCORES = 8
D = 1024
T = 2048
NC_D = 8
NTB = 4
TB = 512

PHASES = ("l0ffn1", "l0attn", "l0ffn2", "l1ffn1", "l1attn", "l1ffn2")
L1STEPS = None
STAGE = None


class Sem:
    def __init__(self, h):
        self.h = h
        self.n = 0

    def inc(self, amt=1):
        self.n += amt
        return self.n


class Ctx:
    def __init__(self, nc):
        self.nc = nc
        self.streams = {e: [] for e in "TVSGY"}

    def emit(self, eng, fn):
        self.streams[eng].append(fn)

    def wait(self, eng, sem, val):
        assert val <= sem.n, f"wait target {val} > emitted {sem.n} on {eng}"
        if val > 0:
            self.emit(eng, lambda e, s=sem.h, v=val: e.wait_ge(s, v))


def build_program(phases=PHASES):
    nc = bass.Bass("TRN2", target_bir_lowering=False, debug=False)

    dr = {}

    def din(name, shape, dtype=F32):
        dr[name] = nc.dram_tensor(name, list(shape), dtype, kind="ExternalInput").ap()
        return dr[name]

    din("xT", [D, T])
    din("ones_col", [128, 1])
    din("ones_row", [1, 128])
    din("idn8", [8, 8])
    din("idn128", [128, 128])
    din("L0f_gate_w", [D, 8])
    din("L0f_sw_gate", [D, D])
    din("L0f_sw_up", [D, D])
    din("L0f_sw_down", [D, D])
    din("L0f_ew_gate", [8, D, D])
    din("L0f_ew_up", [8, D, D])
    din("L0f_ew_down", [8, D, D])
    din("L0a_wq_dT", [D, 512])
    din("L0a_bq_d", [128, 4])
    din("L0a_w_qkT", [512, D])
    din("L0a_b_qk", [64, 16])
    din("L0a_wkv_dT", [D, 64])
    din("L0a_bkv_d", [64, 1])
    din("L0a_wv_uT", [64, D])
    din("L0a_bv_u", [1, D])
    din("L0a_w_projT", [D, D])
    din("L0a_b_proj", [128, 8])
    din("L1f_ln_gamma", [128, 8])
    din("L1f_ln_beta", [128, 8])
    din("L1f_router_wT", [D, 8])
    din("L1f_gate_upT", [8, D, 6144])
    din("L1f_downT", [8, 3072, D])
    din("L1f_sh_gate_upT", [D, 2048])
    din("L1f_sh_downT", [1024, D])
    for p in "qkvo":
        din(f"L1a_{p}A", [4, 16])
        din(f"L1a_{p}S", [4, 65536])
        din(f"L1a_{p}b", [128, 8])
    for nm in ["g1", "b1", "g2", "b2"]:
        din(f"L1a_{nm}", [128, 8])
    din("L1a_ind_dots", [8, 128, 16])
    din("L1a_ind_bc", [8, 16, 128])
    din("L1a_ind_bc7", [8, 16, 128])

    out_t = nc.dram_tensor("out", [D, T], F32, kind="ExternalOutput").ap()
    x_spill = nc.dram_tensor("x_spill", [D, T], F32).ap()
    ckv_in = nc.dram_tensor("ckv_in", [64, T], BF16).ap()
    wscr = nc.dram_tensor("wscr", [2, 16, 4096], BF16).ap()
    ckv_out = nc.dram_tensor("ckv_out", [NCORES * 64, T], BF16).ap()

    es = ExitStack()
    with es:
        _name_ctr = {}

        def sbuf(name, shape, dtype, stack=es):
            n = _name_ctr.get(name, 0)
            _name_ctr[name] = n + 1
            if n:
                name = f"{name}_{n}"
            return stack.enter_context(nc.sbuf_tensor(name, list(shape), dtype))

        X = sbuf("X", [128, NC_D, T], F32)
        XB = sbuf("XB", [128, 2, TB], BF16)
        SQ = sbuf("SQ", [128, 2, TB], BF16)
        ROWS = sbuf("ROWS", [1, 4, TB], F32)
        ONESC = sbuf("ONESC", [128, 1], BF16)
        ONESR_B = sbuf("ONESR_B", [1, 128], BF16)
        ONESR_F = sbuf("ONESR_F", [1, 128], F32)
        IDN8 = sbuf("IDN8", [8, 8], F32)
        IDN128 = sbuf("IDN128", [128, 128], F32)
        BIASV = sbuf("BIASV", [128, 136], F32)
        GW8 = sbuf("GW8", [128, 8, 8], BF16)
        RW8 = sbuf("RW8", [128, 8, 8], BF16)
        EPSC = sbuf("EPSC", [1, 2], F32)

        pb = [es.enter_context(nc.psum_tensor(f"pb{i}", [128, 512], F32)) for i in range(7)]

        sems = {}
        for name in ["ld", "wd", "tm", "vm", "sm", "gm", "cc", "st"]:
            sems[name] = Sem(es.enter_context(nc.semaphore(name)))
        ld, wd, tm, vm, sm = sems["ld"], sems["wd"], sems["tm"], sems["vm"], sems["sm"]
        gm, ccs, st = sems["gm"], sems["cc"], sems["st"]

        C = Ctx(nc)

        def dma(stream, out_ap, in_ap, sem):
            tgt = sem.inc(16)
            C.emit(stream, lambda e, o=out_ap, i=in_ap, s=sem.h:
                   e.dma_start(out=o, in_=i).then_inc(s, 16))
            return tgt

        def mm(out_ap, lhsT_ap, rhs_ap, start, stop, inc=None):
            if inc is not None:
                tgt = inc.inc()
                C.emit("T", lambda e, o=out_ap, l=lhsT_ap, r=rhs_ap, a=start, b=stop, s=inc.h:
                       e.matmul(o, l, r, start=a, stop=b).then_inc(s, 1))
                return tgt
            C.emit("T", lambda e, o=out_ap, l=lhsT_ap, r=rhs_ap, a=start, b=stop:
                   e.matmul(o, l, r, start=a, stop=b))

        def trans(out_ap, in_ap, ident, inc=None):
            if inc is not None:
                tgt = inc.inc()
                C.emit("T", lambda e, o=out_ap, i=in_ap, d=ident, s=inc.h:
                       e.transpose(o, i, d).then_inc(s, 1))
                return tgt
            C.emit("T", lambda e, o=out_ap, i=in_ap, d=ident: e.transpose(o, i, d))

        def act(eng, out_ap, in_ap, func, bias=0.0, scale=1.0, accum=None, inc=None):
            if inc is not None:
                tgt = inc.inc()
                C.emit(eng, lambda e, o=out_ap, i=in_ap, fu=func, b=bias, sc=scale, ac=accum, s=inc.h:
                       e.activation(o, i, fu, bias=b, scale=sc, accum_out=ac).then_inc(s, 1))
                return tgt
            C.emit(eng, lambda e, o=out_ap, i=in_ap, fu=func, b=bias, sc=scale, ac=accum:
                   e.activation(o, i, fu, bias=b, scale=sc, accum_out=ac))

        def vop(name, *args, inc=None, **kw):
            if inc is not None:
                tgt = inc.inc()
                C.emit("V", lambda e, a=args, k=kw, s=inc.h:
                       getattr(e, name)(*a, **k).then_inc(s, 1))
                return tgt
            C.emit("V", lambda e, a=args, k=kw: getattr(e, name)(*a, **k))

        def barrier():
            for eng in "TVSGY":
                for s in (tm, vm, sm, wd, ld, gm, st):
                    if s.n:
                        C.wait(eng, s, s.n)

        C.emit("V", lambda e: e.memset(EPSC[0:1, 0:1], 1e-5))
        C.emit("V", lambda e: e.memset(EPSC[0:1, 1:2], 1e-6))
        dma("G", ONESC[:], dr["ones_col"][:], wd)
        dma("G", ONESR_B[:], dr["ones_row"][:], wd)
        dma("G", GW8[:],
            dr["L0f_gate_w"].rearrange("(c p) e -> p c e", p=128), wd)
        dma("G", RW8[:],
            dr["L1f_router_wT"].rearrange("(c p) e -> p c e", p=128), wd)
        consts_wd = wd.n
        dma("Y", ONESR_F[:], dr["ones_row"][:], ld)
        dma("Y", IDN8[:], dr["idn8"][:], ld)
        dma("Y", IDN128[:], dr["idn128"][:], ld)
        x_ld = dma("Y", X[:],
                   dr["xT"].rearrange("(c p) t -> p c t", p=128), ld)

        bias_slots = {}
        bias_next = [0]

        def load_bias(name, dram_name, nchunk):
            base = bias_next[0]
            bias_next[0] += nchunk
            bias_slots[name] = base
            return dma("Y", BIASV[0:dr[dram_name].shape[0], base:base + nchunk],
                       dr[dram_name][:], ld)

        def bias_ap(name, c):
            b = bias_slots[name]
            return BIASV[:, b + c:b + c + 1]

        for nm, dnm, nch in [("qb", "L1a_qb", 8), ("kb", "L1a_kb", 8),
                         ("vb", "L1a_vb", 8), ("ob", "L1a_ob", 8),
                         ("ln_g", "L1f_ln_gamma", 8), ("ln_b", "L1f_ln_beta", 8),
                             ("b_proj", "L0a_b_proj", 8), ("bq_d", "L0a_bq_d", 4),
                             ("b_qk", "L0a_b_qk", 16), ("bkv_d", "L0a_bkv_d", 1),
                             ("g1", "L1a_g1", 8), ("b1", "L1a_b1", 8),
                             ("g2", "L1a_g2", 8), ("b2", "L1a_b2", 8)]:
            load_bias(nm, dnm, nch)
        bias_slots["bqk8"] = bias_next[0]
        bias_next[0] += 16
        C.wait("V", ld, ld.n)
        vop("tensor_scalar", BIASV[0:64, bias_slots["bqk8"]:bias_slots["bqk8"] + 16],
            BIASV[0:64, bias_slots["b_qk"]:bias_slots["b_qk"] + 16], 0.125, None, ALU.mult)
        consts_ld = ld.n

        def emit_ln(src_sb, src_is_f32, dst_sb, affine=None, eps=0):
            inv = 1.0 / D
            prev_lv = 0
            for tb in range(NTB):
                ts = slice(tb * TB, (tb + 1) * TB)
                mm_done = [0] * NC_D
                for c in range(NC_D):
                    if c >= 2:
                        C.wait("S", tm, mm_done[c - 2])
                    if src_is_f32:
                        act("S", XB[:, c % 2, :], src_sb[:, c, ts], AF.Identity)
                        stat = XB[:, c % 2, :]
                    else:
                        stat = src_sb[:, c, ts]
                    cast_t = act("S", SQ[:, c % 2, :], src_sb[:, c, ts], AF.Square, inc=sm)
                    C.wait("T", sm, cast_t)
                    mm(pb[5][0:1, :], ONESC[:], stat, start=(c == 0), stop=(c == NC_D - 1))
                    mm_done[c] = mm(pb[6][0:1, :], ONESC[:], SQ[:, c % 2, :],
                                    start=(c == 0), stop=(c == NC_D - 1), inc=tm)
                sums_done = tm.n
                C.wait("V", tm, sums_done)
                vop("tensor_scalar", ROWS[0:1, 0, :], pb[5][0:1, :], inv, None, ALU.mult)
                vop("tensor_scalar", ROWS[0:1, 1, :], pb[6][0:1, :], inv, None, ALU.mult)
                vop("tensor_tensor", ROWS[0:1, 2, :], ROWS[0:1, 0, :], ROWS[0:1, 0, :], ALU.mult)
                v_done = vop("tensor_tensor", ROWS[0:1, 1, :], ROWS[0:1, 1, :],
                             ROWS[0:1, 2, :], ALU.subtract, inc=vm)
                C.wait("S", vm, v_done)
                act("S", ROWS[0:1, 2, :], ROWS[0:1, 1, :], AF.Ln,
                    bias=EPSC[0:1, eps:eps + 1], scale=1.0)
                rstd_done = act("S", ROWS[0:1, 3, :], ROWS[0:1, 2, :], AF.Exp,
                                bias=0.0, scale=-0.5, inc=sm)
                C.wait("T", sm, rstd_done)
                mm(pb[5][:, :], ONESR_F[:], ROWS[0:1, 0, :], start=True, stop=True)
                bc_done = mm(pb[6][:, :], ONESR_F[:], ROWS[0:1, 3, :], start=True,
                             stop=True, inc=tm)
                C.wait("V", tm, bc_done)
                aff_s = 0
                for c in range(NC_D):
                    last = (c == NC_D - 1)
                    dst = dst_sb[:, c, ts]
                    vop("tensor_tensor", dst, src_sb[:, c, ts], pb[5][:, :], ALU.subtract)
                    if affine is None:
                        lv = vop("tensor_tensor", dst, dst, pb[6][:, :], ALU.mult,
                                 inc=vm if last else None)
                    else:
                        g_nm, b_nm = affine
                        lv = vop("tensor_tensor", dst, dst, pb[6][:, :], ALU.mult, inc=vm)
                        C.wait("S", vm, lv)
                        aff_s = act("S", dst, dst, AF.Identity,
                                    bias=bias_ap(b_nm, c), scale=bias_ap(g_nm, c),
                                    inc=sm if last else None)
                C.wait("T", vm, lv)
                if affine is not None:
                    C.wait("T", sm, aff_s)
                    lv = vm.n
                prev_lv = lv
            return vm.n

        def gating_16tiles(logits_src_fn, WT_dst, gates_mode, half_scale=True):
            GLOG = gating_16tiles.GLOG
            GTOK = gating_16tiles.GTOK
            GWRK = gating_16tiles.GWRK
            GS1 = gating_16tiles.GS1
            GS2 = gating_16tiles.GS2
            for tb in range(NTB):
                if tb > 0:
                    C.wait("T", sm, sm.n)
                r = logits_src_fn(tb)
                C.wait("S", tm, r)
                act("S", GLOG[:, tb * TB:(tb + 1) * TB], pb[4][0:8, :], AF.Identity, inc=sm)
            if STAGE == "g1":
                C.wait("V", sm, sm.n)
                vop("tensor_copy", X[0:8, 0, :], GLOG[:], inc=vm)
                return sm.n
            C.wait("T", sm, sm.n)
            for i in range(16):
                trans(pb[4][:, 8 * i:8 * i + 8], GLOG[0:8, i * 128:(i + 1) * 128], IDN8[:],
                      inc=tm if i == 15 else None)
            C.wait("V", tm, tm.n)
            vop("tensor_copy", GTOK[:].rearrange("p a b -> p (a b)"), pb[4][:, 0:128], inc=vm)
            if STAGE == "g2":
                vop("tensor_copy", X[:, 0, 0:128], GTOK[:].rearrange("p a b -> p (a b)"), inc=vm)
                return sm.n
            G_ = lambda i: GTOK[:, i, :]
            W_ = lambda i: GWRK[:, i, :]
            M1 = lambda i: GS1[:, i:i + 1]
            M2 = lambda i: GS2[:, i:i + 1]
            M3 = lambda i: gating_16tiles.GS3[:, i:i + 1]
            for i in range(16):
                vop("tensor_reduce", M2(i), G_(i), AX.X, ALU.max, negate=True,
                    inc=vm if i == 15 else None)
            C.wait("S", vm, vm.n)
            for i in range(16):
                act("S", G_(i), G_(i), AF.Exp, bias=M2(i), scale=1.0, accum=M1(i),
                    inc=sm if i == 15 else None)
            C.wait("V", sm, sm.n)
            for i in range(16):
                vop("reciprocal", M3(i), M1(i))
            for i in range(16):
                vop("tensor_scalar", G_(i), G_(i), M3(i), None, ALU.mult)
            for i in range(16):
                vop("tensor_reduce", M1(i), G_(i), AX.X, ALU.max)
            for i in range(16):
                vop("tensor_scalar", W_(i), G_(i), M1(i), None, ALU.is_ge)
            for i in range(16):
                vop("scalar_tensor_tensor", W_(i), W_(i), -1e30, G_(i), ALU.mult, ALU.add)
            for i in range(16):
                vop("tensor_reduce", M2(i), W_(i), AX.X, ALU.max)
            for i in range(16):
                vop("tensor_scalar", W_(i), G_(i), M2(i), None, ALU.is_ge)
            sc2 = 0.5 if half_scale else 1.0
            if gates_mode == "l0":
                for i in range(16):
                    vop("tensor_tensor", W_(i), W_(i), G_(i), ALU.mult)
                for i in range(16):
                    vop("tensor_reduce", M2(i), W_(i), AX.X, ALU.add)
                for i in range(16):
                    vop("tensor_scalar", M3(i), M2(i), 1e-20, None, ALU.add)
                for i in range(16):
                    vop("reciprocal", M1(i), M3(i))
                for i in range(16):
                    vop("tensor_scalar", W_(i), W_(i), M1(i), sc2, ALU.mult, ALU.mult,
                        inc=vm if i == 15 else None)
            else:
                for i in range(16):
                    vop("tensor_scalar", M3(i), M1(i), -1.0, None, ALU.mult,
                        inc=vm if i == 15 else None)
                C.wait("S", vm, vm.n)
                for i in range(16):
                    act("S", G_(i), G_(i), AF.Exp, bias=M3(i), scale=1.0,
                        inc=sm if i == 15 else None)
                C.wait("V", sm, sm.n)
                for i in range(16):
                    vop("tensor_tensor", W_(i), W_(i), G_(i), ALU.mult)
                for i in range(16):
                    vop("tensor_reduce", M2(i), W_(i), AX.X, ALU.add)
                for i in range(16):
                    vop("reciprocal", M1(i), M2(i))
                for i in range(16):
                    vop("tensor_scalar", W_(i), W_(i), M1(i), sc2, ALU.mult, ALU.mult,
                        inc=vm if i == 15 else None)
            gw_done = vm.n
            if STAGE == "g3":
                vop("tensor_copy", X[:, 0, 0:128], GWRK[:].rearrange("p a b -> p (a b)"), inc=vm)
                return sm.n
            C.wait("T", vm, gw_done)
            wt_evac = [0] * 16
            for i in range(16):
                bk = i % 4
                if i >= 4:
                    C.wait("T", sm, wt_evac[i - 4])
                t4 = trans(pb[bk][0:8, 0:128],
                           gating_16tiles.GWRK[:, i, :], IDN128[:], inc=tm)
                C.wait("S", tm, t4)
                wt_evac[i] = act("S", WT_dst[0:8, 128 * i:128 * (i + 1)],
                                 pb[bk][0:8, 0:128], AF.Identity, inc=sm)
            return sm.n

        def l0ffn(XN, xn_v, fstack):
            GLOG = sbuf("GLOG", [8, T], F32, stack=fstack)
            WT = sbuf("WT", [8, T], BF16, stack=fstack)
            GTOK = sbuf("GTOK", [128, 16, 8], F32, stack=fstack)
            GWRK = sbuf("GWRK", [128, 16, 8], F32, stack=fstack)
            GS1 = sbuf("GS1", [128, 16], F32, stack=fstack)
            GS2 = sbuf("GS2", [128, 16], F32, stack=fstack)
            GS3 = sbuf("GS3", [128, 16], F32, stack=fstack)
            WROW = sbuf("WROW", [1, T], BF16, stack=fstack)
            WPOOL = sbuf("WPOOL", [128, 3, 8, 1024], BF16, stack=fstack)
            SG = sbuf("SG", [128, TB], BF16, stack=fstack)
            UU = sbuf("UU", [128, TB], BF16, stack=fstack)
            MID = sbuf("MID", [128, 8, TB], BF16, stack=fstack)
            gating_16tiles.GLOG = GLOG
            gating_16tiles.GTOK = GTOK
            gating_16tiles.GWRK = GWRK
            gating_16tiles.GS1 = GS1
            gating_16tiles.GS2 = GS2
            gating_16tiles.GS3 = GS3

            C.wait("T", wd, consts_wd)
            C.wait("T", vm, xn_v)

            def logits(tb):
                ts = slice(tb * TB, (tb + 1) * TB)
                for c in range(NC_D):
                    r = mm(pb[4][0:8, :], GW8[:, c, :], XN[:, c, ts],
                           start=(c == 0), stop=(c == NC_D - 1),
                           inc=tm if c == NC_D - 1 else None)
                return r

            wt_done = gating_16tiles(logits, WT, "l0")
            C.wait("T", sm, wt_done)
            if STAGE in ("g1", "g2", "g3"):
                return vm.n
            if STAGE == "gating":
                C.wait("V", sm, wt_done)
                vop("tensor_copy", X[0:8, 0, :], WT[:], inc=vm)
                return vm.n

            ew = [dr["L0f_ew_gate"], dr["L0f_ew_up"], dr["L0f_ew_down"]]
            sw = [dr["L0f_sw_gate"], dr["L0f_sw_up"], dr["L0f_sw_down"]]

            def wsrc(kind, e):
                a = sw[kind] if e == 8 else ew[kind][e]
                return a.rearrange("(c p) m -> p c m", p=128)

            order = [8, 0, 1, 2, 3, 4, 5, 6, 7]
            wtgt = {}

            def stream(e):
                for kind in range(3):
                    wtgt[(e, kind)] = dma("G", WPOOL[:, kind, :, :],
                                          wsrc(kind, e), wd)

            last_use = 0
            bank_free = {i: 0 for i in range(7)}
            last_mid = 0
            stream(order[0])
            wrow_ld = 0
            for ei, e in enumerate(order):
                if ei > 0:
                    C.wait("G", tm, last_use)
                    stream(e)
                for kind in range(3):
                    C.wait("T", wd, wtgt[(e, kind)])
                if e != 8:
                    C.wait("Y", sm, wt_done)
                    if wrow_ld:
                        C.wait("Y", tm, last_wbc)
                    wrow_ld = dma("Y", WROW[:], WT[e:e + 1, :], ld)
                for tb in range(NTB):
                    ts = slice(tb * TB, (tb + 1) * TB)
                    if e != 8:
                        C.wait("T", ld, wrow_ld)
                        C.wait("T", vm, bank_free[4])
                        wbc = mm(pb[4][:, :], ONESR_B[:], WROW[0:1, ts],
                                 start=True, stop=True, inc=tm)
                        last_wbc = wbc
                        bank_free[4] = 0
                    for fc in range(NC_D):
                        C.wait("T", sm, bank_free[0])
                        for c in range(NC_D):
                            mm(pb[0][:, :], WPOOL[:, 0, c, 128 * fc:128 * fc + 128],
                               XN[:, c, ts], start=(c == 0), stop=(c == NC_D - 1),
                               inc=tm if c == NC_D - 1 else None)
                        g_done = tm.n
                        C.wait("T", vm, bank_free[1])
                        for c in range(NC_D):
                            mm(pb[1][:, :], WPOOL[:, 1, c, 128 * fc:128 * fc + 128],
                               XN[:, c, ts], start=(c == 0), stop=(c == NC_D - 1),
                               inc=tm if c == NC_D - 1 else None)
                        u_done = tm.n
                        C.wait("S", tm, g_done)
                        C.wait("S", vm, last_mid)
                        sg_t = act("S", SG[:], pb[0][:, :], AF.Silu, inc=sm)
                        bank_free[0] = sg_t
                        C.wait("V", tm, u_done)
                        vop("tensor_copy", UU[:], pb[1][:, :])
                        C.wait("V", sm, sg_t)
                        if e == 8:
                            mv = vop("scalar_tensor_tensor", MID[:, fc, :], SG[:], 0.5, UU[:],
                                     ALU.mult, ALU.mult, inc=vm)
                        else:
                            C.wait("V", tm, wbc)
                            vop("tensor_tensor", MID[:, fc, :], SG[:], UU[:], ALU.mult)
                            mv = vop("tensor_tensor", MID[:, fc, :], MID[:, fc, :],
                                     pb[4][:, :], ALU.mult, inc=vm)
                        bank_free[1] = mv
                        last_mid = mv
                        if e != 8:
                            bank_free[4] = mv
                    mid_done = vm.n
                    C.wait("T", vm, mid_done)
                    for dc in range(NC_D):
                        db = 2 + (dc % 2)
                        C.wait("T", vm, bank_free[db])
                        for fc in range(NC_D):
                            mm(pb[db][:, :], WPOOL[:, 2, fc, 128 * dc:128 * dc + 128],
                               MID[:, fc, :], start=(fc == 0), stop=(fc == NC_D - 1),
                               inc=tm if fc == NC_D - 1 else None)
                        d_done = tm.n
                        C.wait("V", tm, d_done)
                        av = vop("tensor_tensor", X[:, dc, ts], X[:, dc, ts], pb[db][:, :],
                                 ALU.add, inc=vm)
                        bank_free[db] = av
                last_use = tm.n
            return vm.n

        def l0attn(astack):
            CQ = sbuf("CQ", [128, 4, T], BF16, stack=astack)
            with ExitStack() as sA:
                XN = sbuf("XN_a", [128, NC_D, T], BF16, stack=sA)
                WQ = sbuf("WQ", [128, 8, 512], BF16, stack=sA)
                WKV = sbuf("WKV", [128, 8, 64], BF16, stack=sA)
                CKVL = sbuf("CKVL", [64, T], BF16, stack=sA)
                wq_t = dma("G", WQ[:], dr["L0a_wq_dT"].rearrange("(c p) m -> p c m", p=128), wd)
                wkv_t = dma("G", WKV[:], dr["L0a_wkv_dT"].rearrange("(c p) m -> p c m", p=128), wd)
                xn_v = emit_ln(X, True, XN)
                C.wait("T", vm, xn_v)
                C.wait("T", wd, max(wq_t, wkv_t))
                bfree = {i: 0 for i in range(4)}
                for mc in range(4):
                    for tb in range(NTB):
                        ts = slice(tb * TB, (tb + 1) * TB)
                        bk = tb % 2
                        C.wait("T", vm, bfree[bk])
                        for c in range(NC_D):
                            r = mm(pb[bk][:, :], WQ[:, c, 128 * mc:128 * mc + 128],
                                   XN[:, c, ts], start=(c == 0), stop=(c == NC_D - 1),
                                   inc=tm if c == NC_D - 1 else None)
                        C.wait("V", tm, r)
                        bfree[bk] = vop("tensor_scalar", CQ[:, mc, ts], pb[bk][:, :],
                                        bias_ap("bq_d", mc), None, ALU.add, inc=vm)
                for tb in range(NTB):
                    ts = slice(tb * TB, (tb + 1) * TB)
                    bk = 2 + tb % 2
                    C.wait("T", vm, bfree[bk])
                    for c in range(NC_D):
                        r = mm(pb[bk][0:64, :], WKV[:, c, :], XN[:, c, ts],
                               start=(c == 0), stop=(c == NC_D - 1),
                               inc=tm if c == NC_D - 1 else None)
                    C.wait("V", tm, r)
                    bkv_ap = BIASV[0:64, bias_slots["bkv_d"]:bias_slots["bkv_d"] + 1]
                    bfree[bk] = vop("tensor_scalar", CKVL[:, ts], pb[bk][0:64, :],
                                    bkv_ap, None, ALU.add, inc=vm)
                ckv_done = vm.n
                C.wait("G", vm, ckv_done)
                bounce_t = dma("G", ckv_in[:, :], CKVL[:], wd)
            C.wait("G", wd, bounce_t)
            cc_t = ccs.inc()
            C.emit("G", lambda e, s=ccs.h: e.collective_compute(
                "AllGather", ALU.bypass, replica_groups=[list(range(NCORES))],
                ins=[ckv_in], outs=[ckv_out]).then_inc(s, 1))

            WQK = sbuf("WQK", [128, 4, 1024], BF16, stack=astack)
            QEXT = sbuf("QEXT", [65, 16, 1024], BF16, stack=astack)
            CKV = sbuf("CKV", [65, 8192], BF16, stack=astack)
            KM = sbuf("KM", [64, 65], BF16, stack=astack)
            KSUM = sbuf("KSUM", [64, 2], F32, stack=astack)
            WVU = sbuf("WVU", [65, 1024], BF16, stack=astack)
            YALL = sbuf("YALL", [64, 16, 512], BF16, stack=astack)
            YTMP = sbuf("YTMP", [64, 512], BF16, stack=astack)
            ZR = sbuf("ZR", [1, 512], F32, stack=astack)
            WPJ = sbuf("WPJ", [64, 16, 512], BF16, stack=astack)
            wqk_t = dma("G", WQK[:], dr["L0a_w_qkT"].rearrange("(c p) m -> p c m", p=128), wd)
            wvu_t = dma("G", WVU[0:64, :], dr["L0a_wv_uT"][:], wd)
            wvu_t = dma("G", WVU[64:65, :], dr["L0a_bv_u"][:], wd)
            C.emit("V", lambda e: e.memset(CKV[64:65, :], 1.0))
            C.emit("V", lambda e: e.memset(KM[:, :], 0.0))
            last_yv = [0]
            sguard = {i: 0 for i in range(7)}
            p3guard = {i: 0 for i in range(3)}

            for b in range(2):
                C.wait("V", tm, tm.n)
                C.wait("S", tm, tm.n)
                tbs = [2 * b, 2 * b + 1]
                C.wait("G", ccs, cc_t)
                C.wait("G", tm, tm.n)
                ckv_t = dma("G", CKV[0:64, :].rearrange("d (r t) -> d r t", r=8),
                            ckv_out.rearrange("(r d) t -> d r t", d=64)[:, :, 1024 * b:1024 * (b + 1)],
                            wd)
                C.wait("T", wd, wqk_t)
                for h in range(16):
                    for t2 in range(2):
                        ts = slice(tbs[t2] * TB, (tbs[t2] + 1) * TB)
                        qs = slice(t2 * TB, (t2 + 1) * TB)
                        bk = t2 % 2
                        C.wait("T", vm, bfree[bk])
                        for kc in range(4):
                            r = mm(pb[bk][0:64, :], WQK[:, kc, 64 * h:64 * h + 64],
                                   CQ[:, kc, ts], start=(kc == 0), stop=(kc == 3),
                                   inc=tm if kc == 3 else None)
                        C.wait("V", tm, r)
                        bfree[bk] = vop("tensor_scalar", QEXT[0:64, h, qs], pb[bk][0:64, :],
                                        0.125, BIASV[0:64, bias_slots["bqk8"] + h:
                                                      bias_slots["bqk8"] + h + 1],
                                        ALU.mult, ALU.add, inc=vm)
                C.wait("V", wd, ckv_t)
                vop("tensor_reduce", KSUM[:, 0:1], CKV[0:64, :], AX.X, ALU.add)
                vop("tensor_scalar", KSUM[:, 1:2], KSUM[:, 0:1], -1.0 / 8192.0, None, ALU.mult)
                km_v = vop("tensor_copy", KM[:, 64:65], KSUM[:, 1:2], inc=vm)
                C.wait("T", vm, vm.n)
                mrow_ev = 0
                for h in range(16):
                    for t2 in range(2):
                        qs = slice(t2 * TB, (t2 + 1) * TB)
                        if mrow_ev:
                            C.wait("T", sm, mrow_ev)
                        r = mm(pb[6][0:65, :], KM[:], QEXT[0:64, h, qs],
                               start=True, stop=True, inc=tm)
                        C.wait("S", tm, r)
                        mrow_ev = act("S", QEXT[64:65, h, qs], pb[6][64:65, :],
                                      AF.Identity, inc=sm)
                C.wait("T", sm, mrow_ev)
                for t2 in range(2):
                    qs = slice(t2 * TB, (t2 + 1) * TB)
                    ts = slice(tbs[t2] * TB, (tbs[t2] + 1) * TB)
                    with ExitStack() as sC:
                        VEXT = sbuf("VEXT", [128, 64, 65], BF16, stack=sC)
                        WVX = sbuf("WVX", [65, 65], BF16, stack=sC)
                        P3 = sbuf("P3", [128, 3, TB], BF16, stack=sC)
                        C.wait("V", wd, wvu_t)
                        for h in range(16):
                            C.wait("V", tm, tm.n)
                            vop("memset", WVX[:, :], 0.0)
                            vop("tensor_copy", WVX[:, 0:64], WVU[:, 64 * h:64 * h + 64])
                            wvx_v = vop("memset", WVX[64:65, 64:65], 1.0, inc=vm)
                            C.wait("T", vm, wvx_v)
                            vb_last = 0
                            for sc in range(64):
                                bk = 4 + sc % 2
                                C.wait("T", sm, sguard[bk])
                                r = mm(pb[bk][:, 0:65], CKV[:, 128 * sc:128 * sc + 128],
                                       WVX[:], start=True, stop=True, inc=tm)
                                C.wait("S", tm, r)
                                vb_last = act("S", VEXT[:, sc, :], pb[bk][:, 0:65],
                                              AF.Identity, inc=sm)
                                sguard[bk] = vb_last
                            C.wait("T", sm, vb_last)
                            for sc in range(64):
                                bk = sc % 2
                                C.wait("T", sm, sguard[bk])
                                mm(pb[bk][:, :], CKV[:, 128 * sc:128 * sc + 128],
                                   QEXT[0:65, h, qs], start=True, stop=True, inc=tm)
                                sc_t = tm.n
                                C.wait("S", tm, sc_t)
                                C.wait("S", tm, p3guard[sc % 3])
                                ex = act("S", P3[:, sc % 3, :], pb[bk][:, :],
                                         AF.Exp, inc=sm)
                                sguard[bk] = ex
                                C.wait("T", sm, ex)
                                mm(pb[2][0:65, :], VEXT[:, sc, :], P3[:, sc % 3, :],
                                   start=(sc == 0), stop=(sc == 63), inc=tm)
                                p3guard[sc % 3] = tm.n
                            pv_last = tm.n
                            C.wait("V", tm, pv_last)
                            zr_v = vop("reciprocal", ZR[:], pb[2][64:65, :], inc=vm)
                            C.wait("T", vm, zr_v)
                            bc_t = mm(pb[3][0:64, :], ONESR_F[0:1, 0:64], ZR[:],
                                      start=True, stop=True, inc=tm)
                            C.wait("S", tm, pv_last)
                            C.wait("S", vm, last_yv[0])
                            yt_ev = act("S", YTMP[:], pb[2][0:64, :], AF.Identity, inc=sm)
                            C.wait("V", sm, yt_ev)
                            C.wait("V", tm, bc_t)
                            y_v = vop("tensor_tensor", YALL[0:64, h, :],
                                      YTMP[:], pb[3][0:64, :], ALU.mult, inc=vm)
                            last_yv[0] = y_v
                            C.wait("T", sm, yt_ev)
                            C.wait("T", vm, y_v)
                    C.wait("T", vm, vm.n)
                    pfree = {0: 0, 1: 0}
                    for half in range(2):
                        C.wait("G", tm, tm.n)
                        wpj_t = dma("G", WPJ[:],
                                    dr["L0a_w_projT"].rearrange("(h p) m -> p h m", p=64)
                                    [:, :, 512 * half:512 * (half + 1)], wd)
                        C.wait("T", wd, wpj_t)
                        for mc2 in range(4):
                            mc = 4 * half + mc2
                            bk = mc % 2
                            C.wait("T", vm, pfree[bk])
                            for h in range(16):
                                r = mm(pb[bk][:, :], WPJ[:, h, 128 * mc2:128 * mc2 + 128],
                                       YALL[0:64, h, :], start=(h == 0), stop=(h == 15),
                                       inc=tm if h == 15 else None)
                            C.wait("V", tm, r)
                            pfree[bk] = vop("scalar_tensor_tensor", X[:, mc, ts],
                                            pb[bk][:, :], bias_ap("b_proj", mc),
                                            X[:, mc, ts], ALU.add, ALU.add, inc=vm)
            return vm.n

        def l1ffn(XN, xn_v, fstack):
            WT = sbuf("WT1", [8, T], BF16, stack=fstack)
            WROW = sbuf("WROW1", [1, T], BF16, stack=fstack)
            C.wait("S", vm, xn_v)
            if STAGE == "l1nc":
                C.wait("V", vm, xn_v)
                for c in range(NC_D):
                    vop("tensor_copy", X[:, c, :], XN[:, c, :], inc=vm)
                return vm.n
            if STAGE == "l1n0":
                XN2 = sbuf("XN2dbg", [128, NC_D, T], BF16, stack=fstack)
                nrm_v = emit_ln(XN, False, XN2, affine=None)
                XN = XN2
            else:
                nrm_v = emit_ln(XN, False, XN, affine=("ln_g", "ln_b"))
            if STAGE in ("l1n", "l1n0"):
                C.wait("V", vm, nrm_v)
                for c in range(NC_D):
                    vop("tensor_copy", X[:, c, :], XN[:, c, :], inc=vm)
                return vm.n
            with ExitStack() as gs:
                GLOG = sbuf("GLOG1", [8, T], F32, stack=gs)
                GTOK = sbuf("GTOK1", [128, 16, 8], F32, stack=gs)
                GWRK = sbuf("GWRK1", [128, 16, 8], F32, stack=gs)
                GS1 = sbuf("GS11", [128, 16], F32, stack=gs)
                GS2 = sbuf("GS21", [128, 16], F32, stack=gs)
                GS3 = sbuf("GS31", [128, 16], F32, stack=gs)
                gating_16tiles.GLOG = GLOG
                gating_16tiles.GTOK = GTOK
                gating_16tiles.GWRK = GWRK
                gating_16tiles.GS1 = GS1
                gating_16tiles.GS2 = GS2
                gating_16tiles.GS3 = GS3
                C.wait("T", wd, consts_wd)
                C.wait("T", vm, nrm_v)

                def logits(tb):
                    ts = slice(tb * TB, (tb + 1) * TB)
                    for c in range(NC_D):
                        r = mm(pb[4][0:8, :], RW8[:, c, :], XN[:, c, ts],
                               start=(c == 0), stop=(c == NC_D - 1),
                               inc=tm if c == NC_D - 1 else None)
                    return r

                wt_done = gating_16tiles(logits, WT, "l1")
                C.wait("T", sm, wt_done)
                if STAGE == "l1g":
                    C.wait("V", sm, wt_done)
                    vop("tensor_copy", X[0:8, 0, :], WT[:], inc=vm)
                    return vm.n
            WGU = sbuf("WGU", [128, 2, 8, 1024], BF16, stack=fstack)
            WDS = sbuf("WDS", [128, 8, 1024], BF16, stack=fstack)
            SG = sbuf("SG1", [128, TB], BF16, stack=fstack)
            UU = sbuf("UU1", [128, TB], BF16, stack=fstack)
            MID = sbuf("MID1", [128, 8, TB], BF16, stack=fstack)

            guT = dr["L1f_gate_upT"]
            dnT = dr["L1f_downT"]

            def stream_gu(e, sg):
                if e == 8:
                    gsrc = dr["L1f_sh_gate_upT"].rearrange("(c p) m -> p c m", p=128)
                    t1 = dma("G", WGU[:, 0, :, :], gsrc[:, :, 0:1024], wd)
                    t2 = dma("G", WGU[:, 1, :, :], gsrc[:, :, 1024:2048], wd)
                else:
                    gsrc = guT[e].rearrange("(c p) m -> p c m", p=128)
                    t1 = dma("G", WGU[:, 0, :, :], gsrc[:, :, 1024 * sg:1024 * (sg + 1)], wd)
                    t2 = dma("G", WGU[:, 1, :, :], gsrc[:, :, 3072 + 1024 * sg:3072 + 1024 * (sg + 1)], wd)
                return max(t1, t2)

            def stream_ds(e, sg):
                if e == 8:
                    return dma("G", WDS[:], dr["L1f_sh_downT"].rearrange("(c p) m -> p c m", p=128), wd)
                return dma("G", WDS[:], dnT[e].rearrange("(c p) m -> p c m", p=128)
                           [:, 8 * sg:8 * (sg + 1), :], wd)

            steps = [(e, sg) for e in range(8) for sg in range(3)] + [(8, 0)]
            if L1STEPS is not None:
                steps = steps[:L1STEPS] if L1STEPS > 0 else [steps[-1]]
            bank_free = {i: 0 for i in range(7)}
            last_mid = 0
            C.wait("G", sm, sm.n)
            C.wait("G", vm, vm.n)
            C.wait("G", tm, tm.n)
            wgu_tgt = stream_gu(*steps[0])
            wds_tgt = stream_ds(*steps[0])
            wrow_ld = 0
            last_wbc = 0
            cur_e = None
            for si, (e, sg) in enumerate(steps):
                wgu_cur, wds_cur = wgu_tgt, wds_tgt
                if e != 8 and e != cur_e:
                    cur_e = e
                    C.wait("Y", sm, wt_done)
                    if wrow_ld:
                        C.wait("Y", tm, last_wbc)
                    wrow_ld = dma("Y", WROW[:], WT[e:e + 1, :], ld)
                C.wait("T", wd, max(wgu_cur, wds_cur))
                for tb in range(NTB):
                    ts = slice(tb * TB, (tb + 1) * TB)
                    if e != 8:
                        C.wait("T", ld, wrow_ld)
                        C.wait("T", vm, bank_free[4])
                        wbc = mm(pb[4][:, :], ONESR_B[:], WROW[0:1, ts],
                                 start=True, stop=True, inc=tm)
                        last_wbc = wbc
                    for fc in range(NC_D):
                        C.wait("T", sm, bank_free[0])
                        for c in range(NC_D):
                            mm(pb[0][:, :], WGU[:, 0, c, 128 * fc:128 * fc + 128],
                               XN[:, c, ts], start=(c == 0), stop=(c == NC_D - 1),
                               inc=tm if c == NC_D - 1 else None)
                        g_done = tm.n
                        C.wait("T", vm, bank_free[1])
                        for c in range(NC_D):
                            mm(pb[1][:, :], WGU[:, 1, c, 128 * fc:128 * fc + 128],
                               XN[:, c, ts], start=(c == 0), stop=(c == NC_D - 1),
                               inc=tm if c == NC_D - 1 else None)
                        u_done = tm.n
                        C.wait("S", tm, g_done)
                        C.wait("S", vm, last_mid)
                        sg_t = act("S", SG[:], pb[0][:, :], AF.Silu, inc=sm)
                        bank_free[0] = sg_t
                        C.wait("V", tm, u_done)
                        vop("tensor_copy", UU[:], pb[1][:, :])
                        C.wait("V", sm, sg_t)
                        if e == 8:
                            mv = vop("scalar_tensor_tensor", MID[:, fc, :], SG[:], 0.5, UU[:],
                                     ALU.mult, ALU.mult, inc=vm)
                        else:
                            C.wait("V", tm, wbc)
                            vop("tensor_tensor", MID[:, fc, :], SG[:], UU[:], ALU.mult)
                            mv = vop("tensor_tensor", MID[:, fc, :], MID[:, fc, :],
                                     pb[4][:, :], ALU.mult, inc=vm)
                        bank_free[1] = mv
                        last_mid = mv
                        if e != 8:
                            bank_free[4] = mv
                    mid_done = vm.n
                    if tb == NTB - 1 and si + 1 < len(steps):
                        C.wait("G", tm, tm.n)
                        wgu_tgt = stream_gu(*steps[si + 1])
                    C.wait("T", vm, mid_done)
                    for dc in range(NC_D):
                        db = 2 + (dc % 2)
                        C.wait("T", vm, bank_free[db])
                        for fc in range(NC_D):
                            mm(pb[db][:, :], WDS[:, fc, 128 * dc:128 * dc + 128],
                               MID[:, fc, :], start=(fc == 0), stop=(fc == NC_D - 1),
                               inc=tm if fc == NC_D - 1 else None)
                        d_done = tm.n
                        C.wait("V", tm, d_done)
                        av = vop("tensor_tensor", X[:, dc, ts], X[:, dc, ts], pb[db][:, :],
                                 ALU.add, inc=vm)
                        bank_free[db] = av
                if si + 1 < len(steps):
                    C.wait("G", tm, tm.n)
                    wds_tgt = stream_ds(*steps[si + 1])
            return vm.n


        def build_phm_w(Wsb, a_name, s_name, bscratch):
            A_sb, S_sb, Wtmp = bscratch
            C.wait("G", tm, tm.n)
            a_t = dma("G", A_sb[:], dr[a_name][:], wd)
            C.wait("T", wd, a_t)
            last_rr = build_phm_w.last_rr
            for kk in range(8):
                for half in range(2):
                    base = 8192 * kk + 4096 * half
                    C.wait("G", tm, tm.n)
                    s_t = dma("G", S_sb[:], dr[s_name][:, base:base + 4096], wd)
                    C.wait("T", wd, s_t)
                    C.wait("S", ld, last_rr)
                    ev = 0
                    for j2 in range(8):
                        bk = j2 % 2
                        if j2 >= 2:
                            C.wait("T", sm, ev - 1)
                        r = mm(pb[bk][0:16, :], A_sb[:], S_sb[:, 512 * j2:512 * (j2 + 1)],
                               start=True, stop=True, inc=tm)
                        C.wait("S", tm, r)
                        ev = act("S", Wtmp[:, 512 * j2:512 * (j2 + 1)], pb[bk][0:16, :],
                                 AF.Identity, inc=sm)
                    C.wait("Y", sm, ev)
                    kglob = 32 * kk + 16 * half
                    p0 = kglob % 128
                    kh = kglob // 128
                    par = (kk * 2 + half) % 2
                    wr = dma("Y", wscr[par], Wtmp[:], ld)
                    C.wait("Y", ld, wr)
                    for i in range(4):
                        dma("Y",
                            Wsb[p0:p0 + 16, 2 * i + kh, :].rearrange("k (j l) -> k j l", j=4),
                            wscr[par][4 * i:4 * i + 4, :].rearrange("j (k l) -> k j l", k=16),
                            ld)
                    last_rr = ld.n
            build_phm_w.last_rr = last_rr
            return ld.n

        build_phm_w.last_rr = 0

        def l1attn(astack):
            XN = sbuf("XN_a1", [128, NC_D, T], BF16, stack=astack)
            QN = sbuf("QN", [128, NC_D, T], BF16, stack=astack)
            xn_v = emit_ln(X, True, XN)
            qn_v = emit_ln(XN, False, QN, affine=("g1", "b1"), eps=1)
            with ExitStack() as w3:
                WQ_ = sbuf("WQp", [128, 8, 1024], BF16, stack=w3)
                WK_ = sbuf("WKp", [128, 8, 1024], BF16, stack=w3)
                WV_ = sbuf("WVp", [128, 8, 1024], BF16, stack=w3)
                with ExitStack() as bs:
                    A_sb = sbuf("A_sb", [4, 16], BF16, stack=bs)
                    S_sb = sbuf("S_sb", [4, 4096], BF16, stack=bs)
                    Wtmp = sbuf("Wtmp", [16, 4096], BF16, stack=bs)
                    bscratch = (A_sb, S_sb, Wtmp)
                    build_phm_w(WQ_, "L1a_qA", "L1a_qS", bscratch)
                    build_phm_w(WK_, "L1a_kA", "L1a_kS", bscratch)
                    wv_done = build_phm_w(WV_, "L1a_vA", "L1a_vS", bscratch)
                    barrier()
                with ExitStack() as ps:
                    INDD = sbuf("INDD", [128, 8, 16], BF16, stack=ps)
                    INDB = sbuf("INDB", [16, 8, 128], BF16, stack=ps)
                    C.wait("G", tm, tm.n)
                    dma("G", INDD[:], dr["L1a_ind_dots"].rearrange("c p e -> p c e"), wd)
                    i_t = dma("G", INDB[:], dr["L1a_ind_bc"].rearrange("c j p -> j c p"), wd)
                    Q0 = sbuf("Q0", [128, TB], BF16, stack=ps)
                    Q1 = sbuf("Q1", [128, TB], BF16, stack=ps)
                    K0 = sbuf("K0", [128, TB], BF16, stack=ps)
                    K1 = sbuf("K1", [128, TB], BF16, stack=ps)
                    PR = sbuf("PR", [128, 2, TB], BF16, stack=ps)
                    V0, V1 = Q0, Q1
                    T1_ = K0
                    T2_ = K1
                    DOTS = sbuf("DOTS", [16, 4, TB], BF16, stack=ps)
                    COEF = sbuf("COEF", [16, 4, TB], BF16, stack=ps)
                    DEN = XB
                    C.wait("T", vm, qn_v)
                    C.wait("T", wd, i_t)
                    ev_guard = {0: 0, 1: 0, 2: 0}
                    vev = {0: 0, 1: 0}
                    prv = [0, 0]
                    att_v = 0
                    for tbp in range(2):
                        cs0 = slice(512 * tbp, 512 * (tbp + 1))
                        cs1 = slice(1024 + 512 * tbp, 1536 + 512 * tbp)
                        for c in range(NC_D):
                            specs = [(Q0, WQ_, QN, cs0, "qb", 0), (Q1, WQ_, QN, cs1, "qb", 1),
                                     (K0, WK_, XN, cs0, "kb", 2), (K1, WK_, XN, cs1, "kb", 0)]
                            if c > 0 or tbp > 0:
                                C.wait("S", vm, prod_v)
                            for di, (dst, Wp, src, cs, bnm, bk) in enumerate(specs):
                                C.wait("T", sm, ev_guard[bk])
                                for kc in range(NC_D):
                                    r = mm(pb[bk][:, :], Wp[:, kc, 128 * c:128 * c + 128],
                                           src[:, kc, cs], start=(kc == 0), stop=(kc == NC_D - 1),
                                           inc=tm if kc == NC_D - 1 else None)
                                C.wait("S", tm, r)
                                fn = AF.Relu
                                ev_guard[bk] = act("S", dst[:], pb[bk][:, :], fn,
                                                   bias=bias_ap(bnm, c), inc=sm)
                            C.wait("V", sm, ev_guard[0])
                            C.wait("V", sm, ev_guard[1])
                            C.wait("V", sm, ev_guard[2])
                            C.wait("V", tm, prv[0])
                            vop("tensor_tensor", PR[:, 0, :], Q0[:], K0[:], ALU.mult)
                            p1 = vop("tensor_tensor", PR[:, 1, :], Q0[:], K1[:], ALU.mult, inc=vm)
                            C.wait("T", vm, p1)
                            mm(pb[3][0:16, :], INDD[:, c, :], PR[:, 0, :],
                               start=(c == 0), stop=(c == NC_D - 1))
                            r1 = mm(pb[4][0:16, :], INDD[:, c, :], PR[:, 1, :],
                                    start=(c == 0), stop=(c == NC_D - 1), inc=tm)
                            prv[0] = r1
                            C.wait("V", tm, prv[1])
                            vop("tensor_tensor", PR[:, 0, :], Q1[:], K0[:], ALU.mult)
                            p2 = vop("tensor_tensor", PR[:, 1, :], Q1[:], K1[:], ALU.mult, inc=vm)
                            C.wait("T", vm, p2)
                            mm(pb[5][0:16, :], INDD[:, c, :], PR[:, 0, :],
                               start=(c == 0), stop=(c == NC_D - 1))
                            r2 = mm(pb[6][0:16, :], INDD[:, c, :], PR[:, 1, :],
                                    start=(c == 0), stop=(c == NC_D - 1), inc=tm)
                            prv[1] = r2
                            prod_v = p2
                        dots_done = tm.n
                        C.wait("V", tm, dots_done)
                        for k in range(4):
                            vop("tensor_copy", DOTS[:, k, :], pb[2 + 1 + k][0:16, :])
                        vop("scalar_tensor_tensor", DEN[0:16, 0, :], DOTS[:, 1, :], 0.70710678,
                            DOTS[:, 0, :], ALU.mult, ALU.add)
                        vop("scalar_tensor_tensor", DEN[0:16, 1, :], DOTS[:, 2, :], 0.70710678,
                            DOTS[:, 3, :], ALU.mult, ALU.add)
                        vop("tensor_scalar", DEN[0:16, 0, :], DEN[0:16, 0, :], 1e-6, None, ALU.max)
                        vop("tensor_scalar", DEN[0:16, 1, :], DEN[0:16, 1, :], 1e-6, None, ALU.max)
                        def _rcp(e, o, i):
                            with nc.allow_low_precision(reason="bf16 1/z within tolerance"):
                                e.reciprocal(o, i)
                        C.emit("V", lambda e, o=T1_[0:16, 0:TB], i=DEN[0:16, 0, :]: _rcp(e, o, i))
                        C.emit("V", lambda e, o=T2_[0:16, 0:TB], i=DEN[0:16, 1, :]: _rcp(e, o, i))
                        vop("tensor_tensor", COEF[:, 0, :], DOTS[:, 0, :], T1_[0:16, 0:TB], ALU.mult)
                        vop("scalar_tensor_tensor", COEF[:, 1, :], DOTS[:, 1, :], 0.70710678,
                            T1_[0:16, 0:TB], ALU.mult, ALU.mult)
                        vop("scalar_tensor_tensor", COEF[:, 2, :], DOTS[:, 2, :], 0.70710678,
                            T2_[0:16, 0:TB], ALU.mult, ALU.mult)
                        cf = vop("tensor_tensor", COEF[:, 3, :], DOTS[:, 3, :], T2_[0:16, 0:TB],
                                 ALU.mult, inc=vm)
                        C.wait("T", vm, cf)
                        for c in range(NC_D):
                            if att_v:
                                C.wait("S", vm, att_v)
                            for di, (dst, cs) in enumerate([(V0, cs0), (V1, cs1)]):
                                C.wait("T", sm, vev[di])
                                for kc in range(NC_D):
                                    r = mm(pb[di][:, :], WV_[:, kc, 128 * c:128 * c + 128],
                                           XN[:, kc, cs], start=(kc == 0), stop=(kc == NC_D - 1),
                                           inc=tm if kc == NC_D - 1 else None)
                                C.wait("S", tm, r)
                                vev[di] = act("S", dst[:], pb[di][:, :], AF.Identity,
                                              bias=bias_ap("vb", c), inc=sm)
                            C.wait("T", vm, att_v)
                            mm(pb[2][:, :], INDB[:, c, :], COEF[:, 0, :], start=True, stop=True)
                            mm(pb[3][:, :], INDB[:, c, :], COEF[:, 1, :], start=True, stop=True)
                            mm(pb[4][:, :], INDB[:, c, :], COEF[:, 2, :], start=True, stop=True)
                            bc4 = mm(pb[5][:, :], INDB[:, c, :], COEF[:, 3, :],
                                     start=True, stop=True, inc=tm)
                            C.wait("V", tm, bc4)
                            C.wait("V", sm, vev[0])
                            C.wait("V", sm, vev[1])
                            vop("tensor_tensor", T1_[:], V0[:], pb[2][:, :], ALU.mult)
                            vop("tensor_tensor", T2_[:], V1[:], pb[3][:, :], ALU.mult)
                            vop("tensor_tensor", T1_[:], T1_[:], T2_[:], ALU.add)
                            vop("tensor_tensor", QN[:, c, cs0], T1_[:], QN[:, c, cs0], ALU.add)
                            vop("tensor_tensor", T1_[:], V0[:], pb[4][:, :], ALU.mult)
                            vop("tensor_tensor", T2_[:], V1[:], pb[5][:, :], ALU.mult)
                            vop("tensor_tensor", T1_[:], T1_[:], T2_[:], ALU.add)
                            att_v = vop("tensor_tensor", QN[:, c, cs1], T1_[:], QN[:, c, cs1],
                                        ALU.add, inc=vm)
                    barrier()
            ln2_v = emit_ln(QN, False, XN, affine=("g2", "b2"), eps=1)
            with ExitStack() as os_:
                WO_ = sbuf("WOp", [128, 8, 1024], BF16, stack=os_)
                A_sb = sbuf("A_sb2", [4, 16], BF16, stack=os_)
                S_sb = sbuf("S_sb2", [4, 4096], BF16, stack=os_)
                Wtmp = sbuf("Wtmp2", [16, 4096], BF16, stack=os_)
                wo_done = build_phm_w(WO_, "L1a_oA", "L1a_oS", (A_sb, S_sb, Wtmp))
                C.wait("T", ld, wo_done)
                C.wait("T", vm, vm.n)
                C.wait("T", sm, sm.n)
                pfree = {0: 0, 1: 0}
                for mc in range(NC_D):
                    for tb in range(NTB):
                        ts = slice(tb * TB, (tb + 1) * TB)
                        bk = tb % 2
                        C.wait("T", vm, pfree[bk])
                        for kc in range(NC_D):
                            r = mm(pb[bk][:, :], WO_[:, kc, 128 * mc:128 * mc + 128],
                                   XN[:, kc, ts], start=(kc == 0), stop=(kc == NC_D - 1),
                                   inc=tm if kc == NC_D - 1 else None)
                        C.wait("V", tm, r)
                        vop("scalar_tensor_tensor", XB[:, 0, :],
                            pb[bk][:, :], bias_ap("ob", mc), XN[:, mc, ts], ALU.add, ALU.add)
                        pfree[bk] = vop("tensor_tensor", X[:, mc, ts], X[:, mc, ts],
                                        XB[:, 0, :], ALU.add, inc=vm)
            return vm.n

        C.wait("S", ld, x_ld)
        C.wait("V", ld, consts_ld)
        C.wait("T", ld, consts_ld)

        if STAGE == "load":
            pass
        elif STAGE == "ln":
            with ExitStack() as fstack:
                XN = sbuf("XN_f1", [128, NC_D, T], BF16, stack=fstack)
                xn_v = emit_ln(X, True, XN)
                C.wait("V", vm, xn_v)
                for c in range(NC_D):
                    vop("tensor_copy", X[:, c, :], XN[:, c, :], inc=vm)
                barrier()
        else:
            if "l0ffn1" in phases:
                with ExitStack() as fstack:
                    XN = sbuf("XN_f1", [128, NC_D, T], BF16, stack=fstack)
                    xn_v = emit_ln(X, True, XN)
                    x_v = l0ffn(XN, xn_v, fstack)
                    barrier()
            if "l0attn" in phases and STAGE is None:
                with ExitStack() as astack:
                    l0attn(astack)
                    barrier()
            if "l0ffn2" in phases and STAGE is None:
                with ExitStack() as fstack:
                    XN = sbuf("XN_f2", [128, NC_D, T], BF16, stack=fstack)
                    xn_v = emit_ln(X, True, XN)
                    x_v = l0ffn(XN, xn_v, fstack)
                    barrier()
            if "l1ffn1" in phases and STAGE in (None, "l1g", "l1n", "l1n0", "l1nc"):
                with ExitStack() as fstack:
                    XN = sbuf("XN_g1", [128, NC_D, T], BF16, stack=fstack)
                    xn_v = emit_ln(X, True, XN)
                    x_v = l1ffn(XN, xn_v, fstack)
                    barrier()
            if "l1attn" in phases and STAGE is None:
                with ExitStack() as astack:
                    l1attn(astack)
                    barrier()
            if "l1ffn2" in phases and STAGE is None:
                with ExitStack() as fstack:
                    XN = sbuf("XN_g2", [128, NC_D, T], BF16, stack=fstack)
                    xn_v = emit_ln(X, True, XN)
                    x_v = l1ffn(XN, xn_v, fstack)
                    barrier()

        C.wait("Y", vm, vm.n)
        st_t = dma("Y", out_t.rearrange("(c p) t -> p c t", p=128),
                   X[:], st)
        C.wait("Y", st, st_t)

        with nc.Block() as block:
            @block.tensor
            def _(t):
                for f in C.streams["T"]:
                    f(t)

            @block.vector
            def _(v):
                for f in C.streams["V"]:
                    f(v)

            @block.scalar
            def _(s):
                for f in C.streams["S"]:
                    f(s)

            @block.gpsimd
            def _(g):
                for f in C.streams["G"]:
                    f(g)

            @block.sync
            def _(y):
                for f in C.streams["Y"]:
                    f(y)

    return nc


def make_inputs(core, inp):
    i = core
    x = np.asarray(inp["L0_attn_x"])
    xs = x[:, 1024 * i:1024 * (i + 1), :]
    xT = np.ascontiguousarray(xs.reshape(T, D).T)
    d = {"xT": xT,
         "ones_col": np.ones((128, 1), np.float32),
         "ones_row": np.ones((1, 128), np.float32),
         "idn8": np.eye(8, dtype=np.float32),
         "idn128": np.eye(128, dtype=np.float32)}
    g = lambda k: np.ascontiguousarray(np.asarray(inp[k], dtype=np.float32))
    bv = lambda k: np.ascontiguousarray(np.asarray(inp[k], dtype=np.float32).reshape(-1, 128).T)
    d["L0f_gate_w"] = g("L0_ffn_gate_w")
    d["L0f_sw_gate"] = g("L0_ffn_sw_gate")
    d["L0f_sw_up"] = g("L0_ffn_sw_up")
    d["L0f_sw_down"] = g("L0_ffn_sw_down")
    d["L0f_ew_gate"] = g("L0_ffn_ew_gate")
    d["L0f_ew_up"] = g("L0_ffn_ew_up")
    d["L0f_ew_down"] = g("L0_ffn_ew_down")
    d["L0a_wq_dT"] = np.ascontiguousarray(g("L0_attn_wq_d").T)
    d["L0a_bq_d"] = bv("L0_attn_bq_d")
    d["L0a_w_qkT"] = np.ascontiguousarray(g("L0_attn_w_qk").T)
    d["L0a_b_qk"] = np.ascontiguousarray(g("L0_attn_b_qk").reshape(16, 64).T)
    d["L0a_wkv_dT"] = np.ascontiguousarray(g("L0_attn_wkv_d").T)
    d["L0a_bkv_d"] = np.ascontiguousarray(g("L0_attn_bkv_d").reshape(64, 1))
    d["L0a_wv_uT"] = np.ascontiguousarray(g("L0_attn_wv_u").T)
    d["L0a_bv_u"] = np.ascontiguousarray(g("L0_attn_bv_u").reshape(1, D))
    d["L0a_w_projT"] = np.ascontiguousarray(g("L0_attn_w_proj").T)
    d["L0a_b_proj"] = bv("L0_attn_b_proj")
    d["L1f_ln_gamma"] = bv("L1_ffn_ln_gamma")
    d["L1f_ln_beta"] = bv("L1_ffn_ln_beta")
    d["L1f_router_wT"] = np.ascontiguousarray(g("L1_ffn_router_w").T)
    d["L1f_gate_upT"] = np.ascontiguousarray(g("L1_ffn_gate_up_w").transpose(0, 2, 1))
    d["L1f_downT"] = np.ascontiguousarray(g("L1_ffn_down_w").transpose(0, 2, 1))
    d["L1f_sh_gate_upT"] = np.ascontiguousarray(g("L1_ffn_shared_gate_up_w").T)
    d["L1f_sh_downT"] = np.ascontiguousarray(g("L1_ffn_shared_down_w").T)
    for p in "qkvo":
        d[f"L1a_{p}A"] = np.ascontiguousarray(g(f"L1_attn_{p}A").reshape(4, 16))
        d[f"L1a_{p}S"] = np.ascontiguousarray(g(f"L1_attn_{p}S").reshape(4, 65536))
        d[f"L1a_{p}b"] = bv(f"L1_attn_{p}b")
    for nm in ["g1", "b1", "g2", "b2"]:
        d[f"L1a_{nm}"] = bv(f"L1_attn_{nm}")
    ind_dots = np.zeros((8, 128, 16), np.float32)
    ind_bc = np.zeros((8, 16, 128), np.float32)
    for c in range(8):
        ind_dots[c, 0:64, 2 * c] = 1.0
        ind_dots[c, 64:128, 2 * c + 1] = 1.0
        ind_bc[c, 2 * c, 0:64] = 1.0
        ind_bc[c, 2 * c + 1, 64:128] = 1.0
    d["L1a_ind_dots"] = ind_dots
    d["L1a_ind_bc"] = ind_bc
    d["L1a_ind_bc7"] = ind_bc * np.float32(np.cos(np.pi / 4))
    return d


def kernel(**inputs):
    nc = build_program(PHASES)
    in_maps = [make_inputs(i, inputs) for i in range(NCORES)]
    res = run_bass_kernel_spmd(nc, in_maps, core_ids=list(range(NCORES)))
    outs = [res.results[i]["out"] for i in range(NCORES)]
    full = np.stack([o.T.reshape(2, 1024, 1024) for o in outs], axis=1)
    return np.ascontiguousarray(full.reshape(2, 8192, 1024))


# version 03
# speedup vs baseline: 1.1554x; 1.1554x over previous
import numpy as np
from contextlib import ExitStack

import concourse.bass as bass
import concourse.mybir as mybir
from concourse.bass_utils import run_bass_kernel_spmd

F32 = mybir.dt.float32
BF16 = mybir.dt.bfloat16
AF = mybir.ActivationFunctionType
ALU = mybir.AluOpType
AX = mybir.AxisListType

NCORES = 8
D = 1024
T = 2048
NC_D = 8
NTB = 4
TB = 512

PHASES = ("l0ffn1", "l0attn", "l0ffn2", "l1ffn1", "l1attn", "l1ffn2")
L1STEPS = None
STAGE = None


class Sem:
    def __init__(self, h):
        self.h = h
        self.n = 0

    def inc(self, amt=1):
        self.n += amt
        return self.n


class Ctx:
    def __init__(self, nc):
        self.nc = nc
        self.streams = {e: [] for e in "TVSGY"}

    def emit(self, eng, fn):
        self.streams[eng].append(fn)

    def wait(self, eng, sem, val):
        assert val <= sem.n, f"wait target {val} > emitted {sem.n} on {eng}"
        if val > 0:
            self.emit(eng, lambda e, s=sem.h, v=val: e.wait_ge(s, v))


def build_program(phases=PHASES):
    nc = bass.Bass("TRN2", target_bir_lowering=False, debug=False)

    dr = {}

    def din(name, shape, dtype=F32):
        dr[name] = nc.dram_tensor(name, list(shape), dtype, kind="ExternalInput").ap()
        return dr[name]

    din("xT", [D, T])
    din("ones_col", [128, 1])
    din("ones_row", [1, 128])
    din("idn8", [8, 8])
    din("idn128", [128, 128])
    din("L0f_gate_w", [D, 8])
    din("L0f_sw_gate", [D, D])
    din("L0f_sw_up", [D, D])
    din("L0f_sw_down", [D, D])
    din("L0f_ew_gate", [8, D, D])
    din("L0f_ew_up", [8, D, D])
    din("L0f_ew_down", [8, D, D])
    din("L0a_wq_dT", [D, 512])
    din("L0a_bq_d", [128, 4])
    din("L0a_w_qkT", [512, D])
    din("L0a_b_qk", [64, 16])
    din("L0a_wkv_dT", [D, 64])
    din("L0a_bkv_d", [64, 1])
    din("L0a_wv_uT", [64, D])
    din("L0a_bv_u", [1, D])
    din("L0a_w_projT", [D, D])
    din("L0a_b_proj", [128, 8])
    din("L1f_ln_gamma", [128, 8])
    din("L1f_ln_beta", [128, 8])
    din("L1f_router_wT", [D, 8])
    din("L1f_gate_upT", [8, D, 6144])
    din("L1f_downT", [8, 3072, D])
    din("L1f_sh_gate_upT", [D, 2048])
    din("L1f_sh_downT", [1024, D])
    for p in "qkvo":
        din(f"L1a_{p}A", [4, 16])
        din(f"L1a_{p}S", [4, 65536])
        din(f"L1a_{p}b", [128, 8])
    for nm in ["g1", "b1", "g2", "b2"]:
        din(f"L1a_{nm}", [128, 8])
    din("L1a_ind_dots", [8, 128, 16])
    din("L1a_ind_bc", [8, 16, 128])
    din("L1a_ind_bc7", [8, 16, 128])

    out_t = nc.dram_tensor("out", [D, T], F32, kind="ExternalOutput").ap()
    x_spill = nc.dram_tensor("x_spill", [D, T], F32).ap()
    ckv_in = nc.dram_tensor("ckv_in", [64, T], BF16).ap()
    wscr = nc.dram_tensor("wscr", [2, 16, 4096], BF16).ap()
    ckv_out = nc.dram_tensor("ckv_out", [NCORES * 64, T], BF16).ap()

    es = ExitStack()
    with es:
        _name_ctr = {}

        def sbuf(name, shape, dtype, stack=es):
            n = _name_ctr.get(name, 0)
            _name_ctr[name] = n + 1
            if n:
                name = f"{name}_{n}"
            return stack.enter_context(nc.sbuf_tensor(name, list(shape), dtype))

        X = sbuf("X", [128, NC_D, T], F32)
        XB = sbuf("XB", [128, 2, TB], BF16)
        SQ = sbuf("SQ", [128, 2, TB], BF16)
        ROWS = sbuf("ROWS", [1, 4, TB], F32)
        ONESC = sbuf("ONESC", [128, 1], BF16)
        ONESR_B = sbuf("ONESR_B", [1, 128], BF16)
        ONESR_F = sbuf("ONESR_F", [1, 128], F32)
        IDN8 = sbuf("IDN8", [8, 8], F32)
        IDN128 = sbuf("IDN128", [128, 128], F32)
        BIASV = sbuf("BIASV", [128, 136], F32)
        GW8 = sbuf("GW8", [128, 8, 8], BF16)
        RW8 = sbuf("RW8", [128, 8, 8], BF16)
        EPSC = sbuf("EPSC", [1, 2], F32)

        pb = [es.enter_context(nc.psum_tensor(f"pb{i}", [128, 512], F32)) for i in range(7)]

        sems = {}
        for name in ["ld", "wd", "tm", "vm", "sm", "gm", "cc", "st"]:
            sems[name] = Sem(es.enter_context(nc.semaphore(name)))
        ld, wd, tm, vm, sm = sems["ld"], sems["wd"], sems["tm"], sems["vm"], sems["sm"]
        gm, ccs, st = sems["gm"], sems["cc"], sems["st"]

        C = Ctx(nc)

        def dma(stream, out_ap, in_ap, sem):
            tgt = sem.inc(16)
            C.emit(stream, lambda e, o=out_ap, i=in_ap, s=sem.h:
                   e.dma_start(out=o, in_=i).then_inc(s, 16))
            return tgt

        def mm(out_ap, lhsT_ap, rhs_ap, start, stop, inc=None):
            if inc is not None:
                tgt = inc.inc()
                C.emit("T", lambda e, o=out_ap, l=lhsT_ap, r=rhs_ap, a=start, b=stop, s=inc.h:
                       e.matmul(o, l, r, start=a, stop=b).then_inc(s, 1))
                return tgt
            C.emit("T", lambda e, o=out_ap, l=lhsT_ap, r=rhs_ap, a=start, b=stop:
                   e.matmul(o, l, r, start=a, stop=b))

        def trans(out_ap, in_ap, ident, inc=None):
            if inc is not None:
                tgt = inc.inc()
                C.emit("T", lambda e, o=out_ap, i=in_ap, d=ident, s=inc.h:
                       e.transpose(o, i, d).then_inc(s, 1))
                return tgt
            C.emit("T", lambda e, o=out_ap, i=in_ap, d=ident: e.transpose(o, i, d))

        def act(eng, out_ap, in_ap, func, bias=0.0, scale=1.0, accum=None, inc=None):
            if inc is not None:
                tgt = inc.inc()
                C.emit(eng, lambda e, o=out_ap, i=in_ap, fu=func, b=bias, sc=scale, ac=accum, s=inc.h:
                       e.activation(o, i, fu, bias=b, scale=sc, accum_out=ac).then_inc(s, 1))
                return tgt
            C.emit(eng, lambda e, o=out_ap, i=in_ap, fu=func, b=bias, sc=scale, ac=accum:
                   e.activation(o, i, fu, bias=b, scale=sc, accum_out=ac))

        def vop(name, *args, inc=None, **kw):
            if inc is not None:
                tgt = inc.inc()
                C.emit("V", lambda e, a=args, k=kw, s=inc.h:
                       getattr(e, name)(*a, **k).then_inc(s, 1))
                return tgt
            C.emit("V", lambda e, a=args, k=kw: getattr(e, name)(*a, **k))

        def barrier():
            for eng in "TVSGY":
                for s in (tm, vm, sm, wd, ld, gm, st):
                    if s.n:
                        C.wait(eng, s, s.n)

        C.emit("V", lambda e: e.memset(EPSC[0:1, 0:1], 1e-5))
        C.emit("V", lambda e: e.memset(EPSC[0:1, 1:2], 1e-6))
        dma("G", ONESC[:], dr["ones_col"][:], wd)
        dma("G", ONESR_B[:], dr["ones_row"][:], wd)
        dma("G", GW8[:],
            dr["L0f_gate_w"].rearrange("(c p) e -> p c e", p=128), wd)
        dma("G", RW8[:],
            dr["L1f_router_wT"].rearrange("(c p) e -> p c e", p=128), wd)
        consts_wd = wd.n
        dma("Y", ONESR_F[:], dr["ones_row"][:], ld)
        dma("Y", IDN8[:], dr["idn8"][:], ld)
        dma("Y", IDN128[:], dr["idn128"][:], ld)
        x_ld = dma("Y", X[:],
                   dr["xT"].rearrange("(c p) t -> p c t", p=128), ld)

        bias_slots = {}
        bias_next = [0]

        def load_bias(name, dram_name, nchunk):
            base = bias_next[0]
            bias_next[0] += nchunk
            bias_slots[name] = base
            return dma("Y", BIASV[0:dr[dram_name].shape[0], base:base + nchunk],
                       dr[dram_name][:], ld)

        def bias_ap(name, c):
            b = bias_slots[name]
            return BIASV[:, b + c:b + c + 1]

        for nm, dnm, nch in [("qb", "L1a_qb", 8), ("kb", "L1a_kb", 8),
                         ("vb", "L1a_vb", 8), ("ob", "L1a_ob", 8),
                         ("ln_g", "L1f_ln_gamma", 8), ("ln_b", "L1f_ln_beta", 8),
                             ("b_proj", "L0a_b_proj", 8), ("bq_d", "L0a_bq_d", 4),
                             ("b_qk", "L0a_b_qk", 16), ("bkv_d", "L0a_bkv_d", 1),
                             ("g1", "L1a_g1", 8), ("b1", "L1a_b1", 8),
                             ("g2", "L1a_g2", 8), ("b2", "L1a_b2", 8)]:
            load_bias(nm, dnm, nch)
        bias_slots["bqk8"] = bias_next[0]
        bias_next[0] += 16
        C.wait("V", ld, ld.n)
        vop("tensor_scalar", BIASV[0:64, bias_slots["bqk8"]:bias_slots["bqk8"] + 16],
            BIASV[0:64, bias_slots["b_qk"]:bias_slots["b_qk"] + 16], 0.125, None, ALU.mult)
        consts_ld = ld.n

        def emit_ln(src_sb, src_is_f32, dst_sb, affine=None, eps=0):
            inv = 1.0 / D
            prev_lv = 0
            for tb in range(NTB):
                ts = slice(tb * TB, (tb + 1) * TB)
                mm_done = [0] * NC_D
                for c in range(NC_D):
                    if c >= 2:
                        C.wait("S", tm, mm_done[c - 2])
                    if src_is_f32:
                        act("S", XB[:, c % 2, :], src_sb[:, c, ts], AF.Identity)
                        stat = XB[:, c % 2, :]
                    else:
                        stat = src_sb[:, c, ts]
                    cast_t = act("S", SQ[:, c % 2, :], src_sb[:, c, ts], AF.Square, inc=sm)
                    C.wait("T", sm, cast_t)
                    mm(pb[5][0:1, :], ONESC[:], stat, start=(c == 0), stop=(c == NC_D - 1))
                    mm_done[c] = mm(pb[6][0:1, :], ONESC[:], SQ[:, c % 2, :],
                                    start=(c == 0), stop=(c == NC_D - 1), inc=tm)
                sums_done = tm.n
                C.wait("V", tm, sums_done)
                vop("tensor_scalar", ROWS[0:1, 0, :], pb[5][0:1, :], inv, None, ALU.mult)
                vop("tensor_scalar", ROWS[0:1, 1, :], pb[6][0:1, :], inv, None, ALU.mult)
                vop("tensor_tensor", ROWS[0:1, 2, :], ROWS[0:1, 0, :], ROWS[0:1, 0, :], ALU.mult)
                v_done = vop("tensor_tensor", ROWS[0:1, 1, :], ROWS[0:1, 1, :],
                             ROWS[0:1, 2, :], ALU.subtract, inc=vm)
                C.wait("S", vm, v_done)
                act("S", ROWS[0:1, 2, :], ROWS[0:1, 1, :], AF.Ln,
                    bias=EPSC[0:1, eps:eps + 1], scale=1.0)
                rstd_done = act("S", ROWS[0:1, 3, :], ROWS[0:1, 2, :], AF.Exp,
                                bias=0.0, scale=-0.5, inc=sm)
                C.wait("T", sm, rstd_done)
                mm(pb[5][:, :], ONESR_F[:], ROWS[0:1, 0, :], start=True, stop=True)
                bc_done = mm(pb[6][:, :], ONESR_F[:], ROWS[0:1, 3, :], start=True,
                             stop=True, inc=tm)
                C.wait("V", tm, bc_done)
                aff_s = 0
                for c in range(NC_D):
                    last = (c == NC_D - 1)
                    dst = dst_sb[:, c, ts]
                    vop("tensor_tensor", dst, src_sb[:, c, ts], pb[5][:, :], ALU.subtract)
                    if affine is None:
                        lv = vop("tensor_tensor", dst, dst, pb[6][:, :], ALU.mult,
                                 inc=vm if last else None)
                    else:
                        g_nm, b_nm = affine
                        lv = vop("tensor_tensor", dst, dst, pb[6][:, :], ALU.mult, inc=vm)
                        C.wait("S", vm, lv)
                        aff_s = act("S", dst, dst, AF.Identity,
                                    bias=bias_ap(b_nm, c), scale=bias_ap(g_nm, c),
                                    inc=sm if last else None)
                C.wait("T", vm, lv)
                if affine is not None:
                    C.wait("T", sm, aff_s)
                    lv = vm.n
                prev_lv = lv
            return vm.n

        def gating_16tiles(logits_src_fn, WT_dst, gates_mode, half_scale=True):
            GLOG = gating_16tiles.GLOG
            GTOK = gating_16tiles.GTOK
            GWRK = gating_16tiles.GWRK
            GS1 = gating_16tiles.GS1
            GS2 = gating_16tiles.GS2
            for tb in range(NTB):
                if tb > 0:
                    C.wait("T", sm, sm.n)
                r = logits_src_fn(tb)
                C.wait("S", tm, r)
                act("S", GLOG[:, tb * TB:(tb + 1) * TB], pb[4][0:8, :], AF.Identity, inc=sm)
            if STAGE == "g1":
                C.wait("V", sm, sm.n)
                vop("tensor_copy", X[0:8, 0, :], GLOG[:], inc=vm)
                return sm.n
            C.wait("T", sm, sm.n)
            for i in range(16):
                trans(pb[4][:, 8 * i:8 * i + 8], GLOG[0:8, i * 128:(i + 1) * 128], IDN8[:],
                      inc=tm if i == 15 else None)
            C.wait("V", tm, tm.n)
            vop("tensor_copy", GTOK[:].rearrange("p a b -> p (a b)"), pb[4][:, 0:128], inc=vm)
            if STAGE == "g2":
                vop("tensor_copy", X[:, 0, 0:128], GTOK[:].rearrange("p a b -> p (a b)"), inc=vm)
                return sm.n
            G_ = lambda i: GTOK[:, i, :]
            W_ = lambda i: GWRK[:, i, :]
            M1 = lambda i: GS1[:, i:i + 1]
            M2 = lambda i: GS2[:, i:i + 1]
            M3 = lambda i: gating_16tiles.GS3[:, i:i + 1]
            for i in range(16):
                vop("tensor_reduce", M2(i), G_(i), AX.X, ALU.max, negate=True,
                    inc=vm if i == 15 else None)
            C.wait("S", vm, vm.n)
            for i in range(16):
                act("S", G_(i), G_(i), AF.Exp, bias=M2(i), scale=1.0, accum=M1(i),
                    inc=sm if i == 15 else None)
            C.wait("V", sm, sm.n)
            for i in range(16):
                vop("reciprocal", M3(i), M1(i))
            for i in range(16):
                vop("tensor_scalar", G_(i), G_(i), M3(i), None, ALU.mult)
            for i in range(16):
                vop("tensor_reduce", M1(i), G_(i), AX.X, ALU.max)
            for i in range(16):
                vop("tensor_scalar", W_(i), G_(i), M1(i), None, ALU.is_ge)
            for i in range(16):
                vop("scalar_tensor_tensor", W_(i), W_(i), -1e30, G_(i), ALU.mult, ALU.add)
            for i in range(16):
                vop("tensor_reduce", M2(i), W_(i), AX.X, ALU.max)
            for i in range(16):
                vop("tensor_scalar", W_(i), G_(i), M2(i), None, ALU.is_ge)
            sc2 = 0.5 if half_scale else 1.0
            if gates_mode == "l0":
                for i in range(16):
                    vop("tensor_tensor", W_(i), W_(i), G_(i), ALU.mult)
                for i in range(16):
                    vop("tensor_reduce", M2(i), W_(i), AX.X, ALU.add)
                for i in range(16):
                    vop("tensor_scalar", M3(i), M2(i), 1e-20, None, ALU.add)
                for i in range(16):
                    vop("reciprocal", M1(i), M3(i))
                for i in range(16):
                    vop("tensor_scalar", W_(i), W_(i), M1(i), sc2, ALU.mult, ALU.mult,
                        inc=vm if i == 15 else None)
            else:
                for i in range(16):
                    vop("tensor_scalar", M3(i), M1(i), -1.0, None, ALU.mult,
                        inc=vm if i == 15 else None)
                C.wait("S", vm, vm.n)
                for i in range(16):
                    act("S", G_(i), G_(i), AF.Exp, bias=M3(i), scale=1.0,
                        inc=sm if i == 15 else None)
                C.wait("V", sm, sm.n)
                for i in range(16):
                    vop("tensor_tensor", W_(i), W_(i), G_(i), ALU.mult)
                for i in range(16):
                    vop("tensor_reduce", M2(i), W_(i), AX.X, ALU.add)
                for i in range(16):
                    vop("reciprocal", M1(i), M2(i))
                for i in range(16):
                    vop("tensor_scalar", W_(i), W_(i), M1(i), sc2, ALU.mult, ALU.mult,
                        inc=vm if i == 15 else None)
            gw_done = vm.n
            if STAGE == "g3":
                vop("tensor_copy", X[:, 0, 0:128], GWRK[:].rearrange("p a b -> p (a b)"), inc=vm)
                return sm.n
            C.wait("T", vm, gw_done)
            wt_evac = [0] * 16
            for i in range(16):
                bk = i % 4
                if i >= 4:
                    C.wait("T", sm, wt_evac[i - 4])
                t4 = trans(pb[bk][0:8, 0:128],
                           gating_16tiles.GWRK[:, i, :], IDN128[:], inc=tm)
                C.wait("S", tm, t4)
                wt_evac[i] = act("S", WT_dst[0:8, 128 * i:128 * (i + 1)],
                                 pb[bk][0:8, 0:128], AF.Identity, inc=sm)
            return sm.n

        def l0ffn(XN, xn_v, fstack):
            GLOG = sbuf("GLOG", [8, T], F32, stack=fstack)
            WT = sbuf("WT", [8, T], BF16, stack=fstack)
            GTOK = sbuf("GTOK", [128, 16, 8], F32, stack=fstack)
            GWRK = sbuf("GWRK", [128, 16, 8], F32, stack=fstack)
            GS1 = sbuf("GS1", [128, 16], F32, stack=fstack)
            GS2 = sbuf("GS2", [128, 16], F32, stack=fstack)
            GS3 = sbuf("GS3", [128, 16], F32, stack=fstack)
            WROW = sbuf("WROW", [1, T], BF16, stack=fstack)
            WPOOL = sbuf("WPOOL", [128, 3, 8, 1024], BF16, stack=fstack)
            SG = sbuf("SG", [128, TB], BF16, stack=fstack)
            UU = sbuf("UU", [128, TB], BF16, stack=fstack)
            MID = sbuf("MID", [128, 8, TB], BF16, stack=fstack)
            gating_16tiles.GLOG = GLOG
            gating_16tiles.GTOK = GTOK
            gating_16tiles.GWRK = GWRK
            gating_16tiles.GS1 = GS1
            gating_16tiles.GS2 = GS2
            gating_16tiles.GS3 = GS3

            C.wait("T", wd, consts_wd)
            C.wait("T", vm, xn_v)

            def logits(tb):
                ts = slice(tb * TB, (tb + 1) * TB)
                for c in range(NC_D):
                    r = mm(pb[4][0:8, :], GW8[:, c, :], XN[:, c, ts],
                           start=(c == 0), stop=(c == NC_D - 1),
                           inc=tm if c == NC_D - 1 else None)
                return r

            wt_done = gating_16tiles(logits, WT, "l0")
            C.wait("T", sm, wt_done)
            if STAGE in ("g1", "g2", "g3"):
                return vm.n
            if STAGE == "gating":
                C.wait("V", sm, wt_done)
                vop("tensor_copy", X[0:8, 0, :], WT[:], inc=vm)
                return vm.n

            ew = [dr["L0f_ew_gate"], dr["L0f_ew_up"], dr["L0f_ew_down"]]
            sw = [dr["L0f_sw_gate"], dr["L0f_sw_up"], dr["L0f_sw_down"]]

            def wsrc(kind, e):
                a = sw[kind] if e == 8 else ew[kind][e]
                return a.rearrange("(c p) m -> p c m", p=128)

            order = [8, 0, 1, 2, 3, 4, 5, 6, 7]
            wtgt = {}

            def stream(e):
                for kind in range(3):
                    wtgt[(e, kind)] = dma("G", WPOOL[:, kind, :, :],
                                          wsrc(kind, e), wd)

            last_use = 0
            bank_free = {i: 0 for i in range(7)}
            last_mid = 0
            stream(order[0])
            wrow_ld = 0
            for ei, e in enumerate(order):
                if ei > 0:
                    C.wait("G", tm, last_use)
                    stream(e)
                for kind in range(3):
                    C.wait("T", wd, wtgt[(e, kind)])
                if e != 8:
                    C.wait("Y", sm, wt_done)
                    if wrow_ld:
                        C.wait("Y", tm, last_wbc)
                    wrow_ld = dma("Y", WROW[:], WT[e:e + 1, :], ld)
                for tb in range(NTB):
                    ts = slice(tb * TB, (tb + 1) * TB)
                    if e != 8:
                        C.wait("T", ld, wrow_ld)
                        C.wait("T", vm, bank_free[4])
                        wbc = mm(pb[4][:, :], ONESR_B[:], WROW[0:1, ts],
                                 start=True, stop=True, inc=tm)
                        last_wbc = wbc
                        bank_free[4] = 0
                    for fc in range(NC_D):
                        C.wait("T", sm, bank_free[0])
                        for c in range(NC_D):
                            mm(pb[0][:, :], WPOOL[:, 0, c, 128 * fc:128 * fc + 128],
                               XN[:, c, ts], start=(c == 0), stop=(c == NC_D - 1),
                               inc=tm if c == NC_D - 1 else None)
                        g_done = tm.n
                        C.wait("T", vm, bank_free[1])
                        for c in range(NC_D):
                            mm(pb[1][:, :], WPOOL[:, 1, c, 128 * fc:128 * fc + 128],
                               XN[:, c, ts], start=(c == 0), stop=(c == NC_D - 1),
                               inc=tm if c == NC_D - 1 else None)
                        u_done = tm.n
                        C.wait("S", tm, g_done)
                        C.wait("S", vm, last_mid)
                        sg_t = act("S", SG[:], pb[0][:, :], AF.Silu, inc=sm)
                        bank_free[0] = sg_t
                        C.wait("V", tm, u_done)
                        vop("tensor_copy", UU[:], pb[1][:, :])
                        C.wait("V", sm, sg_t)
                        if e == 8:
                            mv = vop("scalar_tensor_tensor", MID[:, fc, :], SG[:], 0.5, UU[:],
                                     ALU.mult, ALU.mult, inc=vm)
                        else:
                            C.wait("V", tm, wbc)
                            vop("tensor_tensor", MID[:, fc, :], SG[:], UU[:], ALU.mult)
                            mv = vop("tensor_tensor", MID[:, fc, :], MID[:, fc, :],
                                     pb[4][:, :], ALU.mult, inc=vm)
                        bank_free[1] = mv
                        last_mid = mv
                        if e != 8:
                            bank_free[4] = mv
                    mid_done = vm.n
                    C.wait("T", vm, mid_done)
                    for dc in range(NC_D):
                        db = 2 + (dc % 2)
                        C.wait("T", vm, bank_free[db])
                        for fc in range(NC_D):
                            mm(pb[db][:, :], WPOOL[:, 2, fc, 128 * dc:128 * dc + 128],
                               MID[:, fc, :], start=(fc == 0), stop=(fc == NC_D - 1),
                               inc=tm if fc == NC_D - 1 else None)
                        d_done = tm.n
                        C.wait("V", tm, d_done)
                        av = vop("tensor_tensor", X[:, dc, ts], X[:, dc, ts], pb[db][:, :],
                                 ALU.add, inc=vm)
                        bank_free[db] = av
                last_use = tm.n
            return vm.n

        def l0attn(astack):
            CQ = sbuf("CQ", [128, 4, T], BF16, stack=astack)
            with ExitStack() as sA:
                XN = sbuf("XN_a", [128, NC_D, T], BF16, stack=sA)
                WQ = sbuf("WQ", [128, 8, 512], BF16, stack=sA)
                WKV = sbuf("WKV", [128, 8, 64], BF16, stack=sA)
                CKVL = sbuf("CKVL", [64, T], BF16, stack=sA)
                wq_t = dma("G", WQ[:], dr["L0a_wq_dT"].rearrange("(c p) m -> p c m", p=128), wd)
                wkv_t = dma("G", WKV[:], dr["L0a_wkv_dT"].rearrange("(c p) m -> p c m", p=128), wd)
                xn_v = emit_ln(X, True, XN)
                C.wait("T", vm, xn_v)
                C.wait("T", wd, max(wq_t, wkv_t))
                bfree = {i: 0 for i in range(4)}
                for mc in range(4):
                    for tb in range(NTB):
                        ts = slice(tb * TB, (tb + 1) * TB)
                        bk = tb % 2
                        C.wait("T", vm, bfree[bk])
                        for c in range(NC_D):
                            r = mm(pb[bk][:, :], WQ[:, c, 128 * mc:128 * mc + 128],
                                   XN[:, c, ts], start=(c == 0), stop=(c == NC_D - 1),
                                   inc=tm if c == NC_D - 1 else None)
                        C.wait("V", tm, r)
                        bfree[bk] = vop("tensor_scalar", CQ[:, mc, ts], pb[bk][:, :],
                                        bias_ap("bq_d", mc), None, ALU.add, inc=vm)
                for tb in range(NTB):
                    ts = slice(tb * TB, (tb + 1) * TB)
                    bk = 2 + tb % 2
                    C.wait("T", vm, bfree[bk])
                    for c in range(NC_D):
                        r = mm(pb[bk][0:64, :], WKV[:, c, :], XN[:, c, ts],
                               start=(c == 0), stop=(c == NC_D - 1),
                               inc=tm if c == NC_D - 1 else None)
                    C.wait("V", tm, r)
                    bkv_ap = BIASV[0:64, bias_slots["bkv_d"]:bias_slots["bkv_d"] + 1]
                    bfree[bk] = vop("tensor_scalar", CKVL[:, ts], pb[bk][0:64, :],
                                    bkv_ap, None, ALU.add, inc=vm)
                ckv_done = vm.n
                C.wait("G", vm, ckv_done)
                bounce_t = dma("G", ckv_in[:, :], CKVL[:], wd)
            C.wait("G", wd, bounce_t)
            cc_t = ccs.inc()
            C.emit("G", lambda e, s=ccs.h: e.collective_compute(
                "AllGather", ALU.bypass, replica_groups=[list(range(NCORES))],
                ins=[ckv_in], outs=[ckv_out]).then_inc(s, 1))

            WQK = sbuf("WQK", [128, 4, 1024], BF16, stack=astack)
            QEXT = sbuf("QEXT", [65, 16, 1024], BF16, stack=astack)
            CKV = sbuf("CKV", [65, 8192], BF16, stack=astack)
            KM = sbuf("KM", [64, 65], BF16, stack=astack)
            KSUM = sbuf("KSUM", [64, 2], F32, stack=astack)
            WVU = sbuf("WVU", [65, 1024], BF16, stack=astack)
            YALL = sbuf("YALL", [64, 16, 512], BF16, stack=astack)
            YTMP = sbuf("YTMP", [64, 512], BF16, stack=astack)
            ZR = sbuf("ZR", [1, 512], F32, stack=astack)
            WPJ = sbuf("WPJ", [64, 16, 512], BF16, stack=astack)
            wqk_t = dma("G", WQK[:], dr["L0a_w_qkT"].rearrange("(c p) m -> p c m", p=128), wd)
            wvu_t = dma("G", WVU[0:64, :], dr["L0a_wv_uT"][:], wd)
            wvu_t = dma("G", WVU[64:65, :], dr["L0a_bv_u"][:], wd)
            C.emit("V", lambda e: e.memset(CKV[64:65, :], 1.0))
            C.emit("V", lambda e: e.memset(KM[:, :], 0.0))
            last_yv = [0]
            sguard = {i: 0 for i in range(7)}
            p3guard = {i: 0 for i in range(3)}

            for b in range(2):
                C.wait("V", tm, tm.n)
                C.wait("S", tm, tm.n)
                tbs = [2 * b, 2 * b + 1]
                C.wait("G", ccs, cc_t)
                C.wait("G", tm, tm.n)
                ckv_t = dma("G", CKV[0:64, :].rearrange("d (r t) -> d r t", r=8),
                            ckv_out.rearrange("(r d) t -> d r t", d=64)[:, :, 1024 * b:1024 * (b + 1)],
                            wd)
                C.wait("T", wd, wqk_t)
                for h in range(16):
                    for t2 in range(2):
                        ts = slice(tbs[t2] * TB, (tbs[t2] + 1) * TB)
                        qs = slice(t2 * TB, (t2 + 1) * TB)
                        bk = t2 % 2
                        C.wait("T", vm, bfree[bk])
                        for kc in range(4):
                            r = mm(pb[bk][0:64, :], WQK[:, kc, 64 * h:64 * h + 64],
                                   CQ[:, kc, ts], start=(kc == 0), stop=(kc == 3),
                                   inc=tm if kc == 3 else None)
                        C.wait("V", tm, r)
                        bfree[bk] = vop("tensor_scalar", QEXT[0:64, h, qs], pb[bk][0:64, :],
                                        0.125, BIASV[0:64, bias_slots["bqk8"] + h:
                                                      bias_slots["bqk8"] + h + 1],
                                        ALU.mult, ALU.add, inc=vm)
                C.wait("V", wd, ckv_t)
                vop("tensor_reduce", KSUM[:, 0:1], CKV[0:64, :], AX.X, ALU.add)
                vop("tensor_scalar", KSUM[:, 1:2], KSUM[:, 0:1], -1.0 / 8192.0, None, ALU.mult)
                km_v = vop("tensor_copy", KM[:, 64:65], KSUM[:, 1:2], inc=vm)
                C.wait("T", vm, vm.n)
                mrow_ev = 0
                for h in range(16):
                    for t2 in range(2):
                        qs = slice(t2 * TB, (t2 + 1) * TB)
                        if mrow_ev:
                            C.wait("T", sm, mrow_ev)
                        r = mm(pb[6][0:65, :], KM[:], QEXT[0:64, h, qs],
                               start=True, stop=True, inc=tm)
                        C.wait("S", tm, r)
                        mrow_ev = act("S", QEXT[64:65, h, qs], pb[6][64:65, :],
                                      AF.Identity, inc=sm)
                C.wait("T", sm, mrow_ev)
                for t2 in range(2):
                    qs = slice(t2 * TB, (t2 + 1) * TB)
                    ts = slice(tbs[t2] * TB, (tbs[t2] + 1) * TB)
                    with ExitStack() as sC:
                        VEXT = sbuf("VEXT", [128, 64, 65], BF16, stack=sC)
                        WVX = sbuf("WVX", [65, 65], BF16, stack=sC)
                        P3 = sbuf("P3", [128, 3, TB], BF16, stack=sC)
                        C.wait("V", wd, wvu_t)
                        for h in range(16):
                            C.wait("V", tm, tm.n)
                            vop("memset", WVX[:, :], 0.0)
                            vop("tensor_copy", WVX[:, 0:64], WVU[:, 64 * h:64 * h + 64])
                            wvx_v = vop("memset", WVX[64:65, 64:65], 1.0, inc=vm)
                            C.wait("T", vm, wvx_v)
                            vb_last = 0
                            for sc in range(64):
                                bk = 4 + sc % 2
                                C.wait("T", sm, sguard[bk])
                                r = mm(pb[bk][:, 0:65], CKV[:, 128 * sc:128 * sc + 128],
                                       WVX[:], start=True, stop=True, inc=tm)
                                C.wait("S", tm, r)
                                vb_last = act("S", VEXT[:, sc, :], pb[bk][:, 0:65],
                                              AF.Identity, inc=sm)
                                sguard[bk] = vb_last
                            C.wait("T", sm, vb_last)
                            exp_of = [0] * 64
                            for sc in range(64):
                                bk = sc % 2
                                C.wait("T", sm, sguard[bk])
                                mm(pb[bk][:, :], CKV[:, 128 * sc:128 * sc + 128],
                                   QEXT[0:65, h, qs], start=True, stop=True, inc=tm)
                                sc_t = tm.n
                                if sc >= 1:
                                    C.wait("T", sm, exp_of[sc - 1])
                                    mm(pb[2][0:65, :], VEXT[:, sc - 1, :], P3[:, (sc - 1) % 3, :],
                                       start=(sc - 1 == 0), stop=False, inc=tm)
                                    p3guard[(sc - 1) % 3] = tm.n
                                C.wait("S", tm, sc_t)
                                C.wait("S", tm, p3guard[sc % 3])
                                ex = act("S", P3[:, sc % 3, :], pb[bk][:, :],
                                         AF.Exp, inc=sm)
                                exp_of[sc] = ex
                                sguard[bk] = ex
                            C.wait("T", sm, exp_of[63])
                            mm(pb[2][0:65, :], VEXT[:, 63, :], P3[:, 63 % 3, :],
                               start=False, stop=True, inc=tm)
                            p3guard[63 % 3] = tm.n
                            pv_last = tm.n
                            C.wait("V", tm, pv_last)
                            zr_v = vop("reciprocal", ZR[:], pb[2][64:65, :], inc=vm)
                            C.wait("T", vm, zr_v)
                            bc_t = mm(pb[3][0:64, :], ONESR_F[0:1, 0:64], ZR[:],
                                      start=True, stop=True, inc=tm)
                            C.wait("S", tm, pv_last)
                            C.wait("S", vm, last_yv[0])
                            yt_ev = act("S", YTMP[:], pb[2][0:64, :], AF.Identity, inc=sm)
                            C.wait("V", sm, yt_ev)
                            C.wait("V", tm, bc_t)
                            y_v = vop("tensor_tensor", YALL[0:64, h, :],
                                      YTMP[:], pb[3][0:64, :], ALU.mult, inc=vm)
                            last_yv[0] = y_v
                            C.wait("T", sm, yt_ev)
                            C.wait("T", vm, y_v)
                    C.wait("T", vm, vm.n)
                    pfree = {0: 0, 1: 0}
                    for half in range(2):
                        C.wait("G", tm, tm.n)
                        wpj_t = dma("G", WPJ[:],
                                    dr["L0a_w_projT"].rearrange("(h p) m -> p h m", p=64)
                                    [:, :, 512 * half:512 * (half + 1)], wd)
                        C.wait("T", wd, wpj_t)
                        for mc2 in range(4):
                            mc = 4 * half + mc2
                            bk = mc % 2
                            C.wait("T", vm, pfree[bk])
                            for h in range(16):
                                r = mm(pb[bk][:, :], WPJ[:, h, 128 * mc2:128 * mc2 + 128],
                                       YALL[0:64, h, :], start=(h == 0), stop=(h == 15),
                                       inc=tm if h == 15 else None)
                            C.wait("V", tm, r)
                            pfree[bk] = vop("scalar_tensor_tensor", X[:, mc, ts],
                                            pb[bk][:, :], bias_ap("b_proj", mc),
                                            X[:, mc, ts], ALU.add, ALU.add, inc=vm)
            return vm.n

        def l1ffn(XN, xn_v, fstack):
            WT = sbuf("WT1", [8, T], BF16, stack=fstack)
            WROW = sbuf("WROW1", [1, T], BF16, stack=fstack)
            C.wait("S", vm, xn_v)
            if STAGE == "l1nc":
                C.wait("V", vm, xn_v)
                for c in range(NC_D):
                    vop("tensor_copy", X[:, c, :], XN[:, c, :], inc=vm)
                return vm.n
            if STAGE == "l1n0":
                XN2 = sbuf("XN2dbg", [128, NC_D, T], BF16, stack=fstack)
                nrm_v = emit_ln(XN, False, XN2, affine=None)
                XN = XN2
            else:
                nrm_v = emit_ln(XN, False, XN, affine=("ln_g", "ln_b"))
            if STAGE in ("l1n", "l1n0"):
                C.wait("V", vm, nrm_v)
                for c in range(NC_D):
                    vop("tensor_copy", X[:, c, :], XN[:, c, :], inc=vm)
                return vm.n
            with ExitStack() as gs:
                GLOG = sbuf("GLOG1", [8, T], F32, stack=gs)
                GTOK = sbuf("GTOK1", [128, 16, 8], F32, stack=gs)
                GWRK = sbuf("GWRK1", [128, 16, 8], F32, stack=gs)
                GS1 = sbuf("GS11", [128, 16], F32, stack=gs)
                GS2 = sbuf("GS21", [128, 16], F32, stack=gs)
                GS3 = sbuf("GS31", [128, 16], F32, stack=gs)
                gating_16tiles.GLOG = GLOG
                gating_16tiles.GTOK = GTOK
                gating_16tiles.GWRK = GWRK
                gating_16tiles.GS1 = GS1
                gating_16tiles.GS2 = GS2
                gating_16tiles.GS3 = GS3
                C.wait("T", wd, consts_wd)
                C.wait("T", vm, nrm_v)

                def logits(tb):
                    ts = slice(tb * TB, (tb + 1) * TB)
                    for c in range(NC_D):
                        r = mm(pb[4][0:8, :], RW8[:, c, :], XN[:, c, ts],
                               start=(c == 0), stop=(c == NC_D - 1),
                               inc=tm if c == NC_D - 1 else None)
                    return r

                wt_done = gating_16tiles(logits, WT, "l1")
                C.wait("T", sm, wt_done)
                if STAGE == "l1g":
                    C.wait("V", sm, wt_done)
                    vop("tensor_copy", X[0:8, 0, :], WT[:], inc=vm)
                    return vm.n
            WGU = sbuf("WGU", [128, 2, 8, 1024], BF16, stack=fstack)
            WDS = sbuf("WDS", [128, 8, 1024], BF16, stack=fstack)
            SG = sbuf("SG1", [128, TB], BF16, stack=fstack)
            UU = sbuf("UU1", [128, TB], BF16, stack=fstack)
            MID = sbuf("MID1", [128, 8, TB], BF16, stack=fstack)

            guT = dr["L1f_gate_upT"]
            dnT = dr["L1f_downT"]

            def stream_gu(e, sg):
                if e == 8:
                    gsrc = dr["L1f_sh_gate_upT"].rearrange("(c p) m -> p c m", p=128)
                    t1 = dma("G", WGU[:, 0, :, :], gsrc[:, :, 0:1024], wd)
                    t2 = dma("G", WGU[:, 1, :, :], gsrc[:, :, 1024:2048], wd)
                else:
                    gsrc = guT[e].rearrange("(c p) m -> p c m", p=128)
                    t1 = dma("G", WGU[:, 0, :, :], gsrc[:, :, 1024 * sg:1024 * (sg + 1)], wd)
                    t2 = dma("G", WGU[:, 1, :, :], gsrc[:, :, 3072 + 1024 * sg:3072 + 1024 * (sg + 1)], wd)
                return max(t1, t2)

            def stream_ds(e, sg):
                if e == 8:
                    return dma("G", WDS[:], dr["L1f_sh_downT"].rearrange("(c p) m -> p c m", p=128), wd)
                return dma("G", WDS[:], dnT[e].rearrange("(c p) m -> p c m", p=128)
                           [:, 8 * sg:8 * (sg + 1), :], wd)

            steps = [(e, sg) for e in range(8) for sg in range(3)] + [(8, 0)]
            if L1STEPS is not None:
                steps = steps[:L1STEPS] if L1STEPS > 0 else [steps[-1]]
            bank_free = {i: 0 for i in range(7)}
            last_mid = 0
            C.wait("G", sm, sm.n)
            C.wait("G", vm, vm.n)
            C.wait("G", tm, tm.n)
            wgu_tgt = stream_gu(*steps[0])
            wds_tgt = stream_ds(*steps[0])
            wrow_ld = 0
            last_wbc = 0
            cur_e = None
            for si, (e, sg) in enumerate(steps):
                wgu_cur, wds_cur = wgu_tgt, wds_tgt
                if e != 8 and e != cur_e:
                    cur_e = e
                    C.wait("Y", sm, wt_done)
                    if wrow_ld:
                        C.wait("Y", tm, last_wbc)
                    wrow_ld = dma("Y", WROW[:], WT[e:e + 1, :], ld)
                C.wait("T", wd, max(wgu_cur, wds_cur))
                for tb in range(NTB):
                    ts = slice(tb * TB, (tb + 1) * TB)
                    if e != 8:
                        C.wait("T", ld, wrow_ld)
                        C.wait("T", vm, bank_free[4])
                        wbc = mm(pb[4][:, :], ONESR_B[:], WROW[0:1, ts],
                                 start=True, stop=True, inc=tm)
                        last_wbc = wbc
                    for fc in range(NC_D):
                        C.wait("T", sm, bank_free[0])
                        for c in range(NC_D):
                            mm(pb[0][:, :], WGU[:, 0, c, 128 * fc:128 * fc + 128],
                               XN[:, c, ts], start=(c == 0), stop=(c == NC_D - 1),
                               inc=tm if c == NC_D - 1 else None)
                        g_done = tm.n
                        C.wait("T", vm, bank_free[1])
                        for c in range(NC_D):
                            mm(pb[1][:, :], WGU[:, 1, c, 128 * fc:128 * fc + 128],
                               XN[:, c, ts], start=(c == 0), stop=(c == NC_D - 1),
                               inc=tm if c == NC_D - 1 else None)
                        u_done = tm.n
                        C.wait("S", tm, g_done)
                        C.wait("S", vm, last_mid)
                        sg_t = act("S", SG[:], pb[0][:, :], AF.Silu, inc=sm)
                        bank_free[0] = sg_t
                        C.wait("V", tm, u_done)
                        vop("tensor_copy", UU[:], pb[1][:, :])
                        C.wait("V", sm, sg_t)
                        if e == 8:
                            mv = vop("scalar_tensor_tensor", MID[:, fc, :], SG[:], 0.5, UU[:],
                                     ALU.mult, ALU.mult, inc=vm)
                        else:
                            C.wait("V", tm, wbc)
                            vop("tensor_tensor", MID[:, fc, :], SG[:], UU[:], ALU.mult)
                            mv = vop("tensor_tensor", MID[:, fc, :], MID[:, fc, :],
                                     pb[4][:, :], ALU.mult, inc=vm)
                        bank_free[1] = mv
                        last_mid = mv
                        if e != 8:
                            bank_free[4] = mv
                    mid_done = vm.n
                    if tb == NTB - 1 and si + 1 < len(steps):
                        C.wait("G", tm, tm.n)
                        wgu_tgt = stream_gu(*steps[si + 1])
                    C.wait("T", vm, mid_done)
                    for dc in range(NC_D):
                        db = 2 + (dc % 2)
                        C.wait("T", vm, bank_free[db])
                        for fc in range(NC_D):
                            mm(pb[db][:, :], WDS[:, fc, 128 * dc:128 * dc + 128],
                               MID[:, fc, :], start=(fc == 0), stop=(fc == NC_D - 1),
                               inc=tm if fc == NC_D - 1 else None)
                        d_done = tm.n
                        C.wait("V", tm, d_done)
                        av = vop("tensor_tensor", X[:, dc, ts], X[:, dc, ts], pb[db][:, :],
                                 ALU.add, inc=vm)
                        bank_free[db] = av
                if si + 1 < len(steps):
                    C.wait("G", tm, tm.n)
                    wds_tgt = stream_ds(*steps[si + 1])
            return vm.n


        def build_phm_w(Wsb, a_name, s_name, bscratch):
            A_sb, S_sb, Wtmp = bscratch
            C.wait("G", tm, tm.n)
            a_t = dma("G", A_sb[:], dr[a_name][:], wd)
            C.wait("T", wd, a_t)
            last_rr = build_phm_w.last_rr
            for kk in range(8):
                for half in range(2):
                    base = 8192 * kk + 4096 * half
                    C.wait("G", tm, tm.n)
                    s_t = dma("G", S_sb[:], dr[s_name][:, base:base + 4096], wd)
                    C.wait("T", wd, s_t)
                    C.wait("S", ld, last_rr)
                    ev = 0
                    for j2 in range(8):
                        bk = j2 % 2
                        if j2 >= 2:
                            C.wait("T", sm, ev - 1)
                        r = mm(pb[bk][0:16, :], A_sb[:], S_sb[:, 512 * j2:512 * (j2 + 1)],
                               start=True, stop=True, inc=tm)
                        C.wait("S", tm, r)
                        ev = act("S", Wtmp[:, 512 * j2:512 * (j2 + 1)], pb[bk][0:16, :],
                                 AF.Identity, inc=sm)
                    C.wait("Y", sm, ev)
                    kglob = 32 * kk + 16 * half
                    p0 = kglob % 128
                    kh = kglob // 128
                    par = (kk * 2 + half) % 2
                    wr = dma("Y", wscr[par], Wtmp[:], ld)
                    C.wait("Y", ld, wr)
                    for i in range(4):
                        dma("Y",
                            Wsb[p0:p0 + 16, 2 * i + kh, :].rearrange("k (j l) -> k j l", j=4),
                            wscr[par][4 * i:4 * i + 4, :].rearrange("j (k l) -> k j l", k=16),
                            ld)
                    last_rr = ld.n
            build_phm_w.last_rr = last_rr
            return ld.n

        build_phm_w.last_rr = 0

        def l1attn(astack):
            XN = sbuf("XN_a1", [128, NC_D, T], BF16, stack=astack)
            QN = sbuf("QN", [128, NC_D, T], BF16, stack=astack)
            xn_v = emit_ln(X, True, XN)
            qn_v = emit_ln(XN, False, QN, affine=("g1", "b1"), eps=1)
            with ExitStack() as w3:
                WQ_ = sbuf("WQp", [128, 8, 1024], BF16, stack=w3)
                WK_ = sbuf("WKp", [128, 8, 1024], BF16, stack=w3)
                WV_ = sbuf("WVp", [128, 8, 1024], BF16, stack=w3)
                with ExitStack() as bs:
                    A_sb = sbuf("A_sb", [4, 16], BF16, stack=bs)
                    S_sb = sbuf("S_sb", [4, 4096], BF16, stack=bs)
                    Wtmp = sbuf("Wtmp", [16, 4096], BF16, stack=bs)
                    bscratch = (A_sb, S_sb, Wtmp)
                    build_phm_w(WQ_, "L1a_qA", "L1a_qS", bscratch)
                    build_phm_w(WK_, "L1a_kA", "L1a_kS", bscratch)
                    wv_done = build_phm_w(WV_, "L1a_vA", "L1a_vS", bscratch)
                    barrier()
                with ExitStack() as ps:
                    INDD = sbuf("INDD", [128, 8, 16], BF16, stack=ps)
                    INDB = sbuf("INDB", [16, 8, 128], BF16, stack=ps)
                    C.wait("G", tm, tm.n)
                    dma("G", INDD[:], dr["L1a_ind_dots"].rearrange("c p e -> p c e"), wd)
                    i_t = dma("G", INDB[:], dr["L1a_ind_bc"].rearrange("c j p -> j c p"), wd)
                    Q0 = sbuf("Q0", [128, TB], BF16, stack=ps)
                    Q1 = sbuf("Q1", [128, TB], BF16, stack=ps)
                    K0 = sbuf("K0", [128, TB], BF16, stack=ps)
                    K1 = sbuf("K1", [128, TB], BF16, stack=ps)
                    PR = sbuf("PR", [128, 2, TB], BF16, stack=ps)
                    V0, V1 = Q0, Q1
                    T1_ = K0
                    T2_ = K1
                    DOTS = sbuf("DOTS", [16, 4, TB], BF16, stack=ps)
                    COEF = sbuf("COEF", [16, 4, TB], BF16, stack=ps)
                    DEN = XB
                    C.wait("T", vm, qn_v)
                    C.wait("T", wd, i_t)
                    ev_guard = {0: 0, 1: 0, 2: 0}
                    vev = {0: 0, 1: 0}
                    prv = [0, 0]
                    att_v = 0
                    for tbp in range(2):
                        cs0 = slice(512 * tbp, 512 * (tbp + 1))
                        cs1 = slice(1024 + 512 * tbp, 1536 + 512 * tbp)
                        for c in range(NC_D):
                            specs = [(Q0, WQ_, QN, cs0, "qb", 0), (Q1, WQ_, QN, cs1, "qb", 1),
                                     (K0, WK_, XN, cs0, "kb", 2), (K1, WK_, XN, cs1, "kb", 0)]
                            if c > 0 or tbp > 0:
                                C.wait("S", vm, prod_v)
                            for di, (dst, Wp, src, cs, bnm, bk) in enumerate(specs):
                                C.wait("T", sm, ev_guard[bk])
                                for kc in range(NC_D):
                                    r = mm(pb[bk][:, :], Wp[:, kc, 128 * c:128 * c + 128],
                                           src[:, kc, cs], start=(kc == 0), stop=(kc == NC_D - 1),
                                           inc=tm if kc == NC_D - 1 else None)
                                C.wait("S", tm, r)
                                fn = AF.Relu
                                ev_guard[bk] = act("S", dst[:], pb[bk][:, :], fn,
                                                   bias=bias_ap(bnm, c), inc=sm)
                            C.wait("V", sm, ev_guard[0])
                            C.wait("V", sm, ev_guard[1])
                            C.wait("V", sm, ev_guard[2])
                            C.wait("V", tm, prv[0])
                            vop("tensor_tensor", PR[:, 0, :], Q0[:], K0[:], ALU.mult)
                            p1 = vop("tensor_tensor", PR[:, 1, :], Q0[:], K1[:], ALU.mult, inc=vm)
                            C.wait("T", vm, p1)
                            mm(pb[3][0:16, :], INDD[:, c, :], PR[:, 0, :],
                               start=(c == 0), stop=(c == NC_D - 1))
                            r1 = mm(pb[4][0:16, :], INDD[:, c, :], PR[:, 1, :],
                                    start=(c == 0), stop=(c == NC_D - 1), inc=tm)
                            prv[0] = r1
                            C.wait("V", tm, prv[1])
                            vop("tensor_tensor", PR[:, 0, :], Q1[:], K0[:], ALU.mult)
                            p2 = vop("tensor_tensor", PR[:, 1, :], Q1[:], K1[:], ALU.mult, inc=vm)
                            C.wait("T", vm, p2)
                            mm(pb[5][0:16, :], INDD[:, c, :], PR[:, 0, :],
                               start=(c == 0), stop=(c == NC_D - 1))
                            r2 = mm(pb[6][0:16, :], INDD[:, c, :], PR[:, 1, :],
                                    start=(c == 0), stop=(c == NC_D - 1), inc=tm)
                            prv[1] = r2
                            prod_v = p2
                        dots_done = tm.n
                        C.wait("V", tm, dots_done)
                        for k in range(4):
                            vop("tensor_copy", DOTS[:, k, :], pb[2 + 1 + k][0:16, :])
                        vop("scalar_tensor_tensor", DEN[0:16, 0, :], DOTS[:, 1, :], 0.70710678,
                            DOTS[:, 0, :], ALU.mult, ALU.add)
                        vop("scalar_tensor_tensor", DEN[0:16, 1, :], DOTS[:, 2, :], 0.70710678,
                            DOTS[:, 3, :], ALU.mult, ALU.add)
                        vop("tensor_scalar", DEN[0:16, 0, :], DEN[0:16, 0, :], 1e-6, None, ALU.max)
                        vop("tensor_scalar", DEN[0:16, 1, :], DEN[0:16, 1, :], 1e-6, None, ALU.max)
                        def _rcp(e, o, i):
                            with nc.allow_low_precision(reason="bf16 1/z within tolerance"):
                                e.reciprocal(o, i)
                        C.emit("V", lambda e, o=T1_[0:16, 0:TB], i=DEN[0:16, 0, :]: _rcp(e, o, i))
                        C.emit("V", lambda e, o=T2_[0:16, 0:TB], i=DEN[0:16, 1, :]: _rcp(e, o, i))
                        vop("tensor_tensor", COEF[:, 0, :], DOTS[:, 0, :], T1_[0:16, 0:TB], ALU.mult)
                        vop("scalar_tensor_tensor", COEF[:, 1, :], DOTS[:, 1, :], 0.70710678,
                            T1_[0:16, 0:TB], ALU.mult, ALU.mult)
                        vop("scalar_tensor_tensor", COEF[:, 2, :], DOTS[:, 2, :], 0.70710678,
                            T2_[0:16, 0:TB], ALU.mult, ALU.mult)
                        cf = vop("tensor_tensor", COEF[:, 3, :], DOTS[:, 3, :], T2_[0:16, 0:TB],
                                 ALU.mult, inc=vm)
                        C.wait("T", vm, cf)
                        for c in range(NC_D):
                            if att_v:
                                C.wait("S", vm, att_v)
                            for di, (dst, cs) in enumerate([(V0, cs0), (V1, cs1)]):
                                C.wait("T", sm, vev[di])
                                for kc in range(NC_D):
                                    r = mm(pb[di][:, :], WV_[:, kc, 128 * c:128 * c + 128],
                                           XN[:, kc, cs], start=(kc == 0), stop=(kc == NC_D - 1),
                                           inc=tm if kc == NC_D - 1 else None)
                                C.wait("S", tm, r)
                                vev[di] = act("S", dst[:], pb[di][:, :], AF.Identity,
                                              bias=bias_ap("vb", c), inc=sm)
                            C.wait("T", vm, att_v)
                            mm(pb[2][:, :], INDB[:, c, :], COEF[:, 0, :], start=True, stop=True)
                            mm(pb[3][:, :], INDB[:, c, :], COEF[:, 1, :], start=True, stop=True)
                            mm(pb[4][:, :], INDB[:, c, :], COEF[:, 2, :], start=True, stop=True)
                            bc4 = mm(pb[5][:, :], INDB[:, c, :], COEF[:, 3, :],
                                     start=True, stop=True, inc=tm)
                            C.wait("V", tm, bc4)
                            C.wait("V", sm, vev[0])
                            C.wait("V", sm, vev[1])
                            vop("tensor_tensor", T1_[:], V0[:], pb[2][:, :], ALU.mult)
                            vop("tensor_tensor", T2_[:], V1[:], pb[3][:, :], ALU.mult)
                            vop("tensor_tensor", T1_[:], T1_[:], T2_[:], ALU.add)
                            vop("tensor_tensor", QN[:, c, cs0], T1_[:], QN[:, c, cs0], ALU.add)
                            vop("tensor_tensor", T1_[:], V0[:], pb[4][:, :], ALU.mult)
                            vop("tensor_tensor", T2_[:], V1[:], pb[5][:, :], ALU.mult)
                            vop("tensor_tensor", T1_[:], T1_[:], T2_[:], ALU.add)
                            att_v = vop("tensor_tensor", QN[:, c, cs1], T1_[:], QN[:, c, cs1],
                                        ALU.add, inc=vm)
                    barrier()
            ln2_v = emit_ln(QN, False, XN, affine=("g2", "b2"), eps=1)
            with ExitStack() as os_:
                WO_ = sbuf("WOp", [128, 8, 1024], BF16, stack=os_)
                A_sb = sbuf("A_sb2", [4, 16], BF16, stack=os_)
                S_sb = sbuf("S_sb2", [4, 4096], BF16, stack=os_)
                Wtmp = sbuf("Wtmp2", [16, 4096], BF16, stack=os_)
                wo_done = build_phm_w(WO_, "L1a_oA", "L1a_oS", (A_sb, S_sb, Wtmp))
                C.wait("T", ld, wo_done)
                C.wait("T", vm, vm.n)
                C.wait("T", sm, sm.n)
                pfree = {0: 0, 1: 0}
                for mc in range(NC_D):
                    for tb in range(NTB):
                        ts = slice(tb * TB, (tb + 1) * TB)
                        bk = tb % 2
                        C.wait("T", vm, pfree[bk])
                        for kc in range(NC_D):
                            r = mm(pb[bk][:, :], WO_[:, kc, 128 * mc:128 * mc + 128],
                                   XN[:, kc, ts], start=(kc == 0), stop=(kc == NC_D - 1),
                                   inc=tm if kc == NC_D - 1 else None)
                        C.wait("V", tm, r)
                        vop("scalar_tensor_tensor", XB[:, 0, :],
                            pb[bk][:, :], bias_ap("ob", mc), XN[:, mc, ts], ALU.add, ALU.add)
                        pfree[bk] = vop("tensor_tensor", X[:, mc, ts], X[:, mc, ts],
                                        XB[:, 0, :], ALU.add, inc=vm)
            return vm.n

        C.wait("S", ld, x_ld)
        C.wait("V", ld, consts_ld)
        C.wait("T", ld, consts_ld)

        if STAGE == "load":
            pass
        elif STAGE == "ln":
            with ExitStack() as fstack:
                XN = sbuf("XN_f1", [128, NC_D, T], BF16, stack=fstack)
                xn_v = emit_ln(X, True, XN)
                C.wait("V", vm, xn_v)
                for c in range(NC_D):
                    vop("tensor_copy", X[:, c, :], XN[:, c, :], inc=vm)
                barrier()
        else:
            if "l0ffn1" in phases:
                with ExitStack() as fstack:
                    XN = sbuf("XN_f1", [128, NC_D, T], BF16, stack=fstack)
                    xn_v = emit_ln(X, True, XN)
                    x_v = l0ffn(XN, xn_v, fstack)
                    barrier()
            if "l0attn" in phases and STAGE is None:
                with ExitStack() as astack:
                    l0attn(astack)
                    barrier()
            if "l0ffn2" in phases and STAGE is None:
                with ExitStack() as fstack:
                    XN = sbuf("XN_f2", [128, NC_D, T], BF16, stack=fstack)
                    xn_v = emit_ln(X, True, XN)
                    x_v = l0ffn(XN, xn_v, fstack)
                    barrier()
            if "l1ffn1" in phases and STAGE in (None, "l1g", "l1n", "l1n0", "l1nc"):
                with ExitStack() as fstack:
                    XN = sbuf("XN_g1", [128, NC_D, T], BF16, stack=fstack)
                    xn_v = emit_ln(X, True, XN)
                    x_v = l1ffn(XN, xn_v, fstack)
                    barrier()
            if "l1attn" in phases and STAGE is None:
                with ExitStack() as astack:
                    l1attn(astack)
                    barrier()
            if "l1ffn2" in phases and STAGE is None:
                with ExitStack() as fstack:
                    XN = sbuf("XN_g2", [128, NC_D, T], BF16, stack=fstack)
                    xn_v = emit_ln(X, True, XN)
                    x_v = l1ffn(XN, xn_v, fstack)
                    barrier()

        C.wait("Y", vm, vm.n)
        st_t = dma("Y", out_t.rearrange("(c p) t -> p c t", p=128),
                   X[:], st)
        C.wait("Y", st, st_t)

        with nc.Block() as block:
            @block.tensor
            def _(t):
                for f in C.streams["T"]:
                    f(t)

            @block.vector
            def _(v):
                for f in C.streams["V"]:
                    f(v)

            @block.scalar
            def _(s):
                for f in C.streams["S"]:
                    f(s)

            @block.gpsimd
            def _(g):
                for f in C.streams["G"]:
                    f(g)

            @block.sync
            def _(y):
                for f in C.streams["Y"]:
                    f(y)

    return nc


def make_inputs(core, inp):
    i = core
    x = np.asarray(inp["L0_attn_x"])
    xs = x[:, 1024 * i:1024 * (i + 1), :]
    xT = np.ascontiguousarray(xs.reshape(T, D).T)
    d = {"xT": xT,
         "ones_col": np.ones((128, 1), np.float32),
         "ones_row": np.ones((1, 128), np.float32),
         "idn8": np.eye(8, dtype=np.float32),
         "idn128": np.eye(128, dtype=np.float32)}
    g = lambda k: np.ascontiguousarray(np.asarray(inp[k], dtype=np.float32))
    bv = lambda k: np.ascontiguousarray(np.asarray(inp[k], dtype=np.float32).reshape(-1, 128).T)
    d["L0f_gate_w"] = g("L0_ffn_gate_w")
    d["L0f_sw_gate"] = g("L0_ffn_sw_gate")
    d["L0f_sw_up"] = g("L0_ffn_sw_up")
    d["L0f_sw_down"] = g("L0_ffn_sw_down")
    d["L0f_ew_gate"] = g("L0_ffn_ew_gate")
    d["L0f_ew_up"] = g("L0_ffn_ew_up")
    d["L0f_ew_down"] = g("L0_ffn_ew_down")
    d["L0a_wq_dT"] = np.ascontiguousarray(g("L0_attn_wq_d").T)
    d["L0a_bq_d"] = bv("L0_attn_bq_d")
    d["L0a_w_qkT"] = np.ascontiguousarray(g("L0_attn_w_qk").T)
    d["L0a_b_qk"] = np.ascontiguousarray(g("L0_attn_b_qk").reshape(16, 64).T)
    d["L0a_wkv_dT"] = np.ascontiguousarray(g("L0_attn_wkv_d").T)
    d["L0a_bkv_d"] = np.ascontiguousarray(g("L0_attn_bkv_d").reshape(64, 1))
    d["L0a_wv_uT"] = np.ascontiguousarray(g("L0_attn_wv_u").T)
    d["L0a_bv_u"] = np.ascontiguousarray(g("L0_attn_bv_u").reshape(1, D))
    d["L0a_w_projT"] = np.ascontiguousarray(g("L0_attn_w_proj").T)
    d["L0a_b_proj"] = bv("L0_attn_b_proj")
    d["L1f_ln_gamma"] = bv("L1_ffn_ln_gamma")
    d["L1f_ln_beta"] = bv("L1_ffn_ln_beta")
    d["L1f_router_wT"] = np.ascontiguousarray(g("L1_ffn_router_w").T)
    d["L1f_gate_upT"] = np.ascontiguousarray(g("L1_ffn_gate_up_w").transpose(0, 2, 1))
    d["L1f_downT"] = np.ascontiguousarray(g("L1_ffn_down_w").transpose(0, 2, 1))
    d["L1f_sh_gate_upT"] = np.ascontiguousarray(g("L1_ffn_shared_gate_up_w").T)
    d["L1f_sh_downT"] = np.ascontiguousarray(g("L1_ffn_shared_down_w").T)
    for p in "qkvo":
        d[f"L1a_{p}A"] = np.ascontiguousarray(g(f"L1_attn_{p}A").reshape(4, 16))
        d[f"L1a_{p}S"] = np.ascontiguousarray(g(f"L1_attn_{p}S").reshape(4, 65536))
        d[f"L1a_{p}b"] = bv(f"L1_attn_{p}b")
    for nm in ["g1", "b1", "g2", "b2"]:
        d[f"L1a_{nm}"] = bv(f"L1_attn_{nm}")
    ind_dots = np.zeros((8, 128, 16), np.float32)
    ind_bc = np.zeros((8, 16, 128), np.float32)
    for c in range(8):
        ind_dots[c, 0:64, 2 * c] = 1.0
        ind_dots[c, 64:128, 2 * c + 1] = 1.0
        ind_bc[c, 2 * c, 0:64] = 1.0
        ind_bc[c, 2 * c + 1, 64:128] = 1.0
    d["L1a_ind_dots"] = ind_dots
    d["L1a_ind_bc"] = ind_bc
    d["L1a_ind_bc7"] = ind_bc * np.float32(np.cos(np.pi / 4))
    return d


def kernel(**inputs):
    nc = build_program(PHASES)
    in_maps = [make_inputs(i, inputs) for i in range(NCORES)]
    res = run_bass_kernel_spmd(nc, in_maps, core_ids=list(range(NCORES)))
    outs = [res.results[i]["out"] for i in range(NCORES)]
    full = np.stack([o.T.reshape(2, 1024, 1024) for o in outs], axis=1)
    return np.ascontiguousarray(full.reshape(2, 8192, 1024))


# version 04
# speedup vs baseline: 1.1757x; 1.0176x over previous
import numpy as np
from contextlib import ExitStack

import concourse.bass as bass
import concourse.mybir as mybir
from concourse.bass_utils import run_bass_kernel_spmd

F32 = mybir.dt.float32
BF16 = mybir.dt.bfloat16
AF = mybir.ActivationFunctionType
ALU = mybir.AluOpType
AX = mybir.AxisListType

NCORES = 8
D = 1024
T = 2048
NC_D = 8
NTB = 4
TB = 512

PHASES = ("l0ffn1", "l0attn", "l0ffn2", "l1ffn1", "l1attn", "l1ffn2")
L1STEPS = None
STAGE = None


class Sem:
    def __init__(self, h):
        self.h = h
        self.n = 0

    def inc(self, amt=1):
        self.n += amt
        return self.n


class Ctx:
    def __init__(self, nc):
        self.nc = nc
        self.streams = {e: [] for e in "TVSGY"}

    def emit(self, eng, fn):
        self.streams[eng].append(fn)

    def wait(self, eng, sem, val):
        assert val <= sem.n, f"wait target {val} > emitted {sem.n} on {eng}"
        if val > 0:
            self.emit(eng, lambda e, s=sem.h, v=val: e.wait_ge(s, v))


def build_program(phases=PHASES):
    nc = bass.Bass("TRN2", target_bir_lowering=False, debug=False)

    dr = {}

    def din(name, shape, dtype=F32):
        dr[name] = nc.dram_tensor(name, list(shape), dtype, kind="ExternalInput").ap()
        return dr[name]

    din("xT", [D, T])
    din("ones_col", [128, 1])
    din("ones_row", [1, 128])
    din("idn8", [8, 8])
    din("idn128", [128, 128])
    din("L0f_gate_w", [D, 8])
    din("L0f_sw_gate", [D, D])
    din("L0f_sw_up", [D, D])
    din("L0f_sw_down", [D, D])
    din("L0f_ew_gate", [8, D, D])
    din("L0f_ew_up", [8, D, D])
    din("L0f_ew_down", [8, D, D])
    din("L0a_wq_dT", [D, 512])
    din("L0a_bq_d", [128, 4])
    din("L0a_w_qkT", [512, D])
    din("L0a_b_qk", [64, 16])
    din("L0a_wkv_dT", [D, 64])
    din("L0a_bkv_d", [64, 1])
    din("L0a_wv_uT", [64, D])
    din("L0a_bv_u", [1, D])
    din("L0a_w_projT", [D, D])
    din("L0a_b_proj", [128, 8])
    din("L1f_ln_gamma", [128, 8])
    din("L1f_ln_beta", [128, 8])
    din("L1f_router_wT", [D, 8])
    din("L1f_gate_upT", [8, D, 6144])
    din("L1f_downT", [8, 3072, D])
    din("L1f_sh_gate_upT", [D, 2048])
    din("L1f_sh_downT", [1024, D])
    for p in "qkvo":
        din(f"L1a_{p}A", [4, 16])
        din(f"L1a_{p}S", [4, 65536])
        din(f"L1a_{p}b", [128, 8])
    for nm in ["g1", "b1", "g2", "b2"]:
        din(f"L1a_{nm}", [128, 8])
    din("L1a_ind_dots", [8, 128, 16])
    din("L1a_ind_bc", [8, 16, 128])
    din("L1a_ind_bc7", [8, 16, 128])

    out_t = nc.dram_tensor("out", [D, T], F32, kind="ExternalOutput").ap()
    x_spill = nc.dram_tensor("x_spill", [D, T], F32).ap()
    ckv_in = nc.dram_tensor("ckv_in", [64, T], BF16).ap()
    wscr = nc.dram_tensor("wscr", [2, 16, 4096], BF16).ap()
    ckv_out = nc.dram_tensor("ckv_out", [NCORES * 64, T], BF16).ap()

    es = ExitStack()
    with es:
        _name_ctr = {}

        def sbuf(name, shape, dtype, stack=es):
            n = _name_ctr.get(name, 0)
            _name_ctr[name] = n + 1
            if n:
                name = f"{name}_{n}"
            return stack.enter_context(nc.sbuf_tensor(name, list(shape), dtype))

        X = sbuf("X", [128, NC_D, T], F32)
        XB = sbuf("XB", [128, 2, TB], BF16)
        SQ = sbuf("SQ", [128, 2, TB], BF16)
        ROWS = sbuf("ROWS", [1, 4, TB], F32)
        ONESC = sbuf("ONESC", [128, 1], BF16)
        ONESR_B = sbuf("ONESR_B", [1, 128], BF16)
        ONESR_F = sbuf("ONESR_F", [1, 128], F32)
        IDN8 = sbuf("IDN8", [8, 8], F32)
        IDN128 = sbuf("IDN128", [128, 128], F32)
        BIASV = sbuf("BIASV", [128, 136], F32)
        GW8 = sbuf("GW8", [128, 8, 8], BF16)
        RW8 = sbuf("RW8", [128, 8, 8], BF16)
        EPSC = sbuf("EPSC", [1, 2], F32)

        pb = [es.enter_context(nc.psum_tensor(f"pb{i}", [128, 512], F32)) for i in range(7)]

        sems = {}
        for name in ["ld", "wd", "tm", "vm", "sm", "gm", "cc", "st"]:
            sems[name] = Sem(es.enter_context(nc.semaphore(name)))
        ld, wd, tm, vm, sm = sems["ld"], sems["wd"], sems["tm"], sems["vm"], sems["sm"]
        gm, ccs, st = sems["gm"], sems["cc"], sems["st"]

        C = Ctx(nc)

        def dma(stream, out_ap, in_ap, sem):
            tgt = sem.inc(16)
            C.emit(stream, lambda e, o=out_ap, i=in_ap, s=sem.h:
                   e.dma_start(out=o, in_=i).then_inc(s, 16))
            return tgt

        def mm(out_ap, lhsT_ap, rhs_ap, start, stop, inc=None):
            if inc is not None:
                tgt = inc.inc()
                C.emit("T", lambda e, o=out_ap, l=lhsT_ap, r=rhs_ap, a=start, b=stop, s=inc.h:
                       e.matmul(o, l, r, start=a, stop=b).then_inc(s, 1))
                return tgt
            C.emit("T", lambda e, o=out_ap, l=lhsT_ap, r=rhs_ap, a=start, b=stop:
                   e.matmul(o, l, r, start=a, stop=b))

        def trans(out_ap, in_ap, ident, inc=None):
            if inc is not None:
                tgt = inc.inc()
                C.emit("T", lambda e, o=out_ap, i=in_ap, d=ident, s=inc.h:
                       e.transpose(o, i, d).then_inc(s, 1))
                return tgt
            C.emit("T", lambda e, o=out_ap, i=in_ap, d=ident: e.transpose(o, i, d))

        def act(eng, out_ap, in_ap, func, bias=0.0, scale=1.0, accum=None, inc=None):
            if inc is not None:
                tgt = inc.inc()
                C.emit(eng, lambda e, o=out_ap, i=in_ap, fu=func, b=bias, sc=scale, ac=accum, s=inc.h:
                       e.activation(o, i, fu, bias=b, scale=sc, accum_out=ac).then_inc(s, 1))
                return tgt
            C.emit(eng, lambda e, o=out_ap, i=in_ap, fu=func, b=bias, sc=scale, ac=accum:
                   e.activation(o, i, fu, bias=b, scale=sc, accum_out=ac))

        def vop(name, *args, inc=None, **kw):
            if inc is not None:
                tgt = inc.inc()
                C.emit("V", lambda e, a=args, k=kw, s=inc.h:
                       getattr(e, name)(*a, **k).then_inc(s, 1))
                return tgt
            C.emit("V", lambda e, a=args, k=kw: getattr(e, name)(*a, **k))

        def barrier():
            for eng in "TVSGY":
                for s in (tm, vm, sm, wd, ld, gm, st):
                    if s.n:
                        C.wait(eng, s, s.n)

        C.emit("V", lambda e: e.memset(EPSC[0:1, 0:1], 1e-5))
        C.emit("V", lambda e: e.memset(EPSC[0:1, 1:2], 1e-6))
        dma("G", ONESC[:], dr["ones_col"][:], wd)
        dma("G", ONESR_B[:], dr["ones_row"][:], wd)
        dma("G", GW8[:],
            dr["L0f_gate_w"].rearrange("(c p) e -> p c e", p=128), wd)
        dma("G", RW8[:],
            dr["L1f_router_wT"].rearrange("(c p) e -> p c e", p=128), wd)
        consts_wd = wd.n
        dma("Y", ONESR_F[:], dr["ones_row"][:], ld)
        dma("Y", IDN8[:], dr["idn8"][:], ld)
        dma("Y", IDN128[:], dr["idn128"][:], ld)
        x_ld = dma("Y", X[:],
                   dr["xT"].rearrange("(c p) t -> p c t", p=128), ld)

        bias_slots = {}
        bias_next = [0]

        def load_bias(name, dram_name, nchunk):
            base = bias_next[0]
            bias_next[0] += nchunk
            bias_slots[name] = base
            return dma("Y", BIASV[0:dr[dram_name].shape[0], base:base + nchunk],
                       dr[dram_name][:], ld)

        def bias_ap(name, c):
            b = bias_slots[name]
            return BIASV[:, b + c:b + c + 1]

        for nm, dnm, nch in [("qb", "L1a_qb", 8), ("kb", "L1a_kb", 8),
                         ("vb", "L1a_vb", 8), ("ob", "L1a_ob", 8),
                         ("ln_g", "L1f_ln_gamma", 8), ("ln_b", "L1f_ln_beta", 8),
                             ("b_proj", "L0a_b_proj", 8), ("bq_d", "L0a_bq_d", 4),
                             ("b_qk", "L0a_b_qk", 16), ("bkv_d", "L0a_bkv_d", 1),
                             ("g1", "L1a_g1", 8), ("b1", "L1a_b1", 8),
                             ("g2", "L1a_g2", 8), ("b2", "L1a_b2", 8)]:
            load_bias(nm, dnm, nch)
        bias_slots["bqk8"] = bias_next[0]
        bias_next[0] += 16
        C.wait("V", ld, ld.n)
        vop("tensor_scalar", BIASV[0:64, bias_slots["bqk8"]:bias_slots["bqk8"] + 16],
            BIASV[0:64, bias_slots["b_qk"]:bias_slots["b_qk"] + 16], 0.125, None, ALU.mult)
        consts_ld = ld.n

        def emit_ln(src_sb, src_is_f32, dst_sb, affine=None, eps=0):
            inv = 1.0 / D
            prev_lv = 0
            for tb in range(NTB):
                ts = slice(tb * TB, (tb + 1) * TB)
                mm_done = [0] * NC_D
                for c in range(NC_D):
                    if c >= 2:
                        C.wait("S", tm, mm_done[c - 2])
                    if src_is_f32:
                        act("S", XB[:, c % 2, :], src_sb[:, c, ts], AF.Identity)
                        stat = XB[:, c % 2, :]
                    else:
                        stat = src_sb[:, c, ts]
                    cast_t = act("S", SQ[:, c % 2, :], src_sb[:, c, ts], AF.Square, inc=sm)
                    C.wait("T", sm, cast_t)
                    mm(pb[5][0:1, :], ONESC[:], stat, start=(c == 0), stop=(c == NC_D - 1))
                    mm_done[c] = mm(pb[6][0:1, :], ONESC[:], SQ[:, c % 2, :],
                                    start=(c == 0), stop=(c == NC_D - 1), inc=tm)
                sums_done = tm.n
                C.wait("V", tm, sums_done)
                vop("tensor_scalar", ROWS[0:1, 0, :], pb[5][0:1, :], inv, None, ALU.mult)
                vop("tensor_scalar", ROWS[0:1, 1, :], pb[6][0:1, :], inv, None, ALU.mult)
                vop("tensor_tensor", ROWS[0:1, 2, :], ROWS[0:1, 0, :], ROWS[0:1, 0, :], ALU.mult)
                v_done = vop("tensor_tensor", ROWS[0:1, 1, :], ROWS[0:1, 1, :],
                             ROWS[0:1, 2, :], ALU.subtract, inc=vm)
                C.wait("S", vm, v_done)
                act("S", ROWS[0:1, 2, :], ROWS[0:1, 1, :], AF.Ln,
                    bias=EPSC[0:1, eps:eps + 1], scale=1.0)
                rstd_done = act("S", ROWS[0:1, 3, :], ROWS[0:1, 2, :], AF.Exp,
                                bias=0.0, scale=-0.5, inc=sm)
                C.wait("T", sm, rstd_done)
                mm(pb[5][:, :], ONESR_F[:], ROWS[0:1, 0, :], start=True, stop=True)
                bc_done = mm(pb[6][:, :], ONESR_F[:], ROWS[0:1, 3, :], start=True,
                             stop=True, inc=tm)
                C.wait("V", tm, bc_done)
                aff_s = 0
                for c in range(NC_D):
                    last = (c == NC_D - 1)
                    dst = dst_sb[:, c, ts]
                    vop("tensor_tensor", dst, src_sb[:, c, ts], pb[5][:, :], ALU.subtract)
                    if affine is None:
                        lv = vop("tensor_tensor", dst, dst, pb[6][:, :], ALU.mult,
                                 inc=vm if last else None)
                    else:
                        g_nm, b_nm = affine
                        lv = vop("tensor_tensor", dst, dst, pb[6][:, :], ALU.mult, inc=vm)
                        C.wait("S", vm, lv)
                        aff_s = act("S", dst, dst, AF.Identity,
                                    bias=bias_ap(b_nm, c), scale=bias_ap(g_nm, c),
                                    inc=sm if last else None)
                C.wait("T", vm, lv)
                if affine is not None:
                    C.wait("T", sm, aff_s)
                    lv = vm.n
                prev_lv = lv
            return vm.n

        def gating_16tiles(logits_src_fn, WT_dst, gates_mode, half_scale=True):
            GLOG = gating_16tiles.GLOG
            GTOK = gating_16tiles.GTOK
            GWRK = gating_16tiles.GWRK
            GS1 = gating_16tiles.GS1
            GS2 = gating_16tiles.GS2
            for tb in range(NTB):
                if tb > 0:
                    C.wait("T", sm, sm.n)
                r = logits_src_fn(tb)
                C.wait("S", tm, r)
                act("S", GLOG[:, tb * TB:(tb + 1) * TB], pb[4][0:8, :], AF.Identity, inc=sm)
            if STAGE == "g1":
                C.wait("V", sm, sm.n)
                vop("tensor_copy", X[0:8, 0, :], GLOG[:], inc=vm)
                return sm.n
            C.wait("T", sm, sm.n)
            for i in range(16):
                trans(pb[4][:, 8 * i:8 * i + 8], GLOG[0:8, i * 128:(i + 1) * 128], IDN8[:],
                      inc=tm if i == 15 else None)
            C.wait("V", tm, tm.n)
            vop("tensor_copy", GTOK[:].rearrange("p a b -> p (a b)"), pb[4][:, 0:128], inc=vm)
            if STAGE == "g2":
                vop("tensor_copy", X[:, 0, 0:128], GTOK[:].rearrange("p a b -> p (a b)"), inc=vm)
                return sm.n
            G_ = lambda i: GTOK[:, i, :]
            W_ = lambda i: GWRK[:, i, :]
            M1 = lambda i: GS1[:, i:i + 1]
            M2 = lambda i: GS2[:, i:i + 1]
            M3 = lambda i: gating_16tiles.GS3[:, i:i + 1]
            for i in range(16):
                vop("tensor_reduce", M2(i), G_(i), AX.X, ALU.max, negate=True,
                    inc=vm if i == 15 else None)
            C.wait("S", vm, vm.n)
            for i in range(16):
                act("S", G_(i), G_(i), AF.Exp, bias=M2(i), scale=1.0, accum=M1(i),
                    inc=sm if i == 15 else None)
            C.wait("V", sm, sm.n)
            for i in range(16):
                vop("reciprocal", M3(i), M1(i))
            for i in range(16):
                vop("tensor_scalar", G_(i), G_(i), M3(i), None, ALU.mult)
            for i in range(16):
                vop("tensor_reduce", M1(i), G_(i), AX.X, ALU.max)
            for i in range(16):
                vop("tensor_scalar", W_(i), G_(i), M1(i), None, ALU.is_ge)
            for i in range(16):
                vop("scalar_tensor_tensor", W_(i), W_(i), -1e30, G_(i), ALU.mult, ALU.add)
            for i in range(16):
                vop("tensor_reduce", M2(i), W_(i), AX.X, ALU.max)
            for i in range(16):
                vop("tensor_scalar", W_(i), G_(i), M2(i), None, ALU.is_ge)
            sc2 = 0.5 if half_scale else 1.0
            if gates_mode == "l0":
                for i in range(16):
                    vop("tensor_tensor", W_(i), W_(i), G_(i), ALU.mult)
                for i in range(16):
                    vop("tensor_reduce", M2(i), W_(i), AX.X, ALU.add)
                for i in range(16):
                    vop("tensor_scalar", M3(i), M2(i), 1e-20, None, ALU.add)
                for i in range(16):
                    vop("reciprocal", M1(i), M3(i))
                for i in range(16):
                    vop("tensor_scalar", W_(i), W_(i), M1(i), sc2, ALU.mult, ALU.mult,
                        inc=vm if i == 15 else None)
            else:
                for i in range(16):
                    vop("tensor_scalar", M3(i), M1(i), -1.0, None, ALU.mult,
                        inc=vm if i == 15 else None)
                C.wait("S", vm, vm.n)
                for i in range(16):
                    act("S", G_(i), G_(i), AF.Exp, bias=M3(i), scale=1.0,
                        inc=sm if i == 15 else None)
                C.wait("V", sm, sm.n)
                for i in range(16):
                    vop("tensor_tensor", W_(i), W_(i), G_(i), ALU.mult)
                for i in range(16):
                    vop("tensor_reduce", M2(i), W_(i), AX.X, ALU.add)
                for i in range(16):
                    vop("reciprocal", M1(i), M2(i))
                for i in range(16):
                    vop("tensor_scalar", W_(i), W_(i), M1(i), sc2, ALU.mult, ALU.mult,
                        inc=vm if i == 15 else None)
            gw_done = vm.n
            if STAGE == "g3":
                vop("tensor_copy", X[:, 0, 0:128], GWRK[:].rearrange("p a b -> p (a b)"), inc=vm)
                return sm.n
            C.wait("T", vm, gw_done)
            wt_evac = [0] * 16
            for i in range(16):
                bk = i % 4
                if i >= 4:
                    C.wait("T", sm, wt_evac[i - 4])
                t4 = trans(pb[bk][0:8, 0:128],
                           gating_16tiles.GWRK[:, i, :], IDN128[:], inc=tm)
                C.wait("S", tm, t4)
                wt_evac[i] = act("S", WT_dst[0:8, 128 * i:128 * (i + 1)],
                                 pb[bk][0:8, 0:128], AF.Identity, inc=sm)
            return sm.n

        def l0ffn(XN, xn_v, fstack):
            GLOG = sbuf("GLOG", [8, T], F32, stack=fstack)
            WT = sbuf("WT", [8, T], BF16, stack=fstack)
            GTOK = sbuf("GTOK", [128, 16, 8], F32, stack=fstack)
            GWRK = sbuf("GWRK", [128, 16, 8], F32, stack=fstack)
            GS1 = sbuf("GS1", [128, 16], F32, stack=fstack)
            GS2 = sbuf("GS2", [128, 16], F32, stack=fstack)
            GS3 = sbuf("GS3", [128, 16], F32, stack=fstack)
            WROW = sbuf("WROW", [1, T], BF16, stack=fstack)
            WPOOL = sbuf("WPOOL", [128, 3, 8, 1024], BF16, stack=fstack)
            SG = sbuf("SG", [128, TB], BF16, stack=fstack)
            UU = sbuf("UU", [128, TB], BF16, stack=fstack)
            MID = sbuf("MID", [128, 8, TB], BF16, stack=fstack)
            gating_16tiles.GLOG = GLOG
            gating_16tiles.GTOK = GTOK
            gating_16tiles.GWRK = GWRK
            gating_16tiles.GS1 = GS1
            gating_16tiles.GS2 = GS2
            gating_16tiles.GS3 = GS3

            C.wait("T", wd, consts_wd)
            C.wait("T", vm, xn_v)

            def logits(tb):
                ts = slice(tb * TB, (tb + 1) * TB)
                for c in range(NC_D):
                    r = mm(pb[4][0:8, :], GW8[:, c, :], XN[:, c, ts],
                           start=(c == 0), stop=(c == NC_D - 1),
                           inc=tm if c == NC_D - 1 else None)
                return r

            wt_done = gating_16tiles(logits, WT, "l0")
            C.wait("T", sm, wt_done)
            if STAGE in ("g1", "g2", "g3"):
                return vm.n
            if STAGE == "gating":
                C.wait("V", sm, wt_done)
                vop("tensor_copy", X[0:8, 0, :], WT[:], inc=vm)
                return vm.n

            ew = [dr["L0f_ew_gate"], dr["L0f_ew_up"], dr["L0f_ew_down"]]
            sw = [dr["L0f_sw_gate"], dr["L0f_sw_up"], dr["L0f_sw_down"]]

            def wsrc(kind, e):
                a = sw[kind] if e == 8 else ew[kind][e]
                return a.rearrange("(c p) m -> p c m", p=128)

            order = [8, 0, 1, 2, 3, 4, 5, 6, 7]
            wtgt = {}

            def stream(e):
                for kind in range(3):
                    wtgt[(e, kind)] = dma("G", WPOOL[:, kind, :, :],
                                          wsrc(kind, e), wd)

            bank_free = {i: 0 for i in range(7)}
            last_mid = 0
            stream(order[0])
            wrow_ld = 0
            for ei, e in enumerate(order):
                for kind in range(3):
                    C.wait("T", wd, wtgt[(e, kind)])
                if e != 8:
                    C.wait("Y", sm, wt_done)
                    if wrow_ld:
                        C.wait("Y", tm, last_wbc)
                    wrow_ld = dma("Y", WROW[:], WT[e:e + 1, :], ld)
                for tb in range(NTB):
                    ts = slice(tb * TB, (tb + 1) * TB)
                    if e != 8:
                        C.wait("T", ld, wrow_ld)
                        C.wait("T", vm, bank_free[4])
                        wbc = mm(pb[4][:, :], ONESR_B[:], WROW[0:1, ts],
                                 start=True, stop=True, inc=tm)
                        last_wbc = wbc
                        bank_free[4] = 0
                    for fc in range(NC_D):
                        C.wait("T", sm, bank_free[0])
                        for c in range(NC_D):
                            mm(pb[0][:, :], WPOOL[:, 0, c, 128 * fc:128 * fc + 128],
                               XN[:, c, ts], start=(c == 0), stop=(c == NC_D - 1),
                               inc=tm if c == NC_D - 1 else None)
                        g_done = tm.n
                        C.wait("T", vm, bank_free[1])
                        for c in range(NC_D):
                            mm(pb[1][:, :], WPOOL[:, 1, c, 128 * fc:128 * fc + 128],
                               XN[:, c, ts], start=(c == 0), stop=(c == NC_D - 1),
                               inc=tm if c == NC_D - 1 else None)
                        u_done = tm.n
                        C.wait("S", tm, g_done)
                        C.wait("S", vm, last_mid)
                        sg_t = act("S", SG[:], pb[0][:, :], AF.Silu, inc=sm)
                        bank_free[0] = sg_t
                        C.wait("V", tm, u_done)
                        vop("tensor_copy", UU[:], pb[1][:, :])
                        C.wait("V", sm, sg_t)
                        if e == 8:
                            mv = vop("scalar_tensor_tensor", MID[:, fc, :], SG[:], 0.5, UU[:],
                                     ALU.mult, ALU.mult, inc=vm)
                        else:
                            C.wait("V", tm, wbc)
                            vop("tensor_tensor", MID[:, fc, :], SG[:], UU[:], ALU.mult)
                            mv = vop("tensor_tensor", MID[:, fc, :], MID[:, fc, :],
                                     pb[4][:, :], ALU.mult, inc=vm)
                        bank_free[1] = mv
                        last_mid = mv
                        if e != 8:
                            bank_free[4] = mv
                    mid_done = vm.n
                    if tb == NTB - 1 and ei + 1 < len(order):
                        ne = order[ei + 1]
                        C.wait("G", tm, tm.n)
                        for kind in range(2):
                            wtgt[(ne, kind)] = dma("G", WPOOL[:, kind, :, :],
                                                   wsrc(kind, ne), wd)
                    C.wait("T", vm, mid_done)
                    for dc in range(NC_D):
                        db = 2 + (dc % 2)
                        C.wait("T", vm, bank_free[db])
                        for fc in range(NC_D):
                            mm(pb[db][:, :], WPOOL[:, 2, fc, 128 * dc:128 * dc + 128],
                               MID[:, fc, :], start=(fc == 0), stop=(fc == NC_D - 1),
                               inc=tm if fc == NC_D - 1 else None)
                        d_done = tm.n
                        C.wait("V", tm, d_done)
                        av = vop("tensor_tensor", X[:, dc, ts], X[:, dc, ts], pb[db][:, :],
                                 ALU.add, inc=vm)
                        bank_free[db] = av
                if ei + 1 < len(order):
                    ne = order[ei + 1]
                    C.wait("G", tm, tm.n)
                    wtgt[(ne, 2)] = dma("G", WPOOL[:, 2, :, :], wsrc(2, ne), wd)
            return vm.n

        def l0attn(astack):
            CQ = sbuf("CQ", [128, 4, T], BF16, stack=astack)
            with ExitStack() as sA:
                XN = sbuf("XN_a", [128, NC_D, T], BF16, stack=sA)
                WQ = sbuf("WQ", [128, 8, 512], BF16, stack=sA)
                WKV = sbuf("WKV", [128, 8, 64], BF16, stack=sA)
                CKVL = sbuf("CKVL", [64, T], BF16, stack=sA)
                wq_t = dma("G", WQ[:], dr["L0a_wq_dT"].rearrange("(c p) m -> p c m", p=128), wd)
                wkv_t = dma("G", WKV[:], dr["L0a_wkv_dT"].rearrange("(c p) m -> p c m", p=128), wd)
                xn_v = emit_ln(X, True, XN)
                C.wait("T", vm, xn_v)
                C.wait("T", wd, max(wq_t, wkv_t))
                bfree = {i: 0 for i in range(4)}
                for mc in range(4):
                    for tb in range(NTB):
                        ts = slice(tb * TB, (tb + 1) * TB)
                        bk = tb % 2
                        C.wait("T", vm, bfree[bk])
                        for c in range(NC_D):
                            r = mm(pb[bk][:, :], WQ[:, c, 128 * mc:128 * mc + 128],
                                   XN[:, c, ts], start=(c == 0), stop=(c == NC_D - 1),
                                   inc=tm if c == NC_D - 1 else None)
                        C.wait("V", tm, r)
                        bfree[bk] = vop("tensor_scalar", CQ[:, mc, ts], pb[bk][:, :],
                                        bias_ap("bq_d", mc), None, ALU.add, inc=vm)
                for tb in range(NTB):
                    ts = slice(tb * TB, (tb + 1) * TB)
                    bk = 2 + tb % 2
                    C.wait("T", vm, bfree[bk])
                    for c in range(NC_D):
                        r = mm(pb[bk][0:64, :], WKV[:, c, :], XN[:, c, ts],
                               start=(c == 0), stop=(c == NC_D - 1),
                               inc=tm if c == NC_D - 1 else None)
                    C.wait("V", tm, r)
                    bkv_ap = BIASV[0:64, bias_slots["bkv_d"]:bias_slots["bkv_d"] + 1]
                    bfree[bk] = vop("tensor_scalar", CKVL[:, ts], pb[bk][0:64, :],
                                    bkv_ap, None, ALU.add, inc=vm)
                ckv_done = vm.n
                C.wait("G", vm, ckv_done)
                bounce_t = dma("G", ckv_in[:, :], CKVL[:], wd)
            C.wait("G", wd, bounce_t)
            cc_t = ccs.inc()
            C.emit("G", lambda e, s=ccs.h: e.collective_compute(
                "AllGather", ALU.bypass, replica_groups=[list(range(NCORES))],
                ins=[ckv_in], outs=[ckv_out]).then_inc(s, 1))

            WQK = sbuf("WQK", [128, 4, 1024], BF16, stack=astack)
            QEXT = sbuf("QEXT", [65, 16, 1024], BF16, stack=astack)
            CKV = sbuf("CKV", [65, 8192], BF16, stack=astack)
            KM = sbuf("KM", [64, 65], BF16, stack=astack)
            KSUM = sbuf("KSUM", [64, 2], F32, stack=astack)
            WVU = sbuf("WVU", [65, 1024], BF16, stack=astack)
            YALL = sbuf("YALL", [64, 16, 512], BF16, stack=astack)
            YTMP = sbuf("YTMP", [64, 512], BF16, stack=astack)
            ZR = sbuf("ZR", [1, 512], F32, stack=astack)
            WPJ = sbuf("WPJ", [64, 16, 512], BF16, stack=astack)
            wqk_t = dma("G", WQK[:], dr["L0a_w_qkT"].rearrange("(c p) m -> p c m", p=128), wd)
            wvu_t = dma("G", WVU[0:64, :], dr["L0a_wv_uT"][:], wd)
            wvu_t = dma("G", WVU[64:65, :], dr["L0a_bv_u"][:], wd)
            C.emit("V", lambda e: e.memset(CKV[64:65, :], 1.0))
            C.emit("V", lambda e: e.memset(KM[:, :], 0.0))
            last_yv = [0]
            sguard = {i: 0 for i in range(7)}
            p3guard = {i: 0 for i in range(3)}

            for b in range(2):
                C.wait("V", tm, tm.n)
                C.wait("S", tm, tm.n)
                tbs = [2 * b, 2 * b + 1]
                C.wait("G", ccs, cc_t)
                C.wait("G", tm, tm.n)
                ckv_t = dma("G", CKV[0:64, :].rearrange("d (r t) -> d r t", r=8),
                            ckv_out.rearrange("(r d) t -> d r t", d=64)[:, :, 1024 * b:1024 * (b + 1)],
                            wd)
                C.wait("T", wd, wqk_t)
                for h in range(16):
                    for t2 in range(2):
                        ts = slice(tbs[t2] * TB, (tbs[t2] + 1) * TB)
                        qs = slice(t2 * TB, (t2 + 1) * TB)
                        bk = t2 % 2
                        C.wait("T", vm, bfree[bk])
                        for kc in range(4):
                            r = mm(pb[bk][0:64, :], WQK[:, kc, 64 * h:64 * h + 64],
                                   CQ[:, kc, ts], start=(kc == 0), stop=(kc == 3),
                                   inc=tm if kc == 3 else None)
                        C.wait("V", tm, r)
                        bfree[bk] = vop("tensor_scalar", QEXT[0:64, h, qs], pb[bk][0:64, :],
                                        0.125, BIASV[0:64, bias_slots["bqk8"] + h:
                                                      bias_slots["bqk8"] + h + 1],
                                        ALU.mult, ALU.add, inc=vm)
                C.wait("V", wd, ckv_t)
                vop("tensor_reduce", KSUM[:, 0:1], CKV[0:64, :], AX.X, ALU.add)
                vop("tensor_scalar", KSUM[:, 1:2], KSUM[:, 0:1], -1.0 / 8192.0, None, ALU.mult)
                km_v = vop("tensor_copy", KM[:, 64:65], KSUM[:, 1:2], inc=vm)
                C.wait("T", vm, vm.n)
                mrow_ev = 0
                for h in range(16):
                    for t2 in range(2):
                        qs = slice(t2 * TB, (t2 + 1) * TB)
                        if mrow_ev:
                            C.wait("T", sm, mrow_ev)
                        r = mm(pb[6][0:65, :], KM[:], QEXT[0:64, h, qs],
                               start=True, stop=True, inc=tm)
                        C.wait("S", tm, r)
                        mrow_ev = act("S", QEXT[64:65, h, qs], pb[6][64:65, :],
                                      AF.Identity, inc=sm)
                C.wait("T", sm, mrow_ev)
                for t2 in range(2):
                    qs = slice(t2 * TB, (t2 + 1) * TB)
                    ts = slice(tbs[t2] * TB, (tbs[t2] + 1) * TB)
                    with ExitStack() as sC:
                        VEXT = sbuf("VEXT", [128, 64, 65], BF16, stack=sC)
                        WVX = sbuf("WVX", [65, 65], BF16, stack=sC)
                        P3 = sbuf("P3", [128, 3, TB], BF16, stack=sC)
                        C.wait("V", wd, wvu_t)
                        for h in range(16):
                            C.wait("V", tm, tm.n)
                            vop("memset", WVX[:, :], 0.0)
                            vop("tensor_copy", WVX[:, 0:64], WVU[:, 64 * h:64 * h + 64])
                            wvx_v = vop("memset", WVX[64:65, 64:65], 1.0, inc=vm)
                            C.wait("T", vm, wvx_v)
                            vb_last = 0
                            for sc in range(64):
                                bk = 4 + sc % 2
                                C.wait("T", sm, sguard[bk])
                                r = mm(pb[bk][:, 0:65], CKV[:, 128 * sc:128 * sc + 128],
                                       WVX[:], start=True, stop=True, inc=tm)
                                C.wait("S", tm, r)
                                vb_last = act("S", VEXT[:, sc, :], pb[bk][:, 0:65],
                                              AF.Identity, inc=sm)
                                sguard[bk] = vb_last
                            C.wait("T", sm, vb_last)
                            exp_of = [0] * 64
                            for sc in range(64):
                                bk = sc % 2
                                C.wait("T", sm, sguard[bk])
                                mm(pb[bk][:, :], CKV[:, 128 * sc:128 * sc + 128],
                                   QEXT[0:65, h, qs], start=True, stop=True, inc=tm)
                                sc_t = tm.n
                                if sc >= 1:
                                    C.wait("T", sm, exp_of[sc - 1])
                                    mm(pb[2][0:65, :], VEXT[:, sc - 1, :], P3[:, (sc - 1) % 3, :],
                                       start=(sc - 1 == 0), stop=False, inc=tm)
                                    p3guard[(sc - 1) % 3] = tm.n
                                C.wait("S", tm, sc_t)
                                C.wait("S", tm, p3guard[sc % 3])
                                ex = act("S", P3[:, sc % 3, :], pb[bk][:, :],
                                         AF.Exp, inc=sm)
                                exp_of[sc] = ex
                                sguard[bk] = ex
                            C.wait("T", sm, exp_of[63])
                            mm(pb[2][0:65, :], VEXT[:, 63, :], P3[:, 63 % 3, :],
                               start=False, stop=True, inc=tm)
                            p3guard[63 % 3] = tm.n
                            pv_last = tm.n
                            C.wait("V", tm, pv_last)
                            zr_v = vop("reciprocal", ZR[:], pb[2][64:65, :], inc=vm)
                            C.wait("T", vm, zr_v)
                            bc_t = mm(pb[3][0:64, :], ONESR_F[0:1, 0:64], ZR[:],
                                      start=True, stop=True, inc=tm)
                            C.wait("S", tm, pv_last)
                            C.wait("S", vm, last_yv[0])
                            yt_ev = act("S", YTMP[:], pb[2][0:64, :], AF.Identity, inc=sm)
                            C.wait("V", sm, yt_ev)
                            C.wait("V", tm, bc_t)
                            y_v = vop("tensor_tensor", YALL[0:64, h, :],
                                      YTMP[:], pb[3][0:64, :], ALU.mult, inc=vm)
                            last_yv[0] = y_v
                            C.wait("T", sm, yt_ev)
                            C.wait("T", vm, y_v)
                    C.wait("T", vm, vm.n)
                    pfree = {0: 0, 1: 0}
                    for half in range(2):
                        C.wait("G", tm, tm.n)
                        wpj_t = dma("G", WPJ[:],
                                    dr["L0a_w_projT"].rearrange("(h p) m -> p h m", p=64)
                                    [:, :, 512 * half:512 * (half + 1)], wd)
                        C.wait("T", wd, wpj_t)
                        for mc2 in range(4):
                            mc = 4 * half + mc2
                            bk = mc % 2
                            C.wait("T", vm, pfree[bk])
                            for h in range(16):
                                r = mm(pb[bk][:, :], WPJ[:, h, 128 * mc2:128 * mc2 + 128],
                                       YALL[0:64, h, :], start=(h == 0), stop=(h == 15),
                                       inc=tm if h == 15 else None)
                            C.wait("V", tm, r)
                            pfree[bk] = vop("scalar_tensor_tensor", X[:, mc, ts],
                                            pb[bk][:, :], bias_ap("b_proj", mc),
                                            X[:, mc, ts], ALU.add, ALU.add, inc=vm)
            return vm.n

        def l1ffn(XN, xn_v, fstack):
            WT = sbuf("WT1", [8, T], BF16, stack=fstack)
            WROW = sbuf("WROW1", [1, T], BF16, stack=fstack)
            C.wait("S", vm, xn_v)
            if STAGE == "l1nc":
                C.wait("V", vm, xn_v)
                for c in range(NC_D):
                    vop("tensor_copy", X[:, c, :], XN[:, c, :], inc=vm)
                return vm.n
            if STAGE == "l1n0":
                XN2 = sbuf("XN2dbg", [128, NC_D, T], BF16, stack=fstack)
                nrm_v = emit_ln(XN, False, XN2, affine=None)
                XN = XN2
            else:
                nrm_v = emit_ln(XN, False, XN, affine=("ln_g", "ln_b"))
            if STAGE in ("l1n", "l1n0"):
                C.wait("V", vm, nrm_v)
                for c in range(NC_D):
                    vop("tensor_copy", X[:, c, :], XN[:, c, :], inc=vm)
                return vm.n
            with ExitStack() as gs:
                GLOG = sbuf("GLOG1", [8, T], F32, stack=gs)
                GTOK = sbuf("GTOK1", [128, 16, 8], F32, stack=gs)
                GWRK = sbuf("GWRK1", [128, 16, 8], F32, stack=gs)
                GS1 = sbuf("GS11", [128, 16], F32, stack=gs)
                GS2 = sbuf("GS21", [128, 16], F32, stack=gs)
                GS3 = sbuf("GS31", [128, 16], F32, stack=gs)
                gating_16tiles.GLOG = GLOG
                gating_16tiles.GTOK = GTOK
                gating_16tiles.GWRK = GWRK
                gating_16tiles.GS1 = GS1
                gating_16tiles.GS2 = GS2
                gating_16tiles.GS3 = GS3
                C.wait("T", wd, consts_wd)
                C.wait("T", vm, nrm_v)

                def logits(tb):
                    ts = slice(tb * TB, (tb + 1) * TB)
                    for c in range(NC_D):
                        r = mm(pb[4][0:8, :], RW8[:, c, :], XN[:, c, ts],
                               start=(c == 0), stop=(c == NC_D - 1),
                               inc=tm if c == NC_D - 1 else None)
                    return r

                wt_done = gating_16tiles(logits, WT, "l1")
                C.wait("T", sm, wt_done)
                if STAGE == "l1g":
                    C.wait("V", sm, wt_done)
                    vop("tensor_copy", X[0:8, 0, :], WT[:], inc=vm)
                    return vm.n
            WGU = sbuf("WGU", [128, 2, 8, 1024], BF16, stack=fstack)
            WDS = sbuf("WDS", [128, 2, 8, 1024], BF16, stack=fstack)
            SG = sbuf("SG1", [128, TB], BF16, stack=fstack)
            UU = sbuf("UU1", [128, TB], BF16, stack=fstack)
            MID = sbuf("MID1", [128, 8, TB], BF16, stack=fstack)

            guT = dr["L1f_gate_upT"]
            dnT = dr["L1f_downT"]

            def stream_gu(e, sg):
                if e == 8:
                    gsrc = dr["L1f_sh_gate_upT"].rearrange("(c p) m -> p c m", p=128)
                    t1 = dma("G", WGU[:, 0, :, :], gsrc[:, :, 0:1024], wd)
                    t2 = dma("G", WGU[:, 1, :, :], gsrc[:, :, 1024:2048], wd)
                else:
                    gsrc = guT[e].rearrange("(c p) m -> p c m", p=128)
                    t1 = dma("G", WGU[:, 0, :, :], gsrc[:, :, 1024 * sg:1024 * (sg + 1)], wd)
                    t2 = dma("G", WGU[:, 1, :, :], gsrc[:, :, 3072 + 1024 * sg:3072 + 1024 * (sg + 1)], wd)
                return max(t1, t2)

            def stream_ds(e, sg, buf):
                if e == 8:
                    return dma("G", WDS[:, buf], dr["L1f_sh_downT"].rearrange("(c p) m -> p c m", p=128), wd)
                return dma("G", WDS[:, buf], dnT[e].rearrange("(c p) m -> p c m", p=128)
                           [:, 8 * sg:8 * (sg + 1), :], wd)

            steps = [(e, sg) for e in range(8) for sg in range(3)] + [(8, 0)]
            if L1STEPS is not None:
                steps = steps[:L1STEPS] if L1STEPS > 0 else [steps[-1]]
            bank_free = {i: 0 for i in range(7)}
            last_mid = 0
            C.wait("G", sm, sm.n)
            C.wait("G", vm, vm.n)
            C.wait("G", tm, tm.n)
            wgu_tgt = stream_gu(*steps[0])
            wds_tgt = stream_ds(*steps[0], 0)
            ds_last_use = [0, 0]
            wrow_ld = 0
            last_wbc = 0
            cur_e = None
            for si, (e, sg) in enumerate(steps):
                wgu_cur, wds_cur = wgu_tgt, wds_tgt
                if e != 8 and e != cur_e:
                    cur_e = e
                    C.wait("Y", sm, wt_done)
                    if wrow_ld:
                        C.wait("Y", tm, last_wbc)
                    wrow_ld = dma("Y", WROW[:], WT[e:e + 1, :], ld)
                C.wait("T", wd, max(wgu_cur, wds_cur))
                for tb in range(NTB):
                    ts = slice(tb * TB, (tb + 1) * TB)
                    if e != 8:
                        C.wait("T", ld, wrow_ld)
                        C.wait("T", vm, bank_free[4])
                        wbc = mm(pb[4][:, :], ONESR_B[:], WROW[0:1, ts],
                                 start=True, stop=True, inc=tm)
                        last_wbc = wbc
                    for fc in range(NC_D):
                        C.wait("T", sm, bank_free[0])
                        for c in range(NC_D):
                            mm(pb[0][:, :], WGU[:, 0, c, 128 * fc:128 * fc + 128],
                               XN[:, c, ts], start=(c == 0), stop=(c == NC_D - 1),
                               inc=tm if c == NC_D - 1 else None)
                        g_done = tm.n
                        C.wait("T", vm, bank_free[1])
                        for c in range(NC_D):
                            mm(pb[1][:, :], WGU[:, 1, c, 128 * fc:128 * fc + 128],
                               XN[:, c, ts], start=(c == 0), stop=(c == NC_D - 1),
                               inc=tm if c == NC_D - 1 else None)
                        u_done = tm.n
                        C.wait("S", tm, g_done)
                        C.wait("S", vm, last_mid)
                        sg_t = act("S", SG[:], pb[0][:, :], AF.Silu, inc=sm)
                        bank_free[0] = sg_t
                        C.wait("V", tm, u_done)
                        vop("tensor_copy", UU[:], pb[1][:, :])
                        C.wait("V", sm, sg_t)
                        if e == 8:
                            mv = vop("scalar_tensor_tensor", MID[:, fc, :], SG[:], 0.5, UU[:],
                                     ALU.mult, ALU.mult, inc=vm)
                        else:
                            C.wait("V", tm, wbc)
                            vop("tensor_tensor", MID[:, fc, :], SG[:], UU[:], ALU.mult)
                            mv = vop("tensor_tensor", MID[:, fc, :], MID[:, fc, :],
                                     pb[4][:, :], ALU.mult, inc=vm)
                        bank_free[1] = mv
                        last_mid = mv
                        if e != 8:
                            bank_free[4] = mv
                    mid_done = vm.n
                    if tb == NTB - 1 and si + 1 < len(steps):
                        C.wait("G", tm, tm.n)
                        wgu_tgt = stream_gu(*steps[si + 1])
                    C.wait("T", vm, mid_done)
                    for dc in range(NC_D):
                        db = 2 + (dc % 2)
                        C.wait("T", vm, bank_free[db])
                        for fc in range(NC_D):
                            mm(pb[db][:, :], WDS[:, si % 2, fc, 128 * dc:128 * dc + 128],
                               MID[:, fc, :], start=(fc == 0), stop=(fc == NC_D - 1),
                               inc=tm if fc == NC_D - 1 else None)
                        d_done = tm.n
                        C.wait("V", tm, d_done)
                        av = vop("tensor_tensor", X[:, dc, ts], X[:, dc, ts], pb[db][:, :],
                                 ALU.add, inc=vm)
                        bank_free[db] = av
                ds_last_use[si % 2] = tm.n
                if si + 1 < len(steps):
                    C.wait("G", tm, ds_last_use[(si + 1) % 2])
                    wds_tgt = stream_ds(*steps[si + 1], (si + 1) % 2)
            return vm.n


        def build_phm_w(Wsb, a_name, s_name, bscratch):
            A_sb, S_sb, Wtmp = bscratch
            C.wait("G", tm, tm.n)
            a_t = dma("G", A_sb[:], dr[a_name][:], wd)
            C.wait("T", wd, a_t)
            last_rr = build_phm_w.last_rr
            for kk in range(8):
                for half in range(2):
                    base = 8192 * kk + 4096 * half
                    C.wait("G", tm, tm.n)
                    s_t = dma("G", S_sb[:], dr[s_name][:, base:base + 4096], wd)
                    C.wait("T", wd, s_t)
                    C.wait("S", ld, last_rr)
                    ev = 0
                    for j2 in range(8):
                        bk = j2 % 2
                        if j2 >= 2:
                            C.wait("T", sm, ev - 1)
                        r = mm(pb[bk][0:16, :], A_sb[:], S_sb[:, 512 * j2:512 * (j2 + 1)],
                               start=True, stop=True, inc=tm)
                        C.wait("S", tm, r)
                        ev = act("S", Wtmp[:, 512 * j2:512 * (j2 + 1)], pb[bk][0:16, :],
                                 AF.Identity, inc=sm)
                    C.wait("Y", sm, ev)
                    kglob = 32 * kk + 16 * half
                    p0 = kglob % 128
                    kh = kglob // 128
                    par = (kk * 2 + half) % 2
                    wr = dma("Y", wscr[par], Wtmp[:], ld)
                    C.wait("Y", ld, wr)
                    for i in range(4):
                        dma("Y",
                            Wsb[p0:p0 + 16, 2 * i + kh, :].rearrange("k (j l) -> k j l", j=4),
                            wscr[par][4 * i:4 * i + 4, :].rearrange("j (k l) -> k j l", k=16),
                            ld)
                    last_rr = ld.n
            build_phm_w.last_rr = last_rr
            return ld.n

        build_phm_w.last_rr = 0

        def l1attn(astack):
            XN = sbuf("XN_a1", [128, NC_D, T], BF16, stack=astack)
            QN = sbuf("QN", [128, NC_D, T], BF16, stack=astack)
            xn_v = emit_ln(X, True, XN)
            qn_v = emit_ln(XN, False, QN, affine=("g1", "b1"), eps=1)
            with ExitStack() as w3:
                WQ_ = sbuf("WQp", [128, 8, 1024], BF16, stack=w3)
                WK_ = sbuf("WKp", [128, 8, 1024], BF16, stack=w3)
                WV_ = sbuf("WVp", [128, 8, 1024], BF16, stack=w3)
                with ExitStack() as bs:
                    A_sb = sbuf("A_sb", [4, 16], BF16, stack=bs)
                    S_sb = sbuf("S_sb", [4, 4096], BF16, stack=bs)
                    Wtmp = sbuf("Wtmp", [16, 4096], BF16, stack=bs)
                    bscratch = (A_sb, S_sb, Wtmp)
                    build_phm_w(WQ_, "L1a_qA", "L1a_qS", bscratch)
                    build_phm_w(WK_, "L1a_kA", "L1a_kS", bscratch)
                    wv_done = build_phm_w(WV_, "L1a_vA", "L1a_vS", bscratch)
                    barrier()
                with ExitStack() as ps:
                    INDD = sbuf("INDD", [128, 8, 16], BF16, stack=ps)
                    INDB = sbuf("INDB", [16, 8, 128], BF16, stack=ps)
                    C.wait("G", tm, tm.n)
                    dma("G", INDD[:], dr["L1a_ind_dots"].rearrange("c p e -> p c e"), wd)
                    i_t = dma("G", INDB[:], dr["L1a_ind_bc"].rearrange("c j p -> j c p"), wd)
                    Q0 = sbuf("Q0", [128, TB], BF16, stack=ps)
                    Q1 = sbuf("Q1", [128, TB], BF16, stack=ps)
                    K0 = sbuf("K0", [128, TB], BF16, stack=ps)
                    K1 = sbuf("K1", [128, TB], BF16, stack=ps)
                    PR = sbuf("PR", [128, 2, TB], BF16, stack=ps)
                    V0, V1 = Q0, Q1
                    T1_ = K0
                    T2_ = K1
                    DOTS = sbuf("DOTS", [16, 4, TB], BF16, stack=ps)
                    COEF = sbuf("COEF", [16, 4, TB], BF16, stack=ps)
                    DEN = XB
                    C.wait("T", vm, qn_v)
                    C.wait("T", wd, i_t)
                    ev_guard = {0: 0, 1: 0, 2: 0}
                    vev = {0: 0, 1: 0}
                    prv = [0, 0]
                    att_v = 0
                    for tbp in range(2):
                        cs0 = slice(512 * tbp, 512 * (tbp + 1))
                        cs1 = slice(1024 + 512 * tbp, 1536 + 512 * tbp)
                        for c in range(NC_D):
                            specs = [(Q0, WQ_, QN, cs0, "qb", 0), (Q1, WQ_, QN, cs1, "qb", 1),
                                     (K0, WK_, XN, cs0, "kb", 2), (K1, WK_, XN, cs1, "kb", 0)]
                            if c > 0 or tbp > 0:
                                C.wait("S", vm, prod_v)
                            for di, (dst, Wp, src, cs, bnm, bk) in enumerate(specs):
                                C.wait("T", sm, ev_guard[bk])
                                for kc in range(NC_D):
                                    r = mm(pb[bk][:, :], Wp[:, kc, 128 * c:128 * c + 128],
                                           src[:, kc, cs], start=(kc == 0), stop=(kc == NC_D - 1),
                                           inc=tm if kc == NC_D - 1 else None)
                                C.wait("S", tm, r)
                                fn = AF.Relu
                                ev_guard[bk] = act("S", dst[:], pb[bk][:, :], fn,
                                                   bias=bias_ap(bnm, c), inc=sm)
                            C.wait("V", sm, ev_guard[0])
                            C.wait("V", sm, ev_guard[1])
                            C.wait("V", sm, ev_guard[2])
                            C.wait("V", tm, prv[0])
                            vop("tensor_tensor", PR[:, 0, :], Q0[:], K0[:], ALU.mult)
                            p1 = vop("tensor_tensor", PR[:, 1, :], Q0[:], K1[:], ALU.mult, inc=vm)
                            C.wait("T", vm, p1)
                            mm(pb[3][0:16, :], INDD[:, c, :], PR[:, 0, :],
                               start=(c == 0), stop=(c == NC_D - 1))
                            r1 = mm(pb[4][0:16, :], INDD[:, c, :], PR[:, 1, :],
                                    start=(c == 0), stop=(c == NC_D - 1), inc=tm)
                            prv[0] = r1
                            C.wait("V", tm, prv[1])
                            vop("tensor_tensor", PR[:, 0, :], Q1[:], K0[:], ALU.mult)
                            p2 = vop("tensor_tensor", PR[:, 1, :], Q1[:], K1[:], ALU.mult, inc=vm)
                            C.wait("T", vm, p2)
                            mm(pb[5][0:16, :], INDD[:, c, :], PR[:, 0, :],
                               start=(c == 0), stop=(c == NC_D - 1))
                            r2 = mm(pb[6][0:16, :], INDD[:, c, :], PR[:, 1, :],
                                    start=(c == 0), stop=(c == NC_D - 1), inc=tm)
                            prv[1] = r2
                            prod_v = p2
                        dots_done = tm.n
                        C.wait("V", tm, dots_done)
                        for k in range(4):
                            vop("tensor_copy", DOTS[:, k, :], pb[2 + 1 + k][0:16, :])
                        vop("scalar_tensor_tensor", DEN[0:16, 0, :], DOTS[:, 1, :], 0.70710678,
                            DOTS[:, 0, :], ALU.mult, ALU.add)
                        vop("scalar_tensor_tensor", DEN[0:16, 1, :], DOTS[:, 2, :], 0.70710678,
                            DOTS[:, 3, :], ALU.mult, ALU.add)
                        vop("tensor_scalar", DEN[0:16, 0, :], DEN[0:16, 0, :], 1e-6, None, ALU.max)
                        vop("tensor_scalar", DEN[0:16, 1, :], DEN[0:16, 1, :], 1e-6, None, ALU.max)
                        def _rcp(e, o, i):
                            with nc.allow_low_precision(reason="bf16 1/z within tolerance"):
                                e.reciprocal(o, i)
                        C.emit("V", lambda e, o=T1_[0:16, 0:TB], i=DEN[0:16, 0, :]: _rcp(e, o, i))
                        C.emit("V", lambda e, o=T2_[0:16, 0:TB], i=DEN[0:16, 1, :]: _rcp(e, o, i))
                        vop("tensor_tensor", COEF[:, 0, :], DOTS[:, 0, :], T1_[0:16, 0:TB], ALU.mult)
                        vop("scalar_tensor_tensor", COEF[:, 1, :], DOTS[:, 1, :], 0.70710678,
                            T1_[0:16, 0:TB], ALU.mult, ALU.mult)
                        vop("scalar_tensor_tensor", COEF[:, 2, :], DOTS[:, 2, :], 0.70710678,
                            T2_[0:16, 0:TB], ALU.mult, ALU.mult)
                        cf = vop("tensor_tensor", COEF[:, 3, :], DOTS[:, 3, :], T2_[0:16, 0:TB],
                                 ALU.mult, inc=vm)
                        C.wait("T", vm, cf)
                        for c in range(NC_D):
                            if att_v:
                                C.wait("S", vm, att_v)
                            for di, (dst, cs) in enumerate([(V0, cs0), (V1, cs1)]):
                                C.wait("T", sm, vev[di])
                                for kc in range(NC_D):
                                    r = mm(pb[di][:, :], WV_[:, kc, 128 * c:128 * c + 128],
                                           XN[:, kc, cs], start=(kc == 0), stop=(kc == NC_D - 1),
                                           inc=tm if kc == NC_D - 1 else None)
                                C.wait("S", tm, r)
                                vev[di] = act("S", dst[:], pb[di][:, :], AF.Identity,
                                              bias=bias_ap("vb", c), inc=sm)
                            C.wait("T", vm, att_v)
                            mm(pb[2][:, :], INDB[:, c, :], COEF[:, 0, :], start=True, stop=True)
                            mm(pb[3][:, :], INDB[:, c, :], COEF[:, 1, :], start=True, stop=True)
                            mm(pb[4][:, :], INDB[:, c, :], COEF[:, 2, :], start=True, stop=True)
                            bc4 = mm(pb[5][:, :], INDB[:, c, :], COEF[:, 3, :],
                                     start=True, stop=True, inc=tm)
                            C.wait("V", tm, bc4)
                            C.wait("V", sm, vev[0])
                            C.wait("V", sm, vev[1])
                            vop("tensor_tensor", T1_[:], V0[:], pb[2][:, :], ALU.mult)
                            vop("tensor_tensor", T2_[:], V1[:], pb[3][:, :], ALU.mult)
                            vop("tensor_tensor", T1_[:], T1_[:], T2_[:], ALU.add)
                            vop("tensor_tensor", QN[:, c, cs0], T1_[:], QN[:, c, cs0], ALU.add)
                            vop("tensor_tensor", T1_[:], V0[:], pb[4][:, :], ALU.mult)
                            vop("tensor_tensor", T2_[:], V1[:], pb[5][:, :], ALU.mult)
                            vop("tensor_tensor", T1_[:], T1_[:], T2_[:], ALU.add)
                            att_v = vop("tensor_tensor", QN[:, c, cs1], T1_[:], QN[:, c, cs1],
                                        ALU.add, inc=vm)
                    barrier()
            ln2_v = emit_ln(QN, False, XN, affine=("g2", "b2"), eps=1)
            with ExitStack() as os_:
                WO_ = sbuf("WOp", [128, 8, 1024], BF16, stack=os_)
                A_sb = sbuf("A_sb2", [4, 16], BF16, stack=os_)
                S_sb = sbuf("S_sb2", [4, 4096], BF16, stack=os_)
                Wtmp = sbuf("Wtmp2", [16, 4096], BF16, stack=os_)
                wo_done = build_phm_w(WO_, "L1a_oA", "L1a_oS", (A_sb, S_sb, Wtmp))
                C.wait("T", ld, wo_done)
                C.wait("T", vm, vm.n)
                C.wait("T", sm, sm.n)
                pfree = {0: 0, 1: 0}
                for mc in range(NC_D):
                    for tb in range(NTB):
                        ts = slice(tb * TB, (tb + 1) * TB)
                        bk = tb % 2
                        C.wait("T", vm, pfree[bk])
                        for kc in range(NC_D):
                            r = mm(pb[bk][:, :], WO_[:, kc, 128 * mc:128 * mc + 128],
                                   XN[:, kc, ts], start=(kc == 0), stop=(kc == NC_D - 1),
                                   inc=tm if kc == NC_D - 1 else None)
                        C.wait("V", tm, r)
                        vop("scalar_tensor_tensor", XB[:, 0, :],
                            pb[bk][:, :], bias_ap("ob", mc), XN[:, mc, ts], ALU.add, ALU.add)
                        pfree[bk] = vop("tensor_tensor", X[:, mc, ts], X[:, mc, ts],
                                        XB[:, 0, :], ALU.add, inc=vm)
            return vm.n

        C.wait("S", ld, x_ld)
        C.wait("V", ld, consts_ld)
        C.wait("T", ld, consts_ld)

        if STAGE == "load":
            pass
        elif STAGE == "ln":
            with ExitStack() as fstack:
                XN = sbuf("XN_f1", [128, NC_D, T], BF16, stack=fstack)
                xn_v = emit_ln(X, True, XN)
                C.wait("V", vm, xn_v)
                for c in range(NC_D):
                    vop("tensor_copy", X[:, c, :], XN[:, c, :], inc=vm)
                barrier()
        else:
            if "l0ffn1" in phases:
                with ExitStack() as fstack:
                    XN = sbuf("XN_f1", [128, NC_D, T], BF16, stack=fstack)
                    xn_v = emit_ln(X, True, XN)
                    x_v = l0ffn(XN, xn_v, fstack)
                    barrier()
            if "l0attn" in phases and STAGE is None:
                with ExitStack() as astack:
                    l0attn(astack)
                    barrier()
            if "l0ffn2" in phases and STAGE is None:
                with ExitStack() as fstack:
                    XN = sbuf("XN_f2", [128, NC_D, T], BF16, stack=fstack)
                    xn_v = emit_ln(X, True, XN)
                    x_v = l0ffn(XN, xn_v, fstack)
                    barrier()
            if "l1ffn1" in phases and STAGE in (None, "l1g", "l1n", "l1n0", "l1nc"):
                with ExitStack() as fstack:
                    XN = sbuf("XN_g1", [128, NC_D, T], BF16, stack=fstack)
                    xn_v = emit_ln(X, True, XN)
                    x_v = l1ffn(XN, xn_v, fstack)
                    barrier()
            if "l1attn" in phases and STAGE is None:
                with ExitStack() as astack:
                    l1attn(astack)
                    barrier()
            if "l1ffn2" in phases and STAGE is None:
                with ExitStack() as fstack:
                    XN = sbuf("XN_g2", [128, NC_D, T], BF16, stack=fstack)
                    xn_v = emit_ln(X, True, XN)
                    x_v = l1ffn(XN, xn_v, fstack)
                    barrier()

        C.wait("Y", vm, vm.n)
        st_t = dma("Y", out_t.rearrange("(c p) t -> p c t", p=128),
                   X[:], st)
        C.wait("Y", st, st_t)

        with nc.Block() as block:
            @block.tensor
            def _(t):
                for f in C.streams["T"]:
                    f(t)

            @block.vector
            def _(v):
                for f in C.streams["V"]:
                    f(v)

            @block.scalar
            def _(s):
                for f in C.streams["S"]:
                    f(s)

            @block.gpsimd
            def _(g):
                for f in C.streams["G"]:
                    f(g)

            @block.sync
            def _(y):
                for f in C.streams["Y"]:
                    f(y)

    return nc


def make_inputs(core, inp):
    i = core
    x = np.asarray(inp["L0_attn_x"])
    xs = x[:, 1024 * i:1024 * (i + 1), :]
    xT = np.ascontiguousarray(xs.reshape(T, D).T)
    d = {"xT": xT,
         "ones_col": np.ones((128, 1), np.float32),
         "ones_row": np.ones((1, 128), np.float32),
         "idn8": np.eye(8, dtype=np.float32),
         "idn128": np.eye(128, dtype=np.float32)}
    g = lambda k: np.ascontiguousarray(np.asarray(inp[k], dtype=np.float32))
    bv = lambda k: np.ascontiguousarray(np.asarray(inp[k], dtype=np.float32).reshape(-1, 128).T)
    d["L0f_gate_w"] = g("L0_ffn_gate_w")
    d["L0f_sw_gate"] = g("L0_ffn_sw_gate")
    d["L0f_sw_up"] = g("L0_ffn_sw_up")
    d["L0f_sw_down"] = g("L0_ffn_sw_down")
    d["L0f_ew_gate"] = g("L0_ffn_ew_gate")
    d["L0f_ew_up"] = g("L0_ffn_ew_up")
    d["L0f_ew_down"] = g("L0_ffn_ew_down")
    d["L0a_wq_dT"] = np.ascontiguousarray(g("L0_attn_wq_d").T)
    d["L0a_bq_d"] = bv("L0_attn_bq_d")
    d["L0a_w_qkT"] = np.ascontiguousarray(g("L0_attn_w_qk").T)
    d["L0a_b_qk"] = np.ascontiguousarray(g("L0_attn_b_qk").reshape(16, 64).T)
    d["L0a_wkv_dT"] = np.ascontiguousarray(g("L0_attn_wkv_d").T)
    d["L0a_bkv_d"] = np.ascontiguousarray(g("L0_attn_bkv_d").reshape(64, 1))
    d["L0a_wv_uT"] = np.ascontiguousarray(g("L0_attn_wv_u").T)
    d["L0a_bv_u"] = np.ascontiguousarray(g("L0_attn_bv_u").reshape(1, D))
    d["L0a_w_projT"] = np.ascontiguousarray(g("L0_attn_w_proj").T)
    d["L0a_b_proj"] = bv("L0_attn_b_proj")
    d["L1f_ln_gamma"] = bv("L1_ffn_ln_gamma")
    d["L1f_ln_beta"] = bv("L1_ffn_ln_beta")
    d["L1f_router_wT"] = np.ascontiguousarray(g("L1_ffn_router_w").T)
    d["L1f_gate_upT"] = np.ascontiguousarray(g("L1_ffn_gate_up_w").transpose(0, 2, 1))
    d["L1f_downT"] = np.ascontiguousarray(g("L1_ffn_down_w").transpose(0, 2, 1))
    d["L1f_sh_gate_upT"] = np.ascontiguousarray(g("L1_ffn_shared_gate_up_w").T)
    d["L1f_sh_downT"] = np.ascontiguousarray(g("L1_ffn_shared_down_w").T)
    for p in "qkvo":
        d[f"L1a_{p}A"] = np.ascontiguousarray(g(f"L1_attn_{p}A").reshape(4, 16))
        d[f"L1a_{p}S"] = np.ascontiguousarray(g(f"L1_attn_{p}S").reshape(4, 65536))
        d[f"L1a_{p}b"] = bv(f"L1_attn_{p}b")
    for nm in ["g1", "b1", "g2", "b2"]:
        d[f"L1a_{nm}"] = bv(f"L1_attn_{nm}")
    ind_dots = np.zeros((8, 128, 16), np.float32)
    ind_bc = np.zeros((8, 16, 128), np.float32)
    for c in range(8):
        ind_dots[c, 0:64, 2 * c] = 1.0
        ind_dots[c, 64:128, 2 * c + 1] = 1.0
        ind_bc[c, 2 * c, 0:64] = 1.0
        ind_bc[c, 2 * c + 1, 64:128] = 1.0
    d["L1a_ind_dots"] = ind_dots
    d["L1a_ind_bc"] = ind_bc
    d["L1a_ind_bc7"] = ind_bc * np.float32(np.cos(np.pi / 4))
    return d


def kernel(**inputs):
    nc = build_program(PHASES)
    in_maps = [make_inputs(i, inputs) for i in range(NCORES)]
    res = run_bass_kernel_spmd(nc, in_maps, core_ids=list(range(NCORES)))
    outs = [res.results[i]["out"] for i in range(NCORES)]
    full = np.stack([o.T.reshape(2, 1024, 1024) for o in outs], axis=1)
    return np.ascontiguousarray(full.reshape(2, 8192, 1024))


# version 05 (final)
# speedup vs baseline: 1.2272x; 1.0438x over previous
import numpy as np
from contextlib import ExitStack

import concourse.bass as bass
import concourse.mybir as mybir
from concourse.bass_utils import run_bass_kernel_spmd

F32 = mybir.dt.float32
BF16 = mybir.dt.bfloat16
AF = mybir.ActivationFunctionType
ALU = mybir.AluOpType
AX = mybir.AxisListType

NCORES = 8
D = 1024
T = 2048
NC_D = 8
NTB = 4
TB = 512

PHASES = ("l0ffn1", "l0attn", "l0ffn2", "l1ffn1", "l1attn", "l1ffn2")
L1STEPS = None
STAGE = None


class Sem:
    def __init__(self, h):
        self.h = h
        self.n = 0

    def inc(self, amt=1):
        self.n += amt
        return self.n


class Ctx:
    def __init__(self, nc):
        self.nc = nc
        self.streams = {e: [] for e in "TVSGY"}

    def emit(self, eng, fn):
        self.streams[eng].append(fn)

    def wait(self, eng, sem, val):
        assert val <= sem.n, f"wait target {val} > emitted {sem.n} on {eng}"
        if val > 0:
            self.emit(eng, lambda e, s=sem.h, v=val: e.wait_ge(s, v))


def build_program(phases=PHASES):
    nc = bass.Bass("TRN2", target_bir_lowering=False, debug=False)

    dr = {}

    def din(name, shape, dtype=F32):
        dr[name] = nc.dram_tensor(name, list(shape), dtype, kind="ExternalInput").ap()
        return dr[name]

    din("xT", [D, T])
    din("ones_col", [128, 1])
    din("ones_row", [1, 128])
    din("idn8", [8, 8])
    din("idn128", [128, 128])
    din("L0f_gate_w", [D, 8])
    din("L0f_sw_gate", [D, D])
    din("L0f_sw_up", [D, D])
    din("L0f_sw_down", [D, D])
    din("L0f_ew_gate", [8, D, D])
    din("L0f_ew_up", [8, D, D])
    din("L0f_ew_down", [8, D, D])
    din("L0a_wq_dT", [D, 512])
    din("L0a_bq_d", [128, 4])
    din("L0a_w_qkT", [512, D])
    din("L0a_b_qk", [64, 16])
    din("L0a_wkv_dT", [D, 64])
    din("L0a_bkv_d", [64, 1])
    din("L0a_wv_uT", [64, D])
    din("L0a_bv_u", [1, D])
    din("L0a_w_projT", [D, D])
    din("L0a_b_proj", [128, 8])
    din("L1f_ln_gamma", [128, 8])
    din("L1f_ln_beta", [128, 8])
    din("L1f_router_wT", [D, 8])
    din("L1f_gate_upT", [8, D, 6144])
    din("L1f_downT", [8, 3072, D])
    din("L1f_sh_gate_upT", [D, 2048])
    din("L1f_sh_downT", [1024, D])
    for p in "qkvo":
        din(f"L1a_{p}A", [4, 16])
        din(f"L1a_{p}S", [4, 65536])
        din(f"L1a_{p}b", [128, 8])
    for nm in ["g1", "b1", "g2", "b2"]:
        din(f"L1a_{nm}", [128, 8])
    din("L1a_ind_dots", [8, 128, 16])
    din("L1a_ind_bc", [8, 16, 128])
    din("L1a_ind_bc7", [8, 16, 128])

    out_t = nc.dram_tensor("out", [D, T], F32, kind="ExternalOutput").ap()
    x_spill = nc.dram_tensor("x_spill", [D, T], F32).ap()
    ckv_in = nc.dram_tensor("ckv_in", [64, T], BF16).ap()
    wscr = nc.dram_tensor("wscr", [2, 16, 4096], BF16).ap()
    ckv_out = nc.dram_tensor("ckv_out", [NCORES * 64, T], BF16).ap()

    es = ExitStack()
    with es:
        _name_ctr = {}

        def sbuf(name, shape, dtype, stack=es):
            n = _name_ctr.get(name, 0)
            _name_ctr[name] = n + 1
            if n:
                name = f"{name}_{n}"
            return stack.enter_context(nc.sbuf_tensor(name, list(shape), dtype))

        X = sbuf("X", [128, NC_D, T], F32)
        XB = sbuf("XB", [128, 2, TB], BF16)
        SQ = sbuf("SQ", [128, 2, TB], BF16)
        ROWS = sbuf("ROWS", [1, 4, TB], F32)
        ONESC = sbuf("ONESC", [128, 1], BF16)
        ONESR_B = sbuf("ONESR_B", [1, 128], BF16)
        ONESR_F = sbuf("ONESR_F", [1, 128], F32)
        IDN8 = sbuf("IDN8", [8, 8], F32)
        IDN128 = sbuf("IDN128", [128, 128], F32)
        BIASV = sbuf("BIASV", [128, 136], F32)
        GW8 = sbuf("GW8", [128, 8, 8], BF16)
        RW8 = sbuf("RW8", [128, 8, 8], BF16)
        EPSC = sbuf("EPSC", [1, 2], F32)

        pb = [es.enter_context(nc.psum_tensor(f"pb{i}", [128, 512], F32)) for i in range(7)]

        sems = {}
        for name in ["ld", "wd", "tm", "vm", "sm", "gm", "cc", "st"]:
            sems[name] = Sem(es.enter_context(nc.semaphore(name)))
        ld, wd, tm, vm, sm = sems["ld"], sems["wd"], sems["tm"], sems["vm"], sems["sm"]
        gm, ccs, st = sems["gm"], sems["cc"], sems["st"]

        C = Ctx(nc)

        def dma(stream, out_ap, in_ap, sem):
            tgt = sem.inc(16)
            C.emit(stream, lambda e, o=out_ap, i=in_ap, s=sem.h:
                   e.dma_start(out=o, in_=i).then_inc(s, 16))
            return tgt

        def mm(out_ap, lhsT_ap, rhs_ap, start, stop, inc=None):
            if inc is not None:
                tgt = inc.inc()
                C.emit("T", lambda e, o=out_ap, l=lhsT_ap, r=rhs_ap, a=start, b=stop, s=inc.h:
                       e.matmul(o, l, r, start=a, stop=b).then_inc(s, 1))
                return tgt
            C.emit("T", lambda e, o=out_ap, l=lhsT_ap, r=rhs_ap, a=start, b=stop:
                   e.matmul(o, l, r, start=a, stop=b))

        def trans(out_ap, in_ap, ident, inc=None):
            if inc is not None:
                tgt = inc.inc()
                C.emit("T", lambda e, o=out_ap, i=in_ap, d=ident, s=inc.h:
                       e.transpose(o, i, d).then_inc(s, 1))
                return tgt
            C.emit("T", lambda e, o=out_ap, i=in_ap, d=ident: e.transpose(o, i, d))

        def act(eng, out_ap, in_ap, func, bias=0.0, scale=1.0, accum=None, inc=None):
            if inc is not None:
                tgt = inc.inc()
                C.emit(eng, lambda e, o=out_ap, i=in_ap, fu=func, b=bias, sc=scale, ac=accum, s=inc.h:
                       e.activation(o, i, fu, bias=b, scale=sc, accum_out=ac).then_inc(s, 1))
                return tgt
            C.emit(eng, lambda e, o=out_ap, i=in_ap, fu=func, b=bias, sc=scale, ac=accum:
                   e.activation(o, i, fu, bias=b, scale=sc, accum_out=ac))

        def vop(name, *args, inc=None, **kw):
            if inc is not None:
                tgt = inc.inc()
                C.emit("V", lambda e, a=args, k=kw, s=inc.h:
                       getattr(e, name)(*a, **k).then_inc(s, 1))
                return tgt
            C.emit("V", lambda e, a=args, k=kw: getattr(e, name)(*a, **k))

        def barrier():
            for eng in "TVSGY":
                for s in (tm, vm, sm, wd, ld, gm, st):
                    if s.n:
                        C.wait(eng, s, s.n)

        C.emit("V", lambda e: e.memset(EPSC[0:1, 0:1], 1e-5))
        C.emit("V", lambda e: e.memset(EPSC[0:1, 1:2], 1e-6))
        dma("G", ONESC[:], dr["ones_col"][:], wd)
        dma("G", ONESR_B[:], dr["ones_row"][:], wd)
        dma("G", GW8[:],
            dr["L0f_gate_w"].rearrange("(c p) e -> p c e", p=128), wd)
        dma("G", RW8[:],
            dr["L1f_router_wT"].rearrange("(c p) e -> p c e", p=128), wd)
        consts_wd = wd.n
        dma("Y", ONESR_F[:], dr["ones_row"][:], ld)
        dma("Y", IDN8[:], dr["idn8"][:], ld)
        dma("Y", IDN128[:], dr["idn128"][:], ld)
        x_ld = dma("Y", X[:],
                   dr["xT"].rearrange("(c p) t -> p c t", p=128), ld)

        bias_slots = {}
        bias_next = [0]

        def load_bias(name, dram_name, nchunk):
            base = bias_next[0]
            bias_next[0] += nchunk
            bias_slots[name] = base
            return dma("Y", BIASV[0:dr[dram_name].shape[0], base:base + nchunk],
                       dr[dram_name][:], ld)

        def bias_ap(name, c):
            b = bias_slots[name]
            return BIASV[:, b + c:b + c + 1]

        for nm, dnm, nch in [("qb", "L1a_qb", 8), ("kb", "L1a_kb", 8),
                         ("vb", "L1a_vb", 8), ("ob", "L1a_ob", 8),
                         ("ln_g", "L1f_ln_gamma", 8), ("ln_b", "L1f_ln_beta", 8),
                             ("b_proj", "L0a_b_proj", 8), ("bq_d", "L0a_bq_d", 4),
                             ("b_qk", "L0a_b_qk", 16), ("bkv_d", "L0a_bkv_d", 1),
                             ("g1", "L1a_g1", 8), ("b1", "L1a_b1", 8),
                             ("g2", "L1a_g2", 8), ("b2", "L1a_b2", 8)]:
            load_bias(nm, dnm, nch)
        bias_slots["bqk8"] = bias_next[0]
        bias_next[0] += 16
        C.wait("V", ld, ld.n)
        vop("tensor_scalar", BIASV[0:64, bias_slots["bqk8"]:bias_slots["bqk8"] + 16],
            BIASV[0:64, bias_slots["b_qk"]:bias_slots["b_qk"] + 16], 0.125, None, ALU.mult)
        consts_ld = ld.n

        def emit_ln(src_sb, src_is_f32, dst_sb, affine=None, eps=0):
            inv = 1.0 / D
            prev_lv = 0
            for tb in range(NTB):
                ts = slice(tb * TB, (tb + 1) * TB)
                mm_done = [0] * NC_D
                for c in range(NC_D):
                    if c >= 2:
                        C.wait("S", tm, mm_done[c - 2])
                    if src_is_f32:
                        act("S", XB[:, c % 2, :], src_sb[:, c, ts], AF.Identity)
                        stat = XB[:, c % 2, :]
                    else:
                        stat = src_sb[:, c, ts]
                    cast_t = act("S", SQ[:, c % 2, :], src_sb[:, c, ts], AF.Square, inc=sm)
                    C.wait("T", sm, cast_t)
                    mm(pb[5][0:1, :], ONESC[:], stat, start=(c == 0), stop=(c == NC_D - 1))
                    mm_done[c] = mm(pb[6][0:1, :], ONESC[:], SQ[:, c % 2, :],
                                    start=(c == 0), stop=(c == NC_D - 1), inc=tm)
                sums_done = tm.n
                C.wait("V", tm, sums_done)
                vop("tensor_scalar", ROWS[0:1, 0, :], pb[5][0:1, :], inv, None, ALU.mult)
                vop("tensor_scalar", ROWS[0:1, 1, :], pb[6][0:1, :], inv, None, ALU.mult)
                vop("tensor_tensor", ROWS[0:1, 2, :], ROWS[0:1, 0, :], ROWS[0:1, 0, :], ALU.mult)
                v_done = vop("tensor_tensor", ROWS[0:1, 1, :], ROWS[0:1, 1, :],
                             ROWS[0:1, 2, :], ALU.subtract, inc=vm)
                C.wait("S", vm, v_done)
                act("S", ROWS[0:1, 2, :], ROWS[0:1, 1, :], AF.Ln,
                    bias=EPSC[0:1, eps:eps + 1], scale=1.0)
                rstd_done = act("S", ROWS[0:1, 3, :], ROWS[0:1, 2, :], AF.Exp,
                                bias=0.0, scale=-0.5, inc=sm)
                C.wait("T", sm, rstd_done)
                mm(pb[5][:, :], ONESR_F[:], ROWS[0:1, 0, :], start=True, stop=True)
                bc_done = mm(pb[6][:, :], ONESR_F[:], ROWS[0:1, 3, :], start=True,
                             stop=True, inc=tm)
                C.wait("V", tm, bc_done)
                aff_s = 0
                for c in range(NC_D):
                    last = (c == NC_D - 1)
                    dst = dst_sb[:, c, ts]
                    vop("tensor_tensor", dst, src_sb[:, c, ts], pb[5][:, :], ALU.subtract)
                    if affine is None:
                        lv = vop("tensor_tensor", dst, dst, pb[6][:, :], ALU.mult,
                                 inc=vm if last else None)
                    else:
                        g_nm, b_nm = affine
                        lv = vop("tensor_tensor", dst, dst, pb[6][:, :], ALU.mult, inc=vm)
                        C.wait("S", vm, lv)
                        aff_s = act("S", dst, dst, AF.Identity,
                                    bias=bias_ap(b_nm, c), scale=bias_ap(g_nm, c),
                                    inc=sm if last else None)
                C.wait("T", vm, lv)
                if affine is not None:
                    C.wait("T", sm, aff_s)
                    lv = vm.n
                prev_lv = lv
            return vm.n

        def gating_16tiles(logits_src_fn, WT_dst, gates_mode, half_scale=True):
            GLOG = gating_16tiles.GLOG
            GTOK = gating_16tiles.GTOK
            GWRK = gating_16tiles.GWRK
            GS1 = gating_16tiles.GS1
            GS2 = gating_16tiles.GS2
            for tb in range(NTB):
                if tb > 0:
                    C.wait("T", sm, sm.n)
                r = logits_src_fn(tb)
                C.wait("S", tm, r)
                act("S", GLOG[:, tb * TB:(tb + 1) * TB], pb[4][0:8, :], AF.Identity, inc=sm)
            if STAGE == "g1":
                C.wait("V", sm, sm.n)
                vop("tensor_copy", X[0:8, 0, :], GLOG[:], inc=vm)
                return sm.n
            C.wait("T", sm, sm.n)
            for i in range(16):
                trans(pb[4][:, 8 * i:8 * i + 8], GLOG[0:8, i * 128:(i + 1) * 128], IDN8[:],
                      inc=tm if i == 15 else None)
            C.wait("V", tm, tm.n)
            vop("tensor_copy", GTOK[:].rearrange("p a b -> p (a b)"), pb[4][:, 0:128], inc=vm)
            if STAGE == "g2":
                vop("tensor_copy", X[:, 0, 0:128], GTOK[:].rearrange("p a b -> p (a b)"), inc=vm)
                return sm.n
            G_ = lambda i: GTOK[:, i, :]
            W_ = lambda i: GWRK[:, i, :]
            M1 = lambda i: GS1[:, i:i + 1]
            M2 = lambda i: GS2[:, i:i + 1]
            M3 = lambda i: gating_16tiles.GS3[:, i:i + 1]
            for i in range(16):
                vop("tensor_reduce", M2(i), G_(i), AX.X, ALU.max, negate=True,
                    inc=vm if i == 15 else None)
            C.wait("S", vm, vm.n)
            for i in range(16):
                act("S", G_(i), G_(i), AF.Exp, bias=M2(i), scale=1.0, accum=M1(i),
                    inc=sm if i == 15 else None)
            C.wait("V", sm, sm.n)
            for i in range(16):
                vop("reciprocal", M3(i), M1(i))
            for i in range(16):
                vop("tensor_scalar", G_(i), G_(i), M3(i), None, ALU.mult)
            for i in range(16):
                vop("tensor_reduce", M1(i), G_(i), AX.X, ALU.max)
            for i in range(16):
                vop("tensor_scalar", W_(i), G_(i), M1(i), None, ALU.is_ge)
            for i in range(16):
                vop("scalar_tensor_tensor", W_(i), W_(i), -1e30, G_(i), ALU.mult, ALU.add)
            for i in range(16):
                vop("tensor_reduce", M2(i), W_(i), AX.X, ALU.max)
            for i in range(16):
                vop("tensor_scalar", W_(i), G_(i), M2(i), None, ALU.is_ge)
            sc2 = 0.5 if half_scale else 1.0
            if gates_mode == "l0":
                for i in range(16):
                    vop("tensor_tensor", W_(i), W_(i), G_(i), ALU.mult)
                for i in range(16):
                    vop("tensor_reduce", M2(i), W_(i), AX.X, ALU.add)
                for i in range(16):
                    vop("tensor_scalar", M3(i), M2(i), 1e-20, None, ALU.add)
                for i in range(16):
                    vop("reciprocal", M1(i), M3(i))
                for i in range(16):
                    vop("tensor_scalar", W_(i), W_(i), M1(i), sc2, ALU.mult, ALU.mult,
                        inc=vm if i == 15 else None)
            else:
                for i in range(16):
                    vop("tensor_scalar", M3(i), M1(i), -1.0, None, ALU.mult,
                        inc=vm if i == 15 else None)
                C.wait("S", vm, vm.n)
                for i in range(16):
                    act("S", G_(i), G_(i), AF.Exp, bias=M3(i), scale=1.0,
                        inc=sm if i == 15 else None)
                C.wait("V", sm, sm.n)
                for i in range(16):
                    vop("tensor_tensor", W_(i), W_(i), G_(i), ALU.mult)
                for i in range(16):
                    vop("tensor_reduce", M2(i), W_(i), AX.X, ALU.add)
                for i in range(16):
                    vop("reciprocal", M1(i), M2(i))
                for i in range(16):
                    vop("tensor_scalar", W_(i), W_(i), M1(i), sc2, ALU.mult, ALU.mult,
                        inc=vm if i == 15 else None)
            gw_done = vm.n
            if STAGE == "g3":
                vop("tensor_copy", X[:, 0, 0:128], GWRK[:].rearrange("p a b -> p (a b)"), inc=vm)
                return sm.n
            C.wait("T", vm, gw_done)
            wt_evac = [0] * 16
            for i in range(16):
                bk = i % 4
                if i >= 4:
                    C.wait("T", sm, wt_evac[i - 4])
                t4 = trans(pb[bk][0:8, 0:128],
                           gating_16tiles.GWRK[:, i, :], IDN128[:], inc=tm)
                C.wait("S", tm, t4)
                wt_evac[i] = act("S", WT_dst[0:8, 128 * i:128 * (i + 1)],
                                 pb[bk][0:8, 0:128], AF.Identity, inc=sm)
            return sm.n

        def l0ffn(XN, xn_v, fstack):
            GLOG = sbuf("GLOG", [8, T], F32, stack=fstack)
            WT = sbuf("WT", [8, T], BF16, stack=fstack)
            GTOK = sbuf("GTOK", [128, 16, 8], F32, stack=fstack)
            GWRK = sbuf("GWRK", [128, 16, 8], F32, stack=fstack)
            GS1 = sbuf("GS1", [128, 16], F32, stack=fstack)
            GS2 = sbuf("GS2", [128, 16], F32, stack=fstack)
            GS3 = sbuf("GS3", [128, 16], F32, stack=fstack)
            WROW = sbuf("WROW", [1, T], BF16, stack=fstack)
            WPOOL = sbuf("WPOOL", [128, 3, 8, 1024], BF16, stack=fstack)
            SG = sbuf("SG", [128, TB], BF16, stack=fstack)
            UU = sbuf("UU", [128, TB], BF16, stack=fstack)
            MID = sbuf("MID", [128, 8, TB], BF16, stack=fstack)
            gating_16tiles.GLOG = GLOG
            gating_16tiles.GTOK = GTOK
            gating_16tiles.GWRK = GWRK
            gating_16tiles.GS1 = GS1
            gating_16tiles.GS2 = GS2
            gating_16tiles.GS3 = GS3

            C.wait("T", wd, consts_wd)
            C.wait("T", vm, xn_v)

            def logits(tb):
                ts = slice(tb * TB, (tb + 1) * TB)
                for c in range(NC_D):
                    r = mm(pb[4][0:8, :], GW8[:, c, :], XN[:, c, ts],
                           start=(c == 0), stop=(c == NC_D - 1),
                           inc=tm if c == NC_D - 1 else None)
                return r

            wt_done = gating_16tiles(logits, WT, "l0")
            C.wait("T", sm, wt_done)
            if STAGE in ("g1", "g2", "g3"):
                return vm.n
            if STAGE == "gating":
                C.wait("V", sm, wt_done)
                vop("tensor_copy", X[0:8, 0, :], WT[:], inc=vm)
                return vm.n

            ew = [dr["L0f_ew_gate"], dr["L0f_ew_up"], dr["L0f_ew_down"]]
            sw = [dr["L0f_sw_gate"], dr["L0f_sw_up"], dr["L0f_sw_down"]]

            def wsrc(kind, e):
                a = sw[kind] if e == 8 else ew[kind][e]
                return a.rearrange("(c p) m -> p c m", p=128)

            order = [8, 0, 1, 2, 3, 4, 5, 6, 7]
            wtgt = {}

            def stream(e):
                for kind in range(3):
                    wtgt[(e, kind)] = dma("G", WPOOL[:, kind, :, :],
                                          wsrc(kind, e), wd)

            bank_free = {i: 0 for i in range(7)}
            last_mid = 0
            stream(order[0])
            wrow_ld = 0
            for ei, e in enumerate(order):
                for kind in range(3):
                    C.wait("T", wd, wtgt[(e, kind)])
                if e != 8:
                    C.wait("Y", sm, wt_done)
                    if wrow_ld:
                        C.wait("Y", tm, last_wbc)
                    wrow_ld = dma("Y", WROW[:], WT[e:e + 1, :], ld)
                for tb in range(NTB):
                    ts = slice(tb * TB, (tb + 1) * TB)
                    if e != 8:
                        C.wait("T", ld, wrow_ld)
                        C.wait("T", vm, bank_free[4])
                        wbc = mm(pb[4][:, :], ONESR_B[:], WROW[0:1, ts],
                                 start=True, stop=True, inc=tm)
                        last_wbc = wbc
                        bank_free[4] = 0
                    for fc in range(NC_D):
                        C.wait("T", sm, bank_free[0])
                        for c in range(NC_D):
                            mm(pb[0][:, :], WPOOL[:, 0, c, 128 * fc:128 * fc + 128],
                               XN[:, c, ts], start=(c == 0), stop=(c == NC_D - 1),
                               inc=tm if c == NC_D - 1 else None)
                        g_done = tm.n
                        C.wait("T", vm, bank_free[1])
                        for c in range(NC_D):
                            mm(pb[1][:, :], WPOOL[:, 1, c, 128 * fc:128 * fc + 128],
                               XN[:, c, ts], start=(c == 0), stop=(c == NC_D - 1),
                               inc=tm if c == NC_D - 1 else None)
                        u_done = tm.n
                        C.wait("S", tm, g_done)
                        C.wait("S", vm, last_mid)
                        sg_t = act("S", SG[:], pb[0][:, :], AF.Silu, inc=sm)
                        bank_free[0] = sg_t
                        C.wait("V", tm, u_done)
                        vop("tensor_copy", UU[:], pb[1][:, :])
                        C.wait("V", sm, sg_t)
                        if e == 8:
                            mv = vop("scalar_tensor_tensor", MID[:, fc, :], SG[:], 0.5, UU[:],
                                     ALU.mult, ALU.mult, inc=vm)
                        else:
                            C.wait("V", tm, wbc)
                            vop("tensor_tensor", MID[:, fc, :], SG[:], UU[:], ALU.mult)
                            mv = vop("tensor_tensor", MID[:, fc, :], MID[:, fc, :],
                                     pb[4][:, :], ALU.mult, inc=vm)
                        bank_free[1] = mv
                        last_mid = mv
                        if e != 8:
                            bank_free[4] = mv
                    mid_done = vm.n
                    if tb == NTB - 1 and ei + 1 < len(order):
                        ne = order[ei + 1]
                        C.wait("G", tm, tm.n)
                        for kind in range(2):
                            wtgt[(ne, kind)] = dma("G", WPOOL[:, kind, :, :],
                                                   wsrc(kind, ne), wd)
                    C.wait("T", vm, mid_done)
                    for dc in range(NC_D):
                        db = 2 + (dc % 2)
                        C.wait("T", vm, bank_free[db])
                        for fc in range(NC_D):
                            mm(pb[db][:, :], WPOOL[:, 2, fc, 128 * dc:128 * dc + 128],
                               MID[:, fc, :], start=(fc == 0), stop=(fc == NC_D - 1),
                               inc=tm if fc == NC_D - 1 else None)
                        d_done = tm.n
                        C.wait("V", tm, d_done)
                        av = vop("tensor_tensor", X[:, dc, ts], X[:, dc, ts], pb[db][:, :],
                                 ALU.add, inc=vm)
                        bank_free[db] = av
                if ei + 1 < len(order):
                    ne = order[ei + 1]
                    C.wait("G", tm, tm.n)
                    wtgt[(ne, 2)] = dma("G", WPOOL[:, 2, :, :], wsrc(2, ne), wd)
            return vm.n

        def l0attn(astack):
            CQ = sbuf("CQ", [128, 4, T], BF16, stack=astack)
            with ExitStack() as sA:
                XN = sbuf("XN_a", [128, NC_D, T], BF16, stack=sA)
                WQ = sbuf("WQ", [128, 8, 512], BF16, stack=sA)
                WKV = sbuf("WKV", [128, 8, 64], BF16, stack=sA)
                CKVL = sbuf("CKVL", [64, T], BF16, stack=sA)
                wq_t = dma("G", WQ[:], dr["L0a_wq_dT"].rearrange("(c p) m -> p c m", p=128), wd)
                wkv_t = dma("G", WKV[:], dr["L0a_wkv_dT"].rearrange("(c p) m -> p c m", p=128), wd)
                xn_v = emit_ln(X, True, XN)
                C.wait("T", vm, xn_v)
                C.wait("T", wd, max(wq_t, wkv_t))
                bfree = {i: 0 for i in range(4)}
                for mc in range(4):
                    for tb in range(NTB):
                        ts = slice(tb * TB, (tb + 1) * TB)
                        bk = tb % 2
                        C.wait("T", vm, bfree[bk])
                        for c in range(NC_D):
                            r = mm(pb[bk][:, :], WQ[:, c, 128 * mc:128 * mc + 128],
                                   XN[:, c, ts], start=(c == 0), stop=(c == NC_D - 1),
                                   inc=tm if c == NC_D - 1 else None)
                        C.wait("V", tm, r)
                        bfree[bk] = vop("tensor_scalar", CQ[:, mc, ts], pb[bk][:, :],
                                        bias_ap("bq_d", mc), None, ALU.add, inc=vm)
                for tb in range(NTB):
                    ts = slice(tb * TB, (tb + 1) * TB)
                    bk = 2 + tb % 2
                    C.wait("T", vm, bfree[bk])
                    for c in range(NC_D):
                        r = mm(pb[bk][0:64, :], WKV[:, c, :], XN[:, c, ts],
                               start=(c == 0), stop=(c == NC_D - 1),
                               inc=tm if c == NC_D - 1 else None)
                    C.wait("V", tm, r)
                    bkv_ap = BIASV[0:64, bias_slots["bkv_d"]:bias_slots["bkv_d"] + 1]
                    bfree[bk] = vop("tensor_scalar", CKVL[:, ts], pb[bk][0:64, :],
                                    bkv_ap, None, ALU.add, inc=vm)
                ckv_done = vm.n
                C.wait("G", vm, ckv_done)
                bounce_t = dma("G", ckv_in[:, :], CKVL[:], wd)
            C.wait("G", wd, bounce_t)
            cc_t = ccs.inc()
            C.emit("G", lambda e, s=ccs.h: e.collective_compute(
                "AllGather", ALU.bypass, replica_groups=[list(range(NCORES))],
                ins=[ckv_in], outs=[ckv_out]).then_inc(s, 1))

            WQK = sbuf("WQK", [128, 4, 1024], BF16, stack=astack)
            QEXT = sbuf("QEXT", [65, 16, 1024], BF16, stack=astack)
            CKV = sbuf("CKV", [65, 8192], BF16, stack=astack)
            KM = sbuf("KM", [64, 65], BF16, stack=astack)
            KSUM = sbuf("KSUM", [64, 2], F32, stack=astack)
            WVU = sbuf("WVU", [65, 1024], BF16, stack=astack)
            YALL = sbuf("YALL", [64, 16, 512], BF16, stack=astack)
            YTMP = sbuf("YTMP", [64, 512], BF16, stack=astack)
            ZR = sbuf("ZR", [1, 512], F32, stack=astack)
            WPJ = sbuf("WPJ", [64, 16, 512], BF16, stack=astack)
            wqk_t = dma("G", WQK[:], dr["L0a_w_qkT"].rearrange("(c p) m -> p c m", p=128), wd)
            wvu_t = dma("G", WVU[0:64, :], dr["L0a_wv_uT"][:], wd)
            wvu_t = dma("G", WVU[64:65, :], dr["L0a_bv_u"][:], wd)
            C.emit("V", lambda e: e.memset(CKV[64:65, :], 1.0))
            C.emit("V", lambda e: e.memset(KM[:, :], 0.0))
            last_yv = [0]
            sguard = {i: 0 for i in range(7)}
            p3guard = {i: 0 for i in range(3)}

            for b in range(2):
                C.wait("V", tm, tm.n)
                C.wait("S", tm, tm.n)
                tbs = [2 * b, 2 * b + 1]
                C.wait("G", ccs, cc_t)
                C.wait("G", tm, tm.n)
                ckv_t = dma("G", CKV[0:64, :].rearrange("d (r t) -> d r t", r=8),
                            ckv_out.rearrange("(r d) t -> d r t", d=64)[:, :, 1024 * b:1024 * (b + 1)],
                            wd)
                C.wait("T", wd, wqk_t)
                for h in range(16):
                    for t2 in range(2):
                        ts = slice(tbs[t2] * TB, (tbs[t2] + 1) * TB)
                        qs = slice(t2 * TB, (t2 + 1) * TB)
                        bk = t2 % 2
                        C.wait("T", vm, bfree[bk])
                        for kc in range(4):
                            r = mm(pb[bk][0:64, :], WQK[:, kc, 64 * h:64 * h + 64],
                                   CQ[:, kc, ts], start=(kc == 0), stop=(kc == 3),
                                   inc=tm if kc == 3 else None)
                        C.wait("V", tm, r)
                        bfree[bk] = vop("tensor_scalar", QEXT[0:64, h, qs], pb[bk][0:64, :],
                                        0.125, BIASV[0:64, bias_slots["bqk8"] + h:
                                                      bias_slots["bqk8"] + h + 1],
                                        ALU.mult, ALU.add, inc=vm)
                C.wait("V", wd, ckv_t)
                vop("tensor_reduce", KSUM[:, 0:1], CKV[0:64, :], AX.X, ALU.add)
                vop("tensor_scalar", KSUM[:, 1:2], KSUM[:, 0:1], -1.0 / 8192.0, None, ALU.mult)
                km_v = vop("tensor_copy", KM[:, 64:65], KSUM[:, 1:2], inc=vm)
                C.wait("T", vm, vm.n)
                mrow_ev = 0
                for h in range(16):
                    for t2 in range(2):
                        qs = slice(t2 * TB, (t2 + 1) * TB)
                        if mrow_ev:
                            C.wait("T", sm, mrow_ev)
                        r = mm(pb[6][0:65, :], KM[:], QEXT[0:64, h, qs],
                               start=True, stop=True, inc=tm)
                        C.wait("S", tm, r)
                        mrow_ev = act("S", QEXT[64:65, h, qs], pb[6][64:65, :],
                                      AF.Identity, inc=sm)
                C.wait("T", sm, mrow_ev)
                for t2 in range(2):
                    qs = slice(t2 * TB, (t2 + 1) * TB)
                    ts = slice(tbs[t2] * TB, (tbs[t2] + 1) * TB)
                    with ExitStack() as sC:
                        VEXT = sbuf("VEXT", [128, 64, 65], BF16, stack=sC)
                        WVX = sbuf("WVX", [65, 65], BF16, stack=sC)
                        P3 = sbuf("P3", [128, 3, TB], BF16, stack=sC)
                        C.wait("V", wd, wvu_t)
                        for h in range(16):
                            C.wait("V", tm, tm.n)
                            vop("memset", WVX[:, :], 0.0)
                            vop("tensor_copy", WVX[:, 0:64], WVU[:, 64 * h:64 * h + 64])
                            wvx_v = vop("memset", WVX[64:65, 64:65], 1.0, inc=vm)
                            C.wait("T", vm, wvx_v)
                            vb_last = 0
                            for g0 in range(0, 64, 7):
                                gn = min(7, 64 - g0)
                                bk = 4 + (g0 // 7) % 2
                                C.wait("T", sm, sguard[bk])
                                for k in range(gn):
                                    sc = g0 + k
                                    r = mm(pb[bk][:, 65 * k:65 * k + 65],
                                           CKV[:, 128 * sc:128 * sc + 128],
                                           WVX[:], start=True, stop=True,
                                           inc=tm if k == gn - 1 else None)
                                C.wait("S", tm, r)
                                vb_last = act("S", VEXT[:, g0:g0 + gn, :]
                                              .rearrange("p a b -> p (a b)"),
                                              pb[bk][:, 0:65 * gn], AF.Identity, inc=sm)
                                sguard[bk] = vb_last
                            C.wait("T", sm, vb_last)
                            exp_of = [0] * 64
                            for sc in range(64):
                                bk = sc % 2
                                C.wait("T", sm, sguard[bk])
                                mm(pb[bk][:, :], CKV[:, 128 * sc:128 * sc + 128],
                                   QEXT[0:65, h, qs], start=True, stop=True, inc=tm)
                                sc_t = tm.n
                                if sc >= 1:
                                    C.wait("T", sm, exp_of[sc - 1])
                                    mm(pb[2][0:65, :], VEXT[:, sc - 1, :], P3[:, (sc - 1) % 3, :],
                                       start=(sc - 1 == 0), stop=False, inc=tm)
                                    p3guard[(sc - 1) % 3] = tm.n
                                C.wait("S", tm, sc_t)
                                C.wait("S", tm, p3guard[sc % 3])
                                ex = act("S", P3[:, sc % 3, :], pb[bk][:, :],
                                         AF.Exp, inc=sm)
                                exp_of[sc] = ex
                                sguard[bk] = ex
                            C.wait("T", sm, exp_of[63])
                            mm(pb[2][0:65, :], VEXT[:, 63, :], P3[:, 63 % 3, :],
                               start=False, stop=True, inc=tm)
                            p3guard[63 % 3] = tm.n
                            pv_last = tm.n
                            C.wait("V", tm, pv_last)
                            zr_v = vop("reciprocal", ZR[:], pb[2][64:65, :], inc=vm)
                            C.wait("T", vm, zr_v)
                            bc_t = mm(pb[3][0:64, :], ONESR_F[0:1, 0:64], ZR[:],
                                      start=True, stop=True, inc=tm)
                            C.wait("S", tm, pv_last)
                            C.wait("S", vm, last_yv[0])
                            yt_ev = act("S", YTMP[:], pb[2][0:64, :], AF.Identity, inc=sm)
                            C.wait("V", sm, yt_ev)
                            C.wait("V", tm, bc_t)
                            y_v = vop("tensor_tensor", YALL[0:64, h, :],
                                      YTMP[:], pb[3][0:64, :], ALU.mult, inc=vm)
                            last_yv[0] = y_v
                            C.wait("T", sm, yt_ev)
                            C.wait("T", vm, y_v)
                    C.wait("T", vm, vm.n)
                    pfree = {0: 0, 1: 0}
                    for half in range(2):
                        C.wait("G", tm, tm.n)
                        wpj_t = dma("G", WPJ[:],
                                    dr["L0a_w_projT"].rearrange("(h p) m -> p h m", p=64)
                                    [:, :, 512 * half:512 * (half + 1)], wd)
                        C.wait("T", wd, wpj_t)
                        for mc2 in range(4):
                            mc = 4 * half + mc2
                            bk = mc % 2
                            C.wait("T", vm, pfree[bk])
                            for h in range(16):
                                r = mm(pb[bk][:, :], WPJ[:, h, 128 * mc2:128 * mc2 + 128],
                                       YALL[0:64, h, :], start=(h == 0), stop=(h == 15),
                                       inc=tm if h == 15 else None)
                            C.wait("V", tm, r)
                            pfree[bk] = vop("scalar_tensor_tensor", X[:, mc, ts],
                                            pb[bk][:, :], bias_ap("b_proj", mc),
                                            X[:, mc, ts], ALU.add, ALU.add, inc=vm)
            return vm.n

        def l1ffn(XN, xn_v, fstack):
            WT = sbuf("WT1", [8, T], BF16, stack=fstack)
            WROW = sbuf("WROW1", [1, T], BF16, stack=fstack)
            C.wait("S", vm, xn_v)
            if STAGE == "l1nc":
                C.wait("V", vm, xn_v)
                for c in range(NC_D):
                    vop("tensor_copy", X[:, c, :], XN[:, c, :], inc=vm)
                return vm.n
            if STAGE == "l1n0":
                XN2 = sbuf("XN2dbg", [128, NC_D, T], BF16, stack=fstack)
                nrm_v = emit_ln(XN, False, XN2, affine=None)
                XN = XN2
            else:
                nrm_v = emit_ln(XN, False, XN, affine=("ln_g", "ln_b"))
            if STAGE in ("l1n", "l1n0"):
                C.wait("V", vm, nrm_v)
                for c in range(NC_D):
                    vop("tensor_copy", X[:, c, :], XN[:, c, :], inc=vm)
                return vm.n
            with ExitStack() as gs:
                GLOG = sbuf("GLOG1", [8, T], F32, stack=gs)
                GTOK = sbuf("GTOK1", [128, 16, 8], F32, stack=gs)
                GWRK = sbuf("GWRK1", [128, 16, 8], F32, stack=gs)
                GS1 = sbuf("GS11", [128, 16], F32, stack=gs)
                GS2 = sbuf("GS21", [128, 16], F32, stack=gs)
                GS3 = sbuf("GS31", [128, 16], F32, stack=gs)
                gating_16tiles.GLOG = GLOG
                gating_16tiles.GTOK = GTOK
                gating_16tiles.GWRK = GWRK
                gating_16tiles.GS1 = GS1
                gating_16tiles.GS2 = GS2
                gating_16tiles.GS3 = GS3
                C.wait("T", wd, consts_wd)
                C.wait("T", vm, nrm_v)

                def logits(tb):
                    ts = slice(tb * TB, (tb + 1) * TB)
                    for c in range(NC_D):
                        r = mm(pb[4][0:8, :], RW8[:, c, :], XN[:, c, ts],
                               start=(c == 0), stop=(c == NC_D - 1),
                               inc=tm if c == NC_D - 1 else None)
                    return r

                wt_done = gating_16tiles(logits, WT, "l1")
                C.wait("T", sm, wt_done)
                if STAGE == "l1g":
                    C.wait("V", sm, wt_done)
                    vop("tensor_copy", X[0:8, 0, :], WT[:], inc=vm)
                    return vm.n
            WGU = sbuf("WGU", [128, 2, 8, 1024], BF16, stack=fstack)
            WDS = sbuf("WDS", [128, 2, 8, 1024], BF16, stack=fstack)
            SG = sbuf("SG1", [128, TB], BF16, stack=fstack)
            UU = sbuf("UU1", [128, TB], BF16, stack=fstack)
            MID = sbuf("MID1", [128, 8, TB], BF16, stack=fstack)

            guT = dr["L1f_gate_upT"]
            dnT = dr["L1f_downT"]

            def stream_gu(e, sg):
                if e == 8:
                    gsrc = dr["L1f_sh_gate_upT"].rearrange("(c p) m -> p c m", p=128)
                    t1 = dma("G", WGU[:, 0, :, :], gsrc[:, :, 0:1024], wd)
                    t2 = dma("G", WGU[:, 1, :, :], gsrc[:, :, 1024:2048], wd)
                else:
                    gsrc = guT[e].rearrange("(c p) m -> p c m", p=128)
                    t1 = dma("G", WGU[:, 0, :, :], gsrc[:, :, 1024 * sg:1024 * (sg + 1)], wd)
                    t2 = dma("G", WGU[:, 1, :, :], gsrc[:, :, 3072 + 1024 * sg:3072 + 1024 * (sg + 1)], wd)
                return max(t1, t2)

            def stream_ds(e, sg, buf):
                if e == 8:
                    return dma("G", WDS[:, buf], dr["L1f_sh_downT"].rearrange("(c p) m -> p c m", p=128), wd)
                return dma("G", WDS[:, buf], dnT[e].rearrange("(c p) m -> p c m", p=128)
                           [:, 8 * sg:8 * (sg + 1), :], wd)

            steps = [(e, sg) for e in range(8) for sg in range(3)] + [(8, 0)]
            if L1STEPS is not None:
                steps = steps[:L1STEPS] if L1STEPS > 0 else [steps[-1]]
            bank_free = {i: 0 for i in range(7)}
            last_mid = 0
            C.wait("G", sm, sm.n)
            C.wait("G", vm, vm.n)
            C.wait("G", tm, tm.n)
            wgu_tgt = stream_gu(*steps[0])
            wds_tgt = stream_ds(*steps[0], 0)
            ds_last_use = [0, 0]
            wrow_ld = 0
            last_wbc = 0
            cur_e = None
            for si, (e, sg) in enumerate(steps):
                wgu_cur, wds_cur = wgu_tgt, wds_tgt
                if e != 8 and e != cur_e:
                    cur_e = e
                    C.wait("Y", sm, wt_done)
                    if wrow_ld:
                        C.wait("Y", tm, last_wbc)
                    wrow_ld = dma("Y", WROW[:], WT[e:e + 1, :], ld)
                C.wait("T", wd, max(wgu_cur, wds_cur))
                for tb in range(NTB):
                    ts = slice(tb * TB, (tb + 1) * TB)
                    if e != 8:
                        C.wait("T", ld, wrow_ld)
                        C.wait("T", vm, bank_free[4])
                        wbc = mm(pb[4][:, :], ONESR_B[:], WROW[0:1, ts],
                                 start=True, stop=True, inc=tm)
                        last_wbc = wbc
                    for fc in range(NC_D):
                        C.wait("T", sm, bank_free[0])
                        for c in range(NC_D):
                            mm(pb[0][:, :], WGU[:, 0, c, 128 * fc:128 * fc + 128],
                               XN[:, c, ts], start=(c == 0), stop=(c == NC_D - 1),
                               inc=tm if c == NC_D - 1 else None)
                        g_done = tm.n
                        C.wait("T", vm, bank_free[1])
                        for c in range(NC_D):
                            mm(pb[1][:, :], WGU[:, 1, c, 128 * fc:128 * fc + 128],
                               XN[:, c, ts], start=(c == 0), stop=(c == NC_D - 1),
                               inc=tm if c == NC_D - 1 else None)
                        u_done = tm.n
                        C.wait("S", tm, g_done)
                        C.wait("S", vm, last_mid)
                        sg_t = act("S", SG[:], pb[0][:, :], AF.Silu, inc=sm)
                        bank_free[0] = sg_t
                        C.wait("V", tm, u_done)
                        vop("tensor_copy", UU[:], pb[1][:, :])
                        C.wait("V", sm, sg_t)
                        if e == 8:
                            mv = vop("scalar_tensor_tensor", MID[:, fc, :], SG[:], 0.5, UU[:],
                                     ALU.mult, ALU.mult, inc=vm)
                        else:
                            C.wait("V", tm, wbc)
                            vop("tensor_tensor", MID[:, fc, :], SG[:], UU[:], ALU.mult)
                            mv = vop("tensor_tensor", MID[:, fc, :], MID[:, fc, :],
                                     pb[4][:, :], ALU.mult, inc=vm)
                        bank_free[1] = mv
                        last_mid = mv
                        if e != 8:
                            bank_free[4] = mv
                    mid_done = vm.n
                    if tb == NTB - 1 and si + 1 < len(steps):
                        C.wait("G", tm, tm.n)
                        wgu_tgt = stream_gu(*steps[si + 1])
                    C.wait("T", vm, mid_done)
                    for dc in range(NC_D):
                        db = 2 + (dc % 2)
                        C.wait("T", vm, bank_free[db])
                        for fc in range(NC_D):
                            mm(pb[db][:, :], WDS[:, si % 2, fc, 128 * dc:128 * dc + 128],
                               MID[:, fc, :], start=(fc == 0), stop=(fc == NC_D - 1),
                               inc=tm if fc == NC_D - 1 else None)
                        d_done = tm.n
                        C.wait("V", tm, d_done)
                        av = vop("tensor_tensor", X[:, dc, ts], X[:, dc, ts], pb[db][:, :],
                                 ALU.add, inc=vm)
                        bank_free[db] = av
                ds_last_use[si % 2] = tm.n
                if si + 1 < len(steps):
                    C.wait("G", tm, ds_last_use[(si + 1) % 2])
                    wds_tgt = stream_ds(*steps[si + 1], (si + 1) % 2)
            return vm.n


        def build_phm_w(Wsb, a_name, s_name, bscratch):
            A_sb, S_sb, Wtmp = bscratch
            C.wait("G", tm, tm.n)
            a_t = dma("G", A_sb[:], dr[a_name][:], wd)
            C.wait("T", wd, a_t)
            last_rr = build_phm_w.last_rr
            for kk in range(8):
                for half in range(2):
                    base = 8192 * kk + 4096 * half
                    C.wait("G", tm, tm.n)
                    s_t = dma("G", S_sb[:], dr[s_name][:, base:base + 4096], wd)
                    C.wait("T", wd, s_t)
                    C.wait("S", ld, last_rr)
                    ev = 0
                    for j2 in range(8):
                        bk = j2 % 2
                        if j2 >= 2:
                            C.wait("T", sm, ev - 1)
                        r = mm(pb[bk][0:16, :], A_sb[:], S_sb[:, 512 * j2:512 * (j2 + 1)],
                               start=True, stop=True, inc=tm)
                        C.wait("S", tm, r)
                        ev = act("S", Wtmp[:, 512 * j2:512 * (j2 + 1)], pb[bk][0:16, :],
                                 AF.Identity, inc=sm)
                    C.wait("Y", sm, ev)
                    kglob = 32 * kk + 16 * half
                    p0 = kglob % 128
                    kh = kglob // 128
                    par = (kk * 2 + half) % 2
                    wr = dma("Y", wscr[par], Wtmp[:], ld)
                    C.wait("Y", ld, wr)
                    for i in range(4):
                        dma("Y",
                            Wsb[p0:p0 + 16, 2 * i + kh, :].rearrange("k (j l) -> k j l", j=4),
                            wscr[par][4 * i:4 * i + 4, :].rearrange("j (k l) -> k j l", k=16),
                            ld)
                    last_rr = ld.n
            build_phm_w.last_rr = last_rr
            return ld.n

        build_phm_w.last_rr = 0

        def l1attn(astack):
            XN = sbuf("XN_a1", [128, NC_D, T], BF16, stack=astack)
            QN = sbuf("QN", [128, NC_D, T], BF16, stack=astack)
            xn_v = emit_ln(X, True, XN)
            qn_v = emit_ln(XN, False, QN, affine=("g1", "b1"), eps=1)
            with ExitStack() as w3:
                WQ_ = sbuf("WQp", [128, 8, 1024], BF16, stack=w3)
                WK_ = sbuf("WKp", [128, 8, 1024], BF16, stack=w3)
                WV_ = sbuf("WVp", [128, 8, 1024], BF16, stack=w3)
                with ExitStack() as bs:
                    A_sb = sbuf("A_sb", [4, 16], BF16, stack=bs)
                    S_sb = sbuf("S_sb", [4, 4096], BF16, stack=bs)
                    Wtmp = sbuf("Wtmp", [16, 4096], BF16, stack=bs)
                    bscratch = (A_sb, S_sb, Wtmp)
                    build_phm_w(WQ_, "L1a_qA", "L1a_qS", bscratch)
                    build_phm_w(WK_, "L1a_kA", "L1a_kS", bscratch)
                    wv_done = build_phm_w(WV_, "L1a_vA", "L1a_vS", bscratch)
                    barrier()
                with ExitStack() as ps:
                    INDD = sbuf("INDD", [128, 8, 16], BF16, stack=ps)
                    INDB = sbuf("INDB", [16, 8, 128], BF16, stack=ps)
                    C.wait("G", tm, tm.n)
                    dma("G", INDD[:], dr["L1a_ind_dots"].rearrange("c p e -> p c e"), wd)
                    i_t = dma("G", INDB[:], dr["L1a_ind_bc"].rearrange("c j p -> j c p"), wd)
                    Q0 = sbuf("Q0", [128, TB], BF16, stack=ps)
                    Q1 = sbuf("Q1", [128, TB], BF16, stack=ps)
                    K0 = sbuf("K0", [128, TB], BF16, stack=ps)
                    K1 = sbuf("K1", [128, TB], BF16, stack=ps)
                    PR = sbuf("PR", [128, 2, TB], BF16, stack=ps)
                    V0, V1 = Q0, Q1
                    T1_ = K0
                    T2_ = K1
                    DOTS = sbuf("DOTS", [16, 4, TB], BF16, stack=ps)
                    COEF = sbuf("COEF", [16, 4, TB], BF16, stack=ps)
                    DEN = XB
                    C.wait("T", vm, qn_v)
                    C.wait("T", wd, i_t)
                    ev_guard = {0: 0, 1: 0, 2: 0}
                    vev = {0: 0, 1: 0}
                    prv = [0, 0]
                    att_v = 0
                    for tbp in range(2):
                        cs0 = slice(512 * tbp, 512 * (tbp + 1))
                        cs1 = slice(1024 + 512 * tbp, 1536 + 512 * tbp)
                        for c in range(NC_D):
                            specs = [(Q0, WQ_, QN, cs0, "qb", 0), (Q1, WQ_, QN, cs1, "qb", 1),
                                     (K0, WK_, XN, cs0, "kb", 2), (K1, WK_, XN, cs1, "kb", 0)]
                            if c > 0 or tbp > 0:
                                C.wait("S", vm, prod_v)
                            for di, (dst, Wp, src, cs, bnm, bk) in enumerate(specs):
                                C.wait("T", sm, ev_guard[bk])
                                for kc in range(NC_D):
                                    r = mm(pb[bk][:, :], Wp[:, kc, 128 * c:128 * c + 128],
                                           src[:, kc, cs], start=(kc == 0), stop=(kc == NC_D - 1),
                                           inc=tm if kc == NC_D - 1 else None)
                                C.wait("S", tm, r)
                                fn = AF.Relu
                                ev_guard[bk] = act("S", dst[:], pb[bk][:, :], fn,
                                                   bias=bias_ap(bnm, c), inc=sm)
                            C.wait("V", sm, ev_guard[0])
                            C.wait("V", sm, ev_guard[1])
                            C.wait("V", sm, ev_guard[2])
                            C.wait("V", tm, prv[0])
                            vop("tensor_tensor", PR[:, 0, :], Q0[:], K0[:], ALU.mult)
                            p1 = vop("tensor_tensor", PR[:, 1, :], Q0[:], K1[:], ALU.mult, inc=vm)
                            C.wait("T", vm, p1)
                            mm(pb[3][0:16, :], INDD[:, c, :], PR[:, 0, :],
                               start=(c == 0), stop=(c == NC_D - 1))
                            r1 = mm(pb[4][0:16, :], INDD[:, c, :], PR[:, 1, :],
                                    start=(c == 0), stop=(c == NC_D - 1), inc=tm)
                            prv[0] = r1
                            C.wait("V", tm, prv[1])
                            vop("tensor_tensor", PR[:, 0, :], Q1[:], K0[:], ALU.mult)
                            p2 = vop("tensor_tensor", PR[:, 1, :], Q1[:], K1[:], ALU.mult, inc=vm)
                            C.wait("T", vm, p2)
                            mm(pb[5][0:16, :], INDD[:, c, :], PR[:, 0, :],
                               start=(c == 0), stop=(c == NC_D - 1))
                            r2 = mm(pb[6][0:16, :], INDD[:, c, :], PR[:, 1, :],
                                    start=(c == 0), stop=(c == NC_D - 1), inc=tm)
                            prv[1] = r2
                            prod_v = p2
                        dots_done = tm.n
                        C.wait("V", tm, dots_done)
                        for k in range(4):
                            vop("tensor_copy", DOTS[:, k, :], pb[2 + 1 + k][0:16, :])
                        vop("scalar_tensor_tensor", DEN[0:16, 0, :], DOTS[:, 1, :], 0.70710678,
                            DOTS[:, 0, :], ALU.mult, ALU.add)
                        vop("scalar_tensor_tensor", DEN[0:16, 1, :], DOTS[:, 2, :], 0.70710678,
                            DOTS[:, 3, :], ALU.mult, ALU.add)
                        vop("tensor_scalar", DEN[0:16, 0, :], DEN[0:16, 0, :], 1e-6, None, ALU.max)
                        vop("tensor_scalar", DEN[0:16, 1, :], DEN[0:16, 1, :], 1e-6, None, ALU.max)
                        def _rcp(e, o, i):
                            with nc.allow_low_precision(reason="bf16 1/z within tolerance"):
                                e.reciprocal(o, i)
                        C.emit("V", lambda e, o=T1_[0:16, 0:TB], i=DEN[0:16, 0, :]: _rcp(e, o, i))
                        C.emit("V", lambda e, o=T2_[0:16, 0:TB], i=DEN[0:16, 1, :]: _rcp(e, o, i))
                        vop("tensor_tensor", COEF[:, 0, :], DOTS[:, 0, :], T1_[0:16, 0:TB], ALU.mult)
                        vop("scalar_tensor_tensor", COEF[:, 1, :], DOTS[:, 1, :], 0.70710678,
                            T1_[0:16, 0:TB], ALU.mult, ALU.mult)
                        vop("scalar_tensor_tensor", COEF[:, 2, :], DOTS[:, 2, :], 0.70710678,
                            T2_[0:16, 0:TB], ALU.mult, ALU.mult)
                        cf = vop("tensor_tensor", COEF[:, 3, :], DOTS[:, 3, :], T2_[0:16, 0:TB],
                                 ALU.mult, inc=vm)
                        C.wait("T", vm, cf)
                        for c in range(NC_D):
                            if att_v:
                                C.wait("S", vm, att_v)
                            for di, (dst, cs) in enumerate([(V0, cs0), (V1, cs1)]):
                                C.wait("T", sm, vev[di])
                                for kc in range(NC_D):
                                    r = mm(pb[di][:, :], WV_[:, kc, 128 * c:128 * c + 128],
                                           XN[:, kc, cs], start=(kc == 0), stop=(kc == NC_D - 1),
                                           inc=tm if kc == NC_D - 1 else None)
                                C.wait("S", tm, r)
                                vev[di] = act("S", dst[:], pb[di][:, :], AF.Identity,
                                              bias=bias_ap("vb", c), inc=sm)
                            C.wait("T", vm, att_v)
                            mm(pb[2][:, :], INDB[:, c, :], COEF[:, 0, :], start=True, stop=True)
                            mm(pb[3][:, :], INDB[:, c, :], COEF[:, 1, :], start=True, stop=True)
                            mm(pb[4][:, :], INDB[:, c, :], COEF[:, 2, :], start=True, stop=True)
                            bc4 = mm(pb[5][:, :], INDB[:, c, :], COEF[:, 3, :],
                                     start=True, stop=True, inc=tm)
                            C.wait("V", tm, bc4)
                            C.wait("V", sm, vev[0])
                            C.wait("V", sm, vev[1])
                            vop("tensor_tensor", T1_[:], V0[:], pb[2][:, :], ALU.mult)
                            vop("tensor_tensor", T2_[:], V1[:], pb[3][:, :], ALU.mult)
                            vop("tensor_tensor", T1_[:], T1_[:], T2_[:], ALU.add)
                            vop("tensor_tensor", QN[:, c, cs0], T1_[:], QN[:, c, cs0], ALU.add)
                            vop("tensor_tensor", T1_[:], V0[:], pb[4][:, :], ALU.mult)
                            vop("tensor_tensor", T2_[:], V1[:], pb[5][:, :], ALU.mult)
                            vop("tensor_tensor", T1_[:], T1_[:], T2_[:], ALU.add)
                            att_v = vop("tensor_tensor", QN[:, c, cs1], T1_[:], QN[:, c, cs1],
                                        ALU.add, inc=vm)
                    barrier()
            ln2_v = emit_ln(QN, False, XN, affine=("g2", "b2"), eps=1)
            with ExitStack() as os_:
                WO_ = sbuf("WOp", [128, 8, 1024], BF16, stack=os_)
                A_sb = sbuf("A_sb2", [4, 16], BF16, stack=os_)
                S_sb = sbuf("S_sb2", [4, 4096], BF16, stack=os_)
                Wtmp = sbuf("Wtmp2", [16, 4096], BF16, stack=os_)
                wo_done = build_phm_w(WO_, "L1a_oA", "L1a_oS", (A_sb, S_sb, Wtmp))
                C.wait("T", ld, wo_done)
                C.wait("T", vm, vm.n)
                C.wait("T", sm, sm.n)
                pfree = {0: 0, 1: 0}
                for mc in range(NC_D):
                    for tb in range(NTB):
                        ts = slice(tb * TB, (tb + 1) * TB)
                        bk = tb % 2
                        C.wait("T", vm, pfree[bk])
                        for kc in range(NC_D):
                            r = mm(pb[bk][:, :], WO_[:, kc, 128 * mc:128 * mc + 128],
                                   XN[:, kc, ts], start=(kc == 0), stop=(kc == NC_D - 1),
                                   inc=tm if kc == NC_D - 1 else None)
                        C.wait("V", tm, r)
                        vop("scalar_tensor_tensor", XB[:, 0, :],
                            pb[bk][:, :], bias_ap("ob", mc), XN[:, mc, ts], ALU.add, ALU.add)
                        pfree[bk] = vop("tensor_tensor", X[:, mc, ts], X[:, mc, ts],
                                        XB[:, 0, :], ALU.add, inc=vm)
            return vm.n

        C.wait("S", ld, x_ld)
        C.wait("V", ld, consts_ld)
        C.wait("T", ld, consts_ld)

        if STAGE == "load":
            pass
        elif STAGE == "ln":
            with ExitStack() as fstack:
                XN = sbuf("XN_f1", [128, NC_D, T], BF16, stack=fstack)
                xn_v = emit_ln(X, True, XN)
                C.wait("V", vm, xn_v)
                for c in range(NC_D):
                    vop("tensor_copy", X[:, c, :], XN[:, c, :], inc=vm)
                barrier()
        else:
            if "l0ffn1" in phases:
                with ExitStack() as fstack:
                    XN = sbuf("XN_f1", [128, NC_D, T], BF16, stack=fstack)
                    xn_v = emit_ln(X, True, XN)
                    x_v = l0ffn(XN, xn_v, fstack)
                    barrier()
            if "l0attn" in phases and STAGE is None:
                with ExitStack() as astack:
                    l0attn(astack)
                    barrier()
            if "l0ffn2" in phases and STAGE is None:
                with ExitStack() as fstack:
                    XN = sbuf("XN_f2", [128, NC_D, T], BF16, stack=fstack)
                    xn_v = emit_ln(X, True, XN)
                    x_v = l0ffn(XN, xn_v, fstack)
                    barrier()
            if "l1ffn1" in phases and STAGE in (None, "l1g", "l1n", "l1n0", "l1nc"):
                with ExitStack() as fstack:
                    XN = sbuf("XN_g1", [128, NC_D, T], BF16, stack=fstack)
                    xn_v = emit_ln(X, True, XN)
                    x_v = l1ffn(XN, xn_v, fstack)
                    barrier()
            if "l1attn" in phases and STAGE is None:
                with ExitStack() as astack:
                    l1attn(astack)
                    barrier()
            if "l1ffn2" in phases and STAGE is None:
                with ExitStack() as fstack:
                    XN = sbuf("XN_g2", [128, NC_D, T], BF16, stack=fstack)
                    xn_v = emit_ln(X, True, XN)
                    x_v = l1ffn(XN, xn_v, fstack)
                    barrier()

        C.wait("Y", vm, vm.n)
        st_t = dma("Y", out_t.rearrange("(c p) t -> p c t", p=128),
                   X[:], st)
        C.wait("Y", st, st_t)

        with nc.Block() as block:
            @block.tensor
            def _(t):
                for f in C.streams["T"]:
                    f(t)

            @block.vector
            def _(v):
                for f in C.streams["V"]:
                    f(v)

            @block.scalar
            def _(s):
                for f in C.streams["S"]:
                    f(s)

            @block.gpsimd
            def _(g):
                for f in C.streams["G"]:
                    f(g)

            @block.sync
            def _(y):
                for f in C.streams["Y"]:
                    f(y)

    return nc


def make_inputs(core, inp):
    i = core
    x = np.asarray(inp["L0_attn_x"])
    xs = x[:, 1024 * i:1024 * (i + 1), :]
    xT = np.ascontiguousarray(xs.reshape(T, D).T)
    d = {"xT": xT,
         "ones_col": np.ones((128, 1), np.float32),
         "ones_row": np.ones((1, 128), np.float32),
         "idn8": np.eye(8, dtype=np.float32),
         "idn128": np.eye(128, dtype=np.float32)}
    g = lambda k: np.ascontiguousarray(np.asarray(inp[k], dtype=np.float32))
    bv = lambda k: np.ascontiguousarray(np.asarray(inp[k], dtype=np.float32).reshape(-1, 128).T)
    d["L0f_gate_w"] = g("L0_ffn_gate_w")
    d["L0f_sw_gate"] = g("L0_ffn_sw_gate")
    d["L0f_sw_up"] = g("L0_ffn_sw_up")
    d["L0f_sw_down"] = g("L0_ffn_sw_down")
    d["L0f_ew_gate"] = g("L0_ffn_ew_gate")
    d["L0f_ew_up"] = g("L0_ffn_ew_up")
    d["L0f_ew_down"] = g("L0_ffn_ew_down")
    d["L0a_wq_dT"] = np.ascontiguousarray(g("L0_attn_wq_d").T)
    d["L0a_bq_d"] = bv("L0_attn_bq_d")
    d["L0a_w_qkT"] = np.ascontiguousarray(g("L0_attn_w_qk").T)
    d["L0a_b_qk"] = np.ascontiguousarray(g("L0_attn_b_qk").reshape(16, 64).T)
    d["L0a_wkv_dT"] = np.ascontiguousarray(g("L0_attn_wkv_d").T)
    d["L0a_bkv_d"] = np.ascontiguousarray(g("L0_attn_bkv_d").reshape(64, 1))
    d["L0a_wv_uT"] = np.ascontiguousarray(g("L0_attn_wv_u").T)
    d["L0a_bv_u"] = np.ascontiguousarray(g("L0_attn_bv_u").reshape(1, D))
    d["L0a_w_projT"] = np.ascontiguousarray(g("L0_attn_w_proj").T)
    d["L0a_b_proj"] = bv("L0_attn_b_proj")
    d["L1f_ln_gamma"] = bv("L1_ffn_ln_gamma")
    d["L1f_ln_beta"] = bv("L1_ffn_ln_beta")
    d["L1f_router_wT"] = np.ascontiguousarray(g("L1_ffn_router_w").T)
    d["L1f_gate_upT"] = np.ascontiguousarray(g("L1_ffn_gate_up_w").transpose(0, 2, 1))
    d["L1f_downT"] = np.ascontiguousarray(g("L1_ffn_down_w").transpose(0, 2, 1))
    d["L1f_sh_gate_upT"] = np.ascontiguousarray(g("L1_ffn_shared_gate_up_w").T)
    d["L1f_sh_downT"] = np.ascontiguousarray(g("L1_ffn_shared_down_w").T)
    for p in "qkvo":
        d[f"L1a_{p}A"] = np.ascontiguousarray(g(f"L1_attn_{p}A").reshape(4, 16))
        d[f"L1a_{p}S"] = np.ascontiguousarray(g(f"L1_attn_{p}S").reshape(4, 65536))
        d[f"L1a_{p}b"] = bv(f"L1_attn_{p}b")
    for nm in ["g1", "b1", "g2", "b2"]:
        d[f"L1a_{nm}"] = bv(f"L1_attn_{nm}")
    ind_dots = np.zeros((8, 128, 16), np.float32)
    ind_bc = np.zeros((8, 16, 128), np.float32)
    for c in range(8):
        ind_dots[c, 0:64, 2 * c] = 1.0
        ind_dots[c, 64:128, 2 * c + 1] = 1.0
        ind_bc[c, 2 * c, 0:64] = 1.0
        ind_bc[c, 2 * c + 1, 64:128] = 1.0
    d["L1a_ind_dots"] = ind_dots
    d["L1a_ind_bc"] = ind_bc
    d["L1a_ind_bc7"] = ind_bc * np.float32(np.cos(np.pi / 4))
    return d


def kernel(**inputs):
    nc = build_program(PHASES)
    in_maps = [make_inputs(i, inputs) for i in range(NCORES)]
    res = run_bass_kernel_spmd(nc, in_maps, core_ids=list(range(NCORES)))
    outs = [res.results[i]["out"] for i in range(NCORES)]
    full = np.stack([o.T.reshape(2, 1024, 1024) for o in outs], axis=1)
    return np.ascontiguousarray(full.reshape(2, 8192, 1024))
